# version 09
# baseline (speedup 1.0000x reference)
import numpy as np
from contextlib import ExitStack

import concourse.bass as bass
import concourse.mybir as mybir
from concourse.bass_utils import run_bass_kernel_spmd

F32 = mybir.dt.float32
BF16 = mybir.dt.bfloat16
AF = mybir.ActivationFunctionType
ALU = mybir.AluOpType
AX = mybir.AxisListType

S = 2048
D = 2048
H = 16
HD = 128
ND = 64
R = 512
HID = 8192
NSEQ = 2
T = NSEQ * S
THETA = 10000.0
EPS = 1e-5
SCALE = float(1.0 / np.float32(np.sqrt(np.float64(HD))))
NEG_BIG = -3.0e38
NCHUNK = 4
CH = 512


class Ctr:
    def __init__(self, stack, nc, name):
        self.h = stack.enter_context(nc.semaphore(name))
        self.n = 0

    def inc(self, ins, k=1):
        ins.then_inc(self.h, k)
        self.n += k
        return self.n

    def wait(self, eng, val=None):
        v = self.n if val is None else val
        assert v <= self.n, "wait on a not-yet-emitted increment"
        if v > 0:
            eng.wait_ge(self.h, v)

    def mark(self):
        return (self, self.n)


def wfree(eng, slot):
    if slot is not None:
        ctr, val = slot
        ctr.wait(eng, val)


def dr3(ap, p=128):
    return ap.rearrange("(k p) c -> p k c", p=p)


def sb3(ap, k):
    return ap.rearrange("p (k c) -> p k c", k=k)


def _rope_tables():
    pos = np.arange(S, dtype=np.float64)
    rd = np.arange(ND)
    freqs = THETA ** (-(2.0 * (rd // 2)) / ND)
    ang = pos[None, :] * freqs[:, None]
    cos64 = np.cos(ang)
    sin64 = np.sin(ang)
    return (np.concatenate([cos64, cos64 * SCALE], 0).astype(np.float32),
            np.concatenate([sin64, sin64 * SCALE], 0).astype(np.float32))


def _rot_mats():
    import ml_dtypes
    mq = np.zeros((HD, HD), dtype=np.float32)
    for i in range(ND // 2):
        a = ND + 2 * i
        mq[a, a + 1] = -1.0
        mq[a + 1, a] = 1.0
    mk = np.zeros((ND, ND), dtype=np.float32)
    for i in range(ND // 2):
        mk[2 * i, 2 * i + 1] = -1.0
        mk[2 * i + 1, 2 * i] = 1.0
    return (np.ascontiguousarray(mq.T).astype(ml_dtypes.bfloat16),
            np.ascontiguousarray(mk.T).astype(ml_dtypes.bfloat16))


def build_program(upto="D2", dump=None):
    nc = bass.Bass("TRN2", target_bir_lowering=False)

    x_in = nc.dram_tensor("x", [T, D], F32, kind="ExternalInput").ap()
    wqT = nc.dram_tensor("wqT", [D, D], F32, kind="ExternalInput").ap()
    wkvT = nc.dram_tensor("wkvT", [D, R], F32, kind="ExternalInput").ap()
    wkrT = nc.dram_tensor("wkrT", [D, ND], F32, kind="ExternalInput").ap()
    wkuT = nc.dram_tensor("wkuT", [R, H * ND], F32, kind="ExternalInput").ap()
    wvuT = nc.dram_tensor("wvuT", [R, D], F32, kind="ExternalInput").ap()
    woT = nc.dram_tensor("woT", [D, D], F32, kind="ExternalInput").ap()
    wgT = nc.dram_tensor("wgT", [D, HID], F32, kind="ExternalInput").ap()
    wuT = nc.dram_tensor("wuT", [D, HID], F32, kind="ExternalInput").ap()
    wdT = nc.dram_tensor("wdT", [HID, D], F32, kind="ExternalInput").ap()
    out_d = nc.dram_tensor("out", [T, D], F32, kind="ExternalOutput").ap()

    scr = {}
    scr["qT_s"] = qT_s = nc.dram_tensor("qT_s", [NSEQ * D, S], BF16).ap()
    scr["knope_s"] = knope_s = nc.dram_tensor("knope_s", [NSEQ * H * ND, S], BF16).ap()
    scr["krope_s"] = krope_s = nc.dram_tensor("krope_s", [NSEQ * ND, S], BF16).ap()
    scr["v_s"] = v_s = nc.dram_tensor("v_s", [T, D], BF16).ap()
    scr["attnT_s"] = attnT_s = nc.dram_tensor("attnT_s", [NSEQ * D, S], BF16).ap()
    scr["y_s"] = y_s = nc.dram_tensor("y_s", [T, D], F32).ap()
    scr["lnyT_s"] = lnyT_s = nc.dram_tensor("lnyT_s", [D, T], BF16).ap()
    scr["hT_s"] = hT_s = nc.dram_tensor("hT_s", [HID, T], BF16).ap()
    scr["dpart_s"] = dpart_s = nc.dram_tensor("dpart_s", [T, D], F32).ap()

    dbg = {}
    if dump:
        for name in ([dump] if isinstance(dump, str) else dump):
            dsrc = scr[name]
            dbg[name] = nc.dram_tensor("dbg_" + name, list(dsrc.shape),
                                       dsrc.dtype, kind="ExternalOutput").ap()

    cos_np, sin_np = _rope_tables()
    mtq_np, mtk_np = _rot_mats()
    cos_d = nc.inline_tensor(cos_np, "cos_t").ap()
    sin_d = nc.inline_tensor(sin_np, "sin_t").ap()
    mtq_d = nc.inline_tensor(np.asarray(mtq_np), "mtq").ap()
    mtk_d = nc.inline_tensor(np.asarray(mtk_np), "mtk").ap()

    te, ve, se, ge, sy = nc.tensor, nc.vector, nc.scalar, nc.gpsimd, nc.sync

    with ExitStack() as topst, nc.Block():
        ident = nc.alloc_sbuf_tensor("ident", [128, 128], BF16)
        eps_sb = nc.alloc_sbuf_tensor("eps_sb", [128, 1], F32)

        c_w = Ctr(topst, nc, "c_w")
        c_x = Ctr(topst, nc, "c_x")
        c_st = Ctr(topst, nc, "c_st")
        c_t = Ctr(topst, nc, "c_t")
        c_v = Ctr(topst, nc, "c_v")
        c_s = Ctr(topst, nc, "c_s")
        c_g = Ctr(topst, nc, "c_g")

        with ExitStack() as ini:
            iop = ini.enter_context(nc.sbuf_tensor("iop", [128, 128], F32))
            io2 = ini.enter_context(nc.sbuf_tensor("io2", [128, 128], F32))
            c_g.inc(ge.iota(iop[:, :], [[0, 128]], channel_multiplier=1,
                            allow_small_or_imprecise_dtypes=True))
            c_g.inc(ge.iota(io2[:, :], [[1, 128]], channel_multiplier=0,
                            allow_small_or_imprecise_dtypes=True))
            c_g.wait(ve)
            c_v.inc(ve.tensor_tensor(ident[:, :], io2[:, :], iop[:, :],
                                     ALU.is_equal))
            c_v.inc(ve.memset(eps_sb[:, :], EPS))
            c_v.wait(ve)
            ident_ready = c_v.n
            c_v.wait(ge)

        def ln_stats(src_ap_fn, bn24, mv, rstd, negmr, ready_slot):
            wfree(ve, ready_slot)
            for i in range(4):
                ins = ve.bn_stats(bn24[:, 6 * i:6 * i + 6],
                                  src_ap_fn(512 * i, 512 * (i + 1)))
            c_v.inc(ins)
            c_v.wait(ve)
            c_v.inc(ve.bn_aggr(mv[:, :], bn24[:, :]))
            aggr_m = c_v.mark()
            wfree(se, aggr_m)
            c_s.wait(se)
            c_s.inc(se.activation(rstd[:, :], mv[:, 1:2], AF.Ln,
                                  bias=eps_sb[:, :]))
            c_s.wait(se)
            c_s.inc(se.activation(rstd[:, :], rstd[:, :], AF.Exp, scale=-0.5))
            rstd_m = c_s.mark()
            wfree(ve, rstd_m)
            c_v.inc(ve.tensor_scalar(negmr[:, :], mv[:, 0:1], rstd[:, :], -1.0,
                                     ALU.mult, ALU.mult))
            return c_v.mark()

        with ExitStack() as pa:
            ctx = pa.enter_context
            cos_sb = ctx(nc.sbuf_tensor("cos_sb", [HD, S], F32))
            sin_sb = ctx(nc.sbuf_tensor("sin_sb", [HD, S], F32))
            mtq_sb = ctx(nc.sbuf_tensor("mtq_sb", [HD, HD], BF16))
            mtk_sb = ctx(nc.sbuf_tensor("mtk_sb", [ND, ND], BF16))
            c_x.inc(sy.dma_start(cos_sb[:, :], cos_d), 16)
            c_x.inc(sy.dma_start(sin_sb[:, :], sin_d), 16)
            c_x.inc(sy.dma_start(mtq_sb[:, :], mtq_d), 16)
            c_x.inc(sy.dma_start(mtk_sb[:, :], mtk_d), 16)
            consts_ld = c_x.n

            wq_sb = ctx(nc.sbuf_tensor("wq_sb", [128, 16 * D], BF16))
            wkv_sb = ctx(nc.sbuf_tensor("wkv_sb", [128, 4 * D], BF16))
            wkr_sb = ctx(nc.sbuf_tensor("wkr_sb", [128, 16 * ND], BF16))
            wku_sb = ctx(nc.sbuf_tensor("wku_sb", [128, 8 * R], BF16))
            wvu_sb = ctx(nc.sbuf_tensor("wvu_sb", [128, 4 * D], BF16))
            for m in range(16):
                c_w.inc(ge.dma_start(
                    sb3(wq_sb[:, m * D:(m + 1) * D], 16),
                    dr3(wqT[:, m * 128:(m + 1) * 128])), 16)
            for m in range(4):
                c_w.inc(ge.dma_start(
                    sb3(wkv_sb[:, m * D:(m + 1) * D], 16),
                    dr3(wkvT[:, m * 128:(m + 1) * 128])), 16)
            c_w.inc(ge.dma_start(sb3(wkr_sb[:, :], 16), dr3(wkrT)), 16)
            for m in range(8):
                c_w.inc(ge.dma_start(
                    sb3(wku_sb[:, m * R:(m + 1) * R], 4),
                    dr3(wkuT[:, m * 128:(m + 1) * 128])), 16)
            for m in range(4):
                c_w.inc(ge.dma_start(
                    wvu_sb[:, m * D:(m + 1) * D],
                    wvuT[m * 128:(m + 1) * 128, :]), 16)
            w_ready = c_w.n

            xin = [ctx(nc.sbuf_tensor(f"xin{i}", [128, D], F32)) for i in range(2)]
            lnx = [ctx(nc.sbuf_tensor(f"lnx{i}", [128, D], BF16)) for i in range(2)]
            lnxT = [ctx(nc.sbuf_tensor(f"lnxT{i}", [128, 16 * CH], BF16))
                    for i in range(2)]
            ckvT = ctx(nc.sbuf_tensor("ckvT", [128, 4 * CH], BF16))
            bn24 = [ctx(nc.sbuf_tensor(f"bn24_{i}", [128, 24], F32)) for i in range(2)]
            mv = [ctx(nc.sbuf_tensor(f"mv{i}", [128, 2], F32)) for i in range(2)]
            rstd = [ctx(nc.sbuf_tensor(f"rstd{i}", [128, 1], F32)) for i in range(2)]
            negmr = [ctx(nc.sbuf_tensor(f"negmr{i}", [128, 1], F32)) for i in range(2)]
            qbf = [ctx(nc.sbuf_tensor(f"qbf{i}", [128, CH], BF16)) for i in range(2)]
            tcos = [ctx(nc.sbuf_tensor(f"tcos{i}", [128, CH], F32)) for i in range(2)]
            trot = [ctx(nc.sbuf_tensor(f"trot{i}", [128, CH], F32)) for i in range(2)]
            qfin = [ctx(nc.sbuf_tensor(f"qfin{i}", [128, CH], BF16)) for i in range(2)]
            evac = [ctx(nc.sbuf_tensor(f"evac{i}", [128, CH], BF16)) for i in range(2)]

            ps_tr = [ctx(nc.psum_tensor(f"ps_tr{i}", [128, CH], F32)) for i in range(2)]
            ps_pj = [ctx(nc.psum_tensor(f"ps_pj{i}", [128, CH], F32)) for i in range(2)]
            ps_rt = [ctx(nc.psum_tensor(f"ps_rt{i}", [128, CH], F32)) for i in range(2)]

            xin_free = [None, None]
            lnx_free = [None, None]
            lnxT_free = [None, None]
            pstr_free = [None, None]
            pspj_free = [None, None]
            psrt_free = [None, None]
            qbf_free = [None, None]
            evac_free = [None, None]
            qfin_free = [None, None]

            for g in range(NSEQ * NCHUNK):
                s_i, c_i = divmod(g, NCHUNK)
                gl = lnxT[g % 2]
                for u in range(4):
                    ui = u % 2
                    row0 = s_i * S + c_i * CH + u * 128
                    wfree(sy, xin_free[ui])
                    c_x.inc(sy.dma_start(xin[ui][:, :],
                                         x_in[row0:row0 + 128, :]), 16)
                    x_here = c_x.mark()
                    nmr_m = ln_stats(lambda a, b, _x=xin[ui]: _x[:, a:b],
                                     bn24[ui], mv[ui], rstd[ui], negmr[ui],
                                     x_here)
                    wfree(se, nmr_m)
                    wfree(se, lnx_free[ui])
                    c_s.inc(se.activation(lnx[ui][:, :], xin[ui][:, :], AF.Identity,
                                          bias=negmr[ui][:, :],
                                          scale=rstd[ui][:, :]))
                    xin_free[ui] = c_s.mark()
                    lnx_here = c_s.n
                    for grp in range(4):
                        pi = grp % 2
                        c_s.wait(te, lnx_here)
                        c_v.wait(te, ident_ready)
                        wfree(te, pstr_free[pi])
                        for i4 in range(4):
                            k = grp * 4 + i4
                            ins = te.matmul(ps_tr[pi][:, i4 * 128:(i4 + 1) * 128],
                                            lnx[ui][:, k * 128:(k + 1) * 128],
                                            ident[:, :],
                                            start=(i4 == 0), stop=(i4 == 3),
                                            skip_group_check=True)
                        c_t.inc(ins)
                        lnx_free[ui] = c_t.mark()
                        wfree(ve, lnx_free[ui])
                        wfree(ve, lnxT_free[g % 2])
                        dst = gl[:, :].rearrange("p (k c) -> p k c", k=16)[
                            :, grp * 4:grp * 4 + 4, u * 128:(u + 1) * 128]
                        src = ps_tr[pi][:, :].rearrange("p (k c) -> p k c", k=4)
                        c_v.inc(ve.tensor_copy(dst, src))
                        pstr_free[pi] = c_v.mark()
                lnxT_ready_v = c_v.n

                ccols = slice(c_i * CH, (c_i + 1) * CH)
                c_w.wait(te, w_ready)
                c_x.wait(te, consts_ld)
                c_v.wait(te, lnxT_ready_v)

                for m in range(16):
                    pi = m % 2
                    wfree(te, pspj_free[pi])
                    for k in range(16):
                        ins = te.matmul(ps_pj[pi][:, :],
                                        wq_sb[:, m * D + k * 128:
                                              m * D + (k + 1) * 128],
                                        gl[:, k * CH:(k + 1) * CH],
                                        start=(k == 0), stop=(k == 15))
                    c_t.inc(ins)
                    q_mm = c_t.n
                    c_t.wait(ve, q_mm)
                    wfree(ve, qbf_free[pi])
                    wfree(ve, qfin_free[pi])
                    c_v.inc(ve.tensor_copy(qbf[pi][:, :], ps_pj[pi][:, :]))
                    c_v.inc(ve.tensor_mul(tcos[pi][64:128, :],
                                          ps_pj[pi][64:128, :],
                                          cos_sb[64:128, ccols]))
                    pspj_free[pi] = c_v.mark()
                    wfree(te, pspj_free[pi])
                    wfree(te, psrt_free[pi])
                    c_t.inc(te.matmul(ps_rt[pi][:, :], mtq_sb[:, :],
                                      qbf[pi][:, :], start=True, stop=True))
                    qbf_rot = c_t.mark()
                    wfree(ve, qbf_rot)
                    c_v.inc(ve.tensor_mul(trot[pi][64:128, :],
                                          ps_rt[pi][64:128, :],
                                          sin_sb[64:128, ccols]))
                    psrt_free[pi] = c_v.mark()
                    c_v.wait(ve)
                    c_v.inc(ve.tensor_add(qfin[pi][64:128, :],
                                          tcos[pi][64:128, :],
                                          trot[pi][64:128, :]))
                    c_v.inc(ve.tensor_scalar_mul(qfin[pi][0:64, :],
                                                 qbf[pi][0:64, :], SCALE))
                    qbf_free[pi] = c_v.mark()
                    qf_v = c_v.n
                    c_v.wait(se, qf_v)
                    c_st.inc(se.dma_start(
                        qT_s[s_i * D + m * 128:s_i * D + (m + 1) * 128, ccols],
                        qfin[pi][:, :]), 16)
                    qfin_free[pi] = c_st.mark()

                for m in range(4):
                    pi = m % 2
                    wfree(te, pspj_free[pi])
                    for k in range(16):
                        ins = te.matmul(ps_pj[pi][:, :],
                                        wkv_sb[:, m * D + k * 128:
                                               m * D + (k + 1) * 128],
                                        gl[:, k * CH:(k + 1) * CH],
                                        start=(k == 0), stop=(k == 15))
                    c_t.inc(ins)
                    c_t.wait(se, c_t.n)
                    c_s.inc(se.copy(ckvT[:, m * CH:(m + 1) * CH],
                                    ps_pj[pi][:, :]))
                    pspj_free[pi] = c_s.mark()
                ckv_ready_s = c_s.n

                wfree(te, pspj_free[0])
                for k in range(16):
                    ins = te.matmul(ps_pj[0][0:64, :],
                                    wkr_sb[:, k * ND:(k + 1) * ND],
                                    gl[:, k * CH:(k + 1) * CH],
                                    start=(k == 0), stop=(k == 15))
                c_t.inc(ins)
                kr_mm = c_t.n
                c_t.wait(ve, kr_mm)
                wfree(ve, qbf_free[0])
                c_v.inc(ve.tensor_copy(qbf[0][0:64, :], ps_pj[0][0:64, :]))
                c_v.inc(ve.tensor_mul(tcos[0][0:64, :], ps_pj[0][0:64, :],
                                      cos_sb[0:64, ccols]))
                pspj_free[0] = c_v.mark()
                wfree(te, pspj_free[0])
                wfree(te, psrt_free[0])
                c_t.inc(te.matmul(ps_rt[0][0:64, :], mtk_sb[:, :],
                                  qbf[0][0:64, :], start=True, stop=True))
                qbf_free[0] = c_t.mark()
                wfree(ve, qbf_free[0])
                c_v.inc(ve.tensor_mul(trot[0][0:64, :], ps_rt[0][0:64, :],
                                      sin_sb[0:64, ccols]))
                psrt_free[0] = c_v.mark()
                c_v.wait(ve)
                wfree(ve, qfin_free[0])
                c_v.inc(ve.tensor_add(qfin[0][0:64, :], tcos[0][0:64, :],
                                      trot[0][0:64, :]))
                c_v.wait(se)
                c_st.inc(se.dma_start(
                    krope_s[s_i * ND:(s_i + 1) * ND, ccols],
                    qfin[0][0:64, :]), 16)
                qfin_free[0] = c_st.mark()

                for m in range(8):
                    pi = m % 2
                    wfree(te, pspj_free[pi])
                    c_s.wait(te, ckv_ready_s)
                    for k in range(4):
                        ins = te.matmul(ps_pj[pi][:, :],
                                        wku_sb[:, m * R + k * 128:
                                               m * R + (k + 1) * 128],
                                        ckvT[:, k * CH:(k + 1) * CH],
                                        start=(k == 0), stop=(k == 3))
                    c_t.inc(ins)
                    c_t.wait(se, c_t.n)
                    wfree(se, evac_free[pi])
                    c_s.inc(se.copy(evac[pi][:, :], ps_pj[pi][:, :]))
                    pspj_free[pi] = c_s.mark()
                    c_s.wait(se)
                    c_st.inc(se.dma_start(
                        knope_s[s_i * H * ND + m * 128:
                                s_i * H * ND + (m + 1) * 128, ccols],
                        evac[pi][:, :]), 16)
                    evac_free[pi] = c_st.mark()

                for u in range(4):
                    for n in range(4):
                        pi = (u * 4 + n) % 2
                        wfree(te, pspj_free[pi])
                        for k in range(4):
                            ins = te.matmul(
                                ps_pj[pi][:, :],
                                ckvT[:, k * CH + u * 128:k * CH + (u + 1) * 128],
                                wvu_sb[:, k * D + n * CH:k * D + (n + 1) * CH],
                                start=(k == 0), stop=(k == 3))
                        c_t.inc(ins)
                        c_t.wait(se, c_t.n)
                        wfree(se, evac_free[pi])
                        c_s.inc(se.copy(evac[pi][:, :], ps_pj[pi][:, :]))
                        pspj_free[pi] = c_s.mark()
                        c_s.wait(se)
                        c_st.inc(se.dma_start(
                            v_s[s_i * S + c_i * CH + u * 128:
                                s_i * S + c_i * CH + (u + 1) * 128,
                                n * CH:(n + 1) * CH],
                            evac[pi][:, :]), 16)
                        evac_free[pi] = c_st.mark()
                lnxT_free[g % 2] = c_t.mark()

            c_st.wait(sy)
            c_x.wait(sy)
            c_w.wait(ge)
        nc.all_engine_barrier()
        a_done_st = c_st.n

        if upto == "A":
            _finish(nc, sy, c_st, c_x, dump, dbg, scr)
            return nc

        with ExitStack() as pb:
            ctx = pb.enter_context
            masks = ctx(nc.sbuf_tensor("masks", [128, 4 * CH], F32))
            iof = ctx(nc.sbuf_tensor("iof", [128, CH], F32))
            c_g.inc(ge.iota(iof[:, :], [[1, CH]], channel_multiplier=-1,
                            allow_small_or_imprecise_dtypes=True))
            c_g.wait(ve)
            for r in range(4):
                c_v.inc(ve.tensor_single_scalar(masks[:, r * CH:(r + 1) * CH],
                                                iof[:, :], float(r * 128),
                                                ALU.is_gt))
                c_v.wait(ve)
                c_v.inc(ve.tensor_scalar_mul(masks[:, r * CH:(r + 1) * CH],
                                             masks[:, r * CH:(r + 1) * CH],
                                             NEG_BIG))
                c_v.wait(ve)
            masks_v = c_v.n

            q_sb = [ctx(nc.sbuf_tensor(f"q_sb{i}", [128, S], BF16)) for i in range(2)]
            kc_sb = [ctx(nc.sbuf_tensor(f"kc_sb{i}", [128, S], BF16)) for i in range(2)]
            v_hb = [ctx(nc.sbuf_tensor(f"v_hb{i}", [128, H * 128], BF16))
                    for i in range(2)]
            row = [ctx(nc.sbuf_tensor(f"row{i}", [128, S], F32)) for i in range(2)]
            pbf = [ctx(nc.sbuf_tensor(f"pbf{i}", [128, S], BF16)) for i in range(3)]
            diag = [ctx(nc.sbuf_tensor(f"diag{i}", [128, 128], BF16))
                    for i in range(3)]
            cmax = [ctx(nc.sbuf_tensor(f"cmax{i}", [128, 4], F32))
                    for i in range(2)]
            pT_sb = ctx(nc.sbuf_tensor("pT_sb", [128, 16 * CH], BF16))
            attbf = [ctx(nc.sbuf_tensor(f"attbf{i}", [128, CH], BF16))
                     for i in range(2)]
            negm = [ctx(nc.sbuf_tensor(f"negm{i}", [128, 1], F32))
                    for i in range(2)]
            den = [ctx(nc.sbuf_tensor(f"den{i}", [128, 1], F32))
                   for i in range(2)]
            dinv = [ctx(nc.sbuf_tensor(f"dinv{i}", [128, 1], F32))
                    for i in range(2)]

            ps_row = [ctx(nc.psum_tensor(f"ps_row{i}", [128, CH], F32))
                      for i in range(4)]
            ps_pt = [ctx(nc.psum_tensor(f"ps_pt{i}", [128, CH], F32))
                     for i in range(2)]
            ps_av = [ctx(nc.psum_tensor(f"ps_av{i}", [128, CH], F32))
                     for i in range(2)]

            qkv_free = [None, None]
            exp_mark = [None, None]
            att_free = [None, None]
            pspt_free = [None, None]
            psav_free = [None, None]
            load_done = [None, None]
            bank_free = [None] * 4
            row_free = [None, None]
            pbf_free = [None] * 3

            def b_transposes(p):
                qt, qs, qtl, bi, s_i, h = (p[k] for k in
                                           ("qt", "qs", "qtl", "bi", "s_i", "h"))
                pb = pbf[qt % 3]
                dg = diag[qt % 3]
                wfree(te, p["pn_v"])
                ngrp = qt // 4 + 1
                for jg in range(ngrp):
                    gsz = 4 if jg < qt // 4 else (qt % 4 + 1)
                    pi = jg % 2
                    wfree(te, pspt_free[pi])
                    for i4 in range(gsz):
                        jt = jg * 4 + i4
                        ins = te.matmul(
                            ps_pt[pi][:, i4 * 128:(i4 + 1) * 128],
                            pb[:, jt * 128:(jt + 1) * 128],
                            dg[:, :],
                            start=(i4 == 0), stop=(i4 == gsz - 1),
                            skip_group_check=True)
                    c_t.inc(ins)
                    grp_m = c_t.mark()
                    dst = pT_sb[:, :].rearrange("p (j c) -> p j c", j=16)[
                        :, jg * 4:jg * 4 + gsz, qtl * 128:(qtl + 1) * 128]
                    srcp = ps_pt[pi][:, 0:gsz * 128].rearrange(
                        "p (j c) -> p j c", j=gsz)
                    if jg % 2 == 0:
                        wfree(ve, grp_m)
                        c_v.inc(ve.tensor_copy(dst, srcp))
                        pspt_free[pi] = c_v.mark()
                    else:
                        wfree(se, grp_m)
                        c_s.inc(se.copy(dst, srcp))
                        pspt_free[pi] = c_s.mark()
                pbf_free[qt % 3] = c_t.mark()
                if qtl == 3:
                    ai = qs % 2
                    wfree(te, psav_free[ai])
                    wfree(te, pspt_free[0])
                    wfree(te, pspt_free[1])
                    for jt in range(4 * qs + 4):
                        if jt < 4 * qs:
                            ins = te.matmul(
                                ps_av[ai][:, :],
                                v_hb[bi][:, jt * 128:(jt + 1) * 128],
                                pT_sb[:, jt * CH:(jt + 1) * CH],
                                start=(jt == 0), stop=False,
                                skip_group_check=True)
                        else:
                            i = jt - 4 * qs
                            ins = te.matmul(
                                ps_av[ai][:, i * 128:CH],
                                v_hb[bi][:, jt * 128:(jt + 1) * 128],
                                pT_sb[:, jt * CH + i * 128:(jt + 1) * CH],
                                start=(jt == 0), stop=(jt == 4 * qs + 3),
                                skip_group_check=True)
                    c_t.inc(ins)
                    av_mm = c_t.n
                    c_t.wait(se, av_mm)
                    wfree(se, att_free[ai])
                    c_s.inc(se.copy(attbf[ai][:, :], ps_av[ai][:, :]))
                    psav_free[ai] = c_s.mark()
                    c_s.wait(se)
                    c_st.inc(se.dma_start(
                        attnT_s[s_i * D + h * 128:s_i * D + (h + 1) * 128,
                                qs * CH:(qs + 1) * CH], attbf[ai][:, :]), 16)
                    att_free[ai] = c_st.mark()

            def b_softmax(p):
                qt, nch, ncol, rw = p["qt"], p["nch"], p["ncol"], p["rw"]
                p2 = qt % 2
                c_v.wait(ve)
                wfree(ve, exp_mark[p2])
                c_v.inc(ve.reduce_max(negm[p2][:, :], cmax[p2][:, 0:nch],
                                      axis=AX.X, negate=True))
                negm_g = c_v.mark()
                wfree(se, negm_g)
                wfree(se, pbf_free[qt % 3])
                c_s.inc(se.activation(pbf[qt % 3][:, 0:ncol],
                                      rw[:, 0:ncol], AF.Exp,
                                      bias=negm[p2][:, :], scale=1.0,
                                      accum_out=den[p2][:, :]))
                row_free[p2] = c_s.mark()
                exp_mark[p2] = c_s.mark()
                wfree(ve, exp_mark[p2])
                c_v.inc(ve.reciprocal(dinv[p2][:, :], den[p2][:, :]))
                c_v.wait(ve)
                c_v.inc(ve.tensor_scalar_mul(diag[qt % 3][:, :],
                                             ident[:, :], dinv[p2][:, :]))
                p["pn_v"] = c_v.mark()
                return p

            pending = []
            soft_pending = None
            for hi in range(NSEQ * H):
                s_i, h = divmod(hi, H)
                bi = hi % 2
                wfree(sy, qkv_free[bi])
                c_st.wait(sy, a_done_st)
                c_x.inc(sy.dma_start(
                    q_sb[bi][:, :],
                    qT_s[s_i * D + h * 128:s_i * D + (h + 1) * 128, :]), 16)
                c_x.inc(sy.dma_start(
                    kc_sb[bi][0:64, :],
                    knope_s[s_i * H * ND + h * ND:
                            s_i * H * ND + (h + 1) * ND, :]), 16)
                c_x.inc(sy.dma_start(
                    kc_sb[bi][64:128, :],
                    krope_s[s_i * ND:(s_i + 1) * ND, :]), 16)
                c_x.inc(sy.dma_start(
                    sb3(v_hb[bi][:, :], 16),
                    dr3(v_s[s_i * S:(s_i + 1) * S, h * 128:(h + 1) * 128])), 16)
                load_done[bi] = c_x.mark()

                for qs in range(4):
                    nch = qs + 1
                    ncol = nch * CH
                    for qtl in range(4):
                        qt = qs * 4 + qtl
                        rw = row[qt % 2]
                        wfree(se, row_free[qt % 2])
                        wfree(ve, row_free[qt % 2])
                        if qtl == 0:
                            wfree(te, load_done[bi])
                        for jc in range(nch):
                            wfree(te, bank_free[jc])
                            ins = te.matmul(
                                ps_row[jc][:, :],
                                q_sb[bi][:, qt * 128:(qt + 1) * 128],
                                kc_sb[bi][:, jc * CH:(jc + 1) * CH],
                                start=True, stop=True)
                            c_t.inc(ins)
                            mm_m = c_t.n
                            p2 = qt % 2
                            if jc == qs:
                                c_t.wait(ve, mm_m)
                                c_v.wait(ve, masks_v)
                                c_v.inc(ve.tensor_tensor(
                                    rw[:, jc * CH:(jc + 1) * CH],
                                    ps_row[jc][:, :],
                                    masks[:, qtl * CH:(qtl + 1) * CH],
                                    ALU.add))
                                bank_free[jc] = c_v.mark()
                            else:
                                c_t.wait(se, mm_m)
                                c_s.inc(se.copy(rw[:, jc * CH:(jc + 1) * CH],
                                                ps_row[jc][:, :]))
                                bank_free[jc] = c_s.mark()
                            wfree(ve, bank_free[jc])
                            c_v.inc(ve.reduce_max(
                                cmax[p2][:, jc:jc + 1],
                                rw[:, jc * CH:(jc + 1) * CH], axis=AX.X))
                        if len(pending) >= 2:
                            b_transposes(pending.pop(0))
                        if soft_pending is not None:
                            pending.append(b_softmax(soft_pending))
                            soft_pending = None
                        soft_pending = {"qt": qt, "qs": qs, "qtl": qtl,
                                        "bi": bi, "s_i": s_i, "h": h,
                                        "rw": rw, "nch": nch, "ncol": ncol}
                if soft_pending is not None:
                    pending.append(b_softmax(soft_pending))
                    soft_pending = None
                while pending:
                    b_transposes(pending.pop(0))
                qkv_free[bi] = c_t.mark()
            c_st.wait(sy)
            c_x.wait(sy)
        nc.all_engine_barrier()
        b_done_st = c_st.n

        if upto == "B":
            _finish(nc, sy, c_st, c_x, dump, dbg, scr)
            return nc

        with ExitStack() as pc:
            ctx = pc.enter_context
            wo_sb = ctx(nc.sbuf_tensor("wo_sb", [128, 16 * D], BF16))
            for k in range(16):
                c_w.inc(ge.dma_start(wo_sb[:, k * D:(k + 1) * D],
                                     woT[k * 128:(k + 1) * 128, :]), 16)
            wo_ready = c_w.n
            atc = [ctx(nc.sbuf_tensor(f"atc{i}", [128, 16 * CH], BF16))
                   for i in range(2)]
            xin2 = [ctx(nc.sbuf_tensor(f"xin2_{i}", [128, D], F32))
                    for i in range(2)]
            lnat = ctx(nc.sbuf_tensor("lnat", [128, D], F32))
            y_sb = [ctx(nc.sbuf_tensor(f"y_sb{i}", [128, D], F32)) for i in range(2)]
            lny = [ctx(nc.sbuf_tensor(f"lny{i}", [128, D], BF16)) for i in range(2)]
            lnyTst = [ctx(nc.sbuf_tensor(f"lnyTst{i}", [128, 16 * 128], BF16))
                      for i in range(2)]
            bn24c = [ctx(nc.sbuf_tensor(f"bn24c{i}", [128, 24], F32))
                     for i in range(2)]
            mvc = [ctx(nc.sbuf_tensor(f"mvc{i}", [128, 2], F32)) for i in range(2)]
            rstdc = [ctx(nc.sbuf_tensor(f"rstdc{i}", [128, 1], F32))
                     for i in range(2)]
            negmrc = [ctx(nc.sbuf_tensor(f"negmrc{i}", [128, 1], F32))
                      for i in range(2)]
            ps_pr = ctx(nc.psum_tensor("ps_pr", [128, D], F32))
            ps_t2 = [ctx(nc.psum_tensor(f"ps_t2{i}", [128, CH], F32))
                     for i in range(2)]

            atc_load = [None, None]
            atc_free = [None, None]
            x2_free = [None, None]
            y_free = [None, None]
            lny_free = [None, None]
            pst2_free = [None, None]
            lnyTst_free = [None, None]

            def c_tail(p):
                ui, lny_s, row0, tg = p["ui"], p["lny_s"], p["row0"], p["tg"]
                wfree(ve, lnyTst_free[ui])
                for grp in range(4):
                    pi = grp % 2
                    c_s.wait(te, lny_s)
                    wfree(te, pst2_free[pi])
                    for i4 in range(4):
                        k = grp * 4 + i4
                        ins = te.matmul(
                            ps_t2[pi][:, i4 * 128:(i4 + 1) * 128],
                            lny[ui][:, k * 128:(k + 1) * 128],
                            ident[:, :],
                            start=(i4 == 0), stop=(i4 == 3),
                            skip_group_check=True)
                    c_t.inc(ins)
                    lny_free[ui] = c_t.mark()
                    c_t.wait(ve)
                    c_v.inc(ve.tensor_copy(
                        lnyTst[ui][:, grp * 512:(grp + 1) * 512],
                        ps_t2[pi][:, :]))
                    pst2_free[pi] = c_v.mark()
                lnyT_v = c_v.n
                c_v.wait(se, lnyT_v)
                c_st.inc(se.dma_start(
                    dr3(lnyT_s[:, tg:tg + 128]),
                    sb3(lnyTst[ui][:, :], 16)), 16)
                lnyTst_free[ui] = c_st.mark()

            c_pending = None
            for g in range(NSEQ * NCHUNK):
                s_i, c_i = divmod(g, NCHUNK)
                gi = g % 2
                if g == 0:
                    c_st.wait(sy, b_done_st)
                    c_x.inc(sy.dma_start(
                        sb3(atc[0][:, :], 16),
                        dr3(attnT_s[0:D, 0:CH])), 16)
                    atc_load[0] = c_x.mark()
                if g + 1 < NSEQ * NCHUNK:
                    s_n, c_n = divmod(g + 1, NCHUNK)
                    wfree(sy, atc_free[(g + 1) % 2])
                    c_x.inc(sy.dma_start(
                        sb3(atc[(g + 1) % 2][:, :], 16),
                        dr3(attnT_s[s_n * D:(s_n + 1) * D,
                                    c_n * CH:(c_n + 1) * CH])), 16)
                    atc_load[(g + 1) % 2] = c_x.mark()
                for u in range(4):
                    ui = u % 2
                    row0 = s_i * S + c_i * CH + u * 128
                    wfree(sy, x2_free[ui])
                    c_x.inc(sy.dma_start(xin2[ui][:, :],
                                         x_in[row0:row0 + 128, :]), 16)
                    x2_here = c_x.mark()
                    wfree(te, atc_load[gi])
                    c_w.wait(te, wo_ready)
                    c_s.wait(te)
                    c_v.wait(te)
                    if c_pending is not None:
                        c_tail(c_pending)
                        c_pending = None
                    for n in range(4):
                        for k in range(16):
                            ins = te.matmul(
                                ps_pr[:, n * CH:(n + 1) * CH],
                                atc[gi][:, k * CH + u * 128:
                                        k * CH + (u + 1) * 128],
                                wo_sb[:, k * D + n * CH:k * D + (n + 1) * CH],
                                start=(k == 0), stop=(k == 15))
                    c_t.inc(ins)
                    pr_mm = c_t.mark()
                    atc_free[gi] = c_t.mark()
                    ln_stats(lambda a, b: ps_pr[:, a:b], bn24c[ui], mvc[ui],
                             rstdc[ui], negmrc[ui], pr_mm)
                    c_v.wait(se)
                    c_s.inc(se.activation(lnat[:, :], ps_pr[:, :], AF.Identity,
                                          bias=negmrc[ui][:, :],
                                          scale=rstdc[ui][:, :]))
                    lnat_s = c_s.n
                    c_s.wait(ve, lnat_s)
                    wfree(ve, x2_here)
                    wfree(ve, y_free[ui])
                    c_v.inc(ve.tensor_add(y_sb[ui][:, :], lnat[:, :],
                                          xin2[ui][:, :]))
                    y_v = c_v.mark()
                    x2_free[ui] = c_v.mark()
                    c_v.wait(se, y_v[1])
                    c_st.inc(se.dma_start(y_s[row0:row0 + 128, :],
                                          y_sb[ui][:, :]), 16)
                    y_free[ui] = c_st.mark()
                    ln_stats(lambda a, b, _y=y_sb[ui]: _y[:, a:b],
                             bn24c[ui], mvc[ui], rstdc[ui], negmrc[ui], y_v)
                    c_v.wait(se)
                    wfree(se, lny_free[ui])
                    c_s.inc(se.activation(lny[ui][:, :], y_sb[ui][:, :],
                                          AF.Identity, bias=negmrc[ui][:, :],
                                          scale=rstdc[ui][:, :]))
                    lny_s = c_s.n
                    c_pending = {"ui": ui, "lny_s": lny_s, "row0": row0,
                                 "tg": s_i * S + c_i * CH + u * 128}
            if c_pending is not None:
                c_tail(c_pending)
                c_pending = None
            c_st.wait(sy)
            c_x.wait(sy)
            c_w.wait(ge)
        nc.all_engine_barrier()
        c_done_st = c_st.n

        if upto == "C":
            _finish(nc, sy, c_st, c_x, dump, dbg, scr)
            return nc

        with ExitStack() as pd1:
            ctx = pd1.enter_context
            lnyT_sb = ctx(nc.sbuf_tensor("lnyT_sb", [128, 16 * T], BF16))
            c_st.wait(sy, c_done_st)
            for kq in range(4):
                c_x.inc(sy.dma_start(
                    sb3(lnyT_sb[:, kq * 4 * T:(kq + 1) * 4 * T], 4),
                    dr3(lnyT_s[kq * 512:(kq + 1) * 512, :])), 16)
            lnyT_ld = c_x.n
            wg_sb = [ctx(nc.sbuf_tensor(f"wg_sb{i}", [128, D], BF16))
                     for i in range(2)]
            wu_sb = [ctx(nc.sbuf_tensor(f"wu_sb{i}", [128, D], BF16))
                     for i in range(2)]
            sg = [ctx(nc.sbuf_tensor(f"sg{i}", [128, CH], BF16)) for i in range(2)]
            hbf = [ctx(nc.sbuf_tensor(f"hbf{i}", [128, CH], BF16)) for i in range(2)]
            ps_g = [ctx(nc.psum_tensor(f"ps_g{i}", [128, CH], F32)) for i in range(2)]
            ps_u = [ctx(nc.psum_tensor(f"ps_u{i}", [128, CH], F32)) for i in range(2)]

            w_loaded = [None, None]
            w_free = [None, None]
            psg_free = [None, None]
            psu_free = [None, None]
            hbf_free = [None, None]
            sg_free = [None, None]

            for m in range(64):
                wi = m % 2
                wfree(ge, w_free[wi])
                c_w.inc(ge.dma_start(
                    sb3(wg_sb[wi][:, :], 16),
                    dr3(wgT[:, m * 128:(m + 1) * 128])), 16)
                c_w.inc(ge.dma_start(
                    sb3(wu_sb[wi][:, :], 16),
                    dr3(wuT[:, m * 128:(m + 1) * 128])), 16)
                w_loaded[wi] = c_w.mark()
                for g in range(NSEQ * NCHUNK):
                    pi = g % 2
                    wfree(te, w_loaded[wi])
                    c_x.wait(te, lnyT_ld)
                    wfree(te, psg_free[pi])
                    for k in range(16):
                        ins = te.matmul(ps_g[pi][:, :],
                                        wg_sb[wi][:, k * 128:(k + 1) * 128],
                                        lnyT_sb[:, k * T + g * CH:
                                                k * T + (g + 1) * CH],
                                        start=(k == 0), stop=(k == 15))
                    c_t.inc(ins)
                    g_mm = c_t.n
                    wfree(te, psu_free[pi])
                    for k in range(16):
                        ins = te.matmul(ps_u[pi][:, :],
                                        wu_sb[wi][:, k * 128:(k + 1) * 128],
                                        lnyT_sb[:, k * T + g * CH:
                                                k * T + (g + 1) * CH],
                                        start=(k == 0), stop=(k == 15))
                    c_t.inc(ins)
                    u_mm = c_t.n
                    c_t.wait(se, g_mm)
                    wfree(se, sg_free[pi])
                    c_s.inc(se.activation(sg[pi][:, :], ps_g[pi][:, :], AF.Silu))
                    psg_free[pi] = c_s.mark()
                    sg_s = c_s.n
                    c_t.wait(ve, u_mm)
                    c_s.wait(ve, sg_s)
                    wfree(ve, hbf_free[pi])
                    c_v.inc(ve.tensor_mul(hbf[pi][:, :], sg[pi][:, :],
                                          ps_u[pi][:, :]))
                    psu_free[pi] = c_v.mark()
                    sg_free[pi] = c_v.mark()
                    h_v = c_v.n
                    c_v.wait(sy, h_v)
                    c_st.inc(sy.dma_start(
                        hT_s[m * 128:(m + 1) * 128, g * CH:(g + 1) * CH],
                        hbf[pi][:, :]), 16)
                    hbf_free[pi] = c_st.mark()
                w_free[wi] = c_t.mark()
            c_st.wait(sy)
            c_x.wait(sy)
            c_w.wait(ge)
        nc.all_engine_barrier()
        d1_done_st = c_st.n

        if upto == "D1":
            _finish(nc, sy, c_st, c_x, dump, dbg, scr)
            return nc

        with ExitStack() as pd2:
            ctx = pd2.enter_context
            wd_sb = ctx(nc.sbuf_tensor("wd_sb", [128, 32 * D], BF16))
            hsl = [ctx(nc.sbuf_tensor(f"hsl{i}", [128, 32 * 128], BF16))
                   for i in range(2)]
            dpart = [ctx(nc.sbuf_tensor(f"dpart{i}", [128, D], F32))
                     for i in range(2)]
            dfull = ctx(nc.sbuf_tensor("dfull", [128, D], F32))
            lnd = ctx(nc.sbuf_tensor("lnd", [128, D], F32))
            osb = ctx(nc.sbuf_tensor("osb", [128, D], F32))
            y2 = [ctx(nc.sbuf_tensor(f"y2_{i}", [128, D], F32))
                  for i in range(2)]
            bn24d = ctx(nc.sbuf_tensor("bn24d", [128, 24], F32))
            mvd = ctx(nc.sbuf_tensor("mvd", [128, 2], F32))
            rstdd = ctx(nc.sbuf_tensor("rstdd", [128, 1], F32))
            negmrd = ctx(nc.sbuf_tensor("negmrd", [128, 1], F32))
            ps_d = [ctx(nc.psum_tensor(f"ps_d{i}", [128, D], F32))
                    for i in range(2)]

            hsl_load = [None, None]
            hsl_free = [None, None]
            psd_free = [None, None]
            dpart_free = [None, None]
            dpart_ld = [None, None]
            y2_free = [None, None]
            y2_ld = [None, None]
            osb_free = None
            dpart_all_stored = None

            for hp in range(2):
                c_st.wait(sy, d1_done_st)
                if hp == 1:
                    c_t.wait(ge)
                    wfree(sy, dpart_all_stored)
                for k in range(32):
                    c_w.inc(ge.dma_start(
                        wd_sb[:, k * D:(k + 1) * D],
                        wdT[hp * 4096 + k * 128:
                            hp * 4096 + (k + 1) * 128, :]), 16)
                wd_ready = c_w.n

                def d2_loads(ts, _hp=hp):
                    bi = ts % 2
                    row0 = ts * 128
                    wfree(sy, hsl_free[bi])
                    c_x.inc(sy.dma_start(
                        sb3(hsl[bi][:, :], 32),
                        dr3(hT_s[_hp * 4096:(_hp + 1) * 4096,
                                 row0:row0 + 128])), 16)
                    hsl_load[bi] = c_x.mark()
                    if _hp == 1:
                        wfree(sy, dpart_free[bi])
                        c_x.inc(sy.dma_start(dpart[bi][:, :],
                                             dpart_s[row0:row0 + 128, :]), 16)
                        dpart_ld[bi] = c_x.mark()
                        wfree(sy, y2_free[bi])
                        c_x.inc(sy.dma_start(y2[bi][:, :],
                                             y_s[row0:row0 + 128, :]), 16)
                        y2_ld[bi] = c_x.mark()

                d2_loads(0)
                for ts_i in range(32):
                    bi = ts_i % 2
                    row0 = ts_i * 128
                    if ts_i + 1 < 32:
                        d2_loads(ts_i + 1)
                    pd_i = ts_i % 2
                    c_w.wait(te, wd_ready)
                    wfree(te, hsl_load[bi])
                    wfree(te, psd_free[pd_i])
                    for n in range(4):
                        for k in range(32):
                            ins = te.matmul(
                                ps_d[pd_i][:, n * CH:(n + 1) * CH],
                                hsl[bi][:, k * 128:(k + 1) * 128],
                                wd_sb[:, k * D + n * CH:k * D + (n + 1) * CH],
                                start=(k == 0), stop=(k == 31))
                    c_t.inc(ins)
                    d_mm = c_t.mark()
                    hsl_free[bi] = c_t.mark()
                    if hp == 0:
                        c_t.wait(se, d_mm[1])
                        wfree(se, dpart_free[bi])
                        c_s.inc(se.copy(dpart[bi][:, :], ps_d[pd_i][:, :]))
                        psd_free[pd_i] = c_s.mark()
                        c_s.wait(se)
                        c_st.inc(se.dma_start(dpart_s[row0:row0 + 128, :],
                                              dpart[bi][:, :]), 16)
                        dpart_free[bi] = c_st.mark()
                    else:
                        c_t.wait(ve, d_mm[1])
                        wfree(ve, dpart_ld[bi])
                        c_v.inc(ve.tensor_add(dfull[:, :], ps_d[pd_i][:, :],
                                              dpart[bi][:, :]))
                        psd_free[pd_i] = c_v.mark()
                        dpart_free[bi] = c_v.mark()
                        df_v = c_v.mark()
                        ln_stats(lambda a, b: dfull[:, a:b], bn24d, mvd,
                                 rstdd, negmrd, df_v)
                        c_v.wait(se)
                        c_s.inc(se.activation(lnd[:, :], dfull[:, :], AF.Identity,
                                              bias=negmrd[:, :],
                                              scale=rstdd[:, :]))
                        lnd_s = c_s.n
                        c_s.wait(ve, lnd_s)
                        wfree(ve, y2_ld[bi])
                        wfree(ve, osb_free)
                        c_v.inc(ve.tensor_add(osb[:, :], lnd[:, :], y2[bi][:, :]))
                        o_v = c_v.n
                        y2_free[bi] = c_v.mark()
                        c_v.wait(sy, o_v)
                        c_st.inc(sy.dma_start(out_d[row0:row0 + 128, :],
                                              osb[:, :]), 16)
                        osb_free = c_st.mark()
                if hp == 0:
                    dpart_all_stored = c_st.mark()
                    dpart_free = [None, None]
                    hsl_load = [None, None]
            c_st.wait(sy)
            c_x.wait(sy)
            c_w.wait(ge)
        nc.all_engine_barrier()
        _finish(nc, sy, c_st, c_x, dump, dbg, scr)
    return nc


def _finish(nc, sy, c_st, c_x, dump, dbg, scr):
    for name, d in dbg.items():
        c_st.inc(sy.dma_start(d, scr[name]), 16)
    c_st.wait(sy)


def kernel(L0_attn_x, L0_attn_wq, L0_attn_w_kv_down, L0_attn_w_k_up,
           L0_attn_w_k_rope, L0_attn_w_v_up, L0_attn_wo,
           L0_ffn_w_gate, L0_ffn_w_up, L0_ffn_w_down):
    x = np.ascontiguousarray(np.asarray(L0_attn_x, dtype=np.float32))
    assert x.shape == (16, S, D)

    def tr(w):
        return np.ascontiguousarray(np.asarray(w, dtype=np.float32).T)

    weights = {
        "wqT": tr(L0_attn_wq),
        "wkvT": tr(L0_attn_w_kv_down),
        "wkrT": tr(L0_attn_w_k_rope),
        "wkuT": tr(L0_attn_w_k_up),
        "wvuT": tr(L0_attn_w_v_up),
        "woT": tr(L0_attn_wo),
        "wgT": tr(L0_ffn_w_gate),
        "wuT": tr(L0_ffn_w_up),
        "wdT": tr(L0_ffn_w_down),
    }
    nc = build_program()
    in_maps = []
    for core in range(8):
        xc = np.ascontiguousarray(
            x[NSEQ * core:NSEQ * (core + 1)].reshape(T, D))
        m = {"x": xc}
        m.update(weights)
        in_maps.append(m)
    res = run_bass_kernel_spmd(nc, in_maps, core_ids=list(range(8)))
    out = np.concatenate(
        [np.asarray(res.results[i]["out"]).reshape(NSEQ, S, D)
         for i in range(8)], axis=0)
    return out


# version 11
# speedup vs baseline: 1.1476x; 1.1476x over previous
import numpy as np
from contextlib import ExitStack

import concourse.bass as bass
import concourse.mybir as mybir
from concourse.bass_utils import run_bass_kernel_spmd

F32 = mybir.dt.float32
BF16 = mybir.dt.bfloat16
AF = mybir.ActivationFunctionType
ALU = mybir.AluOpType
AX = mybir.AxisListType

S = 2048
D = 2048
H = 16
HD = 128
ND = 64
R = 512
HID = 8192
NSEQ = 2
T = NSEQ * S
THETA = 10000.0
EPS = 1e-5
SCALE = float(1.0 / np.float32(np.sqrt(np.float64(HD))))
NEG_BIG = -3.0e38
NCHUNK = 4
CH = 512


class Ctr:
    def __init__(self, stack, nc, name):
        self.h = stack.enter_context(nc.semaphore(name))
        self.n = 0

    def inc(self, ins, k=1):
        ins.then_inc(self.h, k)
        self.n += k
        return self.n

    def wait(self, eng, val=None):
        v = self.n if val is None else val
        assert v <= self.n, "wait on a not-yet-emitted increment"
        if v > 0:
            eng.wait_ge(self.h, v)

    def mark(self):
        return (self, self.n)


def wfree(eng, slot):
    if slot is not None:
        ctr, val = slot
        ctr.wait(eng, val)


def dr3(ap, p=128):
    return ap.rearrange("(k p) c -> p k c", p=p)


def sb3(ap, k):
    return ap.rearrange("p (k c) -> p k c", k=k)


def _rope_tables():
    pos = np.arange(S, dtype=np.float64)
    rd = np.arange(ND)
    freqs = THETA ** (-(2.0 * (rd // 2)) / ND)
    ang = pos[None, :] * freqs[:, None]
    cos64 = np.cos(ang)
    sin64 = np.sin(ang)
    return (np.concatenate([cos64, cos64 * SCALE], 0).astype(np.float32),
            np.concatenate([sin64, sin64 * SCALE], 0).astype(np.float32))


def _rot_mats():
    import ml_dtypes
    mq = np.zeros((HD, HD), dtype=np.float32)
    for i in range(ND // 2):
        a = ND + 2 * i
        mq[a, a + 1] = -1.0
        mq[a + 1, a] = 1.0
    mk = np.zeros((ND, ND), dtype=np.float32)
    for i in range(ND // 2):
        mk[2 * i, 2 * i + 1] = -1.0
        mk[2 * i + 1, 2 * i] = 1.0
    return (np.ascontiguousarray(mq.T).astype(ml_dtypes.bfloat16),
            np.ascontiguousarray(mk.T).astype(ml_dtypes.bfloat16))


def build_program(upto="D2", dump=None):
    nc = bass.Bass("TRN2", target_bir_lowering=False)

    x_in = nc.dram_tensor("x", [T, D], F32, kind="ExternalInput").ap()
    wqT = nc.dram_tensor("wqT", [D, D], F32, kind="ExternalInput").ap()
    wkvT = nc.dram_tensor("wkvT", [D, R], F32, kind="ExternalInput").ap()
    wkrT = nc.dram_tensor("wkrT", [D, ND], F32, kind="ExternalInput").ap()
    wkuT = nc.dram_tensor("wkuT", [R, H * ND], F32, kind="ExternalInput").ap()
    wvuT = nc.dram_tensor("wvuT", [R, D], F32, kind="ExternalInput").ap()
    woT = nc.dram_tensor("woT", [D, D], F32, kind="ExternalInput").ap()
    wgT = nc.dram_tensor("wgT", [D, HID], F32, kind="ExternalInput").ap()
    wuT = nc.dram_tensor("wuT", [D, HID], F32, kind="ExternalInput").ap()
    wdT = nc.dram_tensor("wdT", [HID, D], F32, kind="ExternalInput").ap()
    out_d = nc.dram_tensor("out", [T, D], F32, kind="ExternalOutput").ap()

    scr = {}
    scr["qT_s"] = qT_s = nc.dram_tensor("qT_s", [NSEQ * D, S], BF16).ap()
    scr["knope_s"] = knope_s = nc.dram_tensor("knope_s", [NSEQ * H * ND, S], BF16).ap()
    scr["krope_s"] = krope_s = nc.dram_tensor("krope_s", [NSEQ * ND, S], BF16).ap()
    scr["v_s"] = v_s = nc.dram_tensor("v_s", [T, D], BF16).ap()
    scr["attnT_s"] = attnT_s = nc.dram_tensor("attnT_s", [NSEQ * D, S], BF16).ap()
    scr["y_s"] = y_s = nc.dram_tensor("y_s", [T, D], F32).ap()
    scr["lnyT_s"] = lnyT_s = nc.dram_tensor("lnyT_s", [D, T], BF16).ap()
    scr["hT_s"] = hT_s = nc.dram_tensor("hT_s", [HID, T], BF16).ap()
    scr["dpart_s"] = dpart_s = nc.dram_tensor("dpart_s", [T, D], F32).ap()

    dbg = {}
    if dump:
        for name in ([dump] if isinstance(dump, str) else dump):
            dsrc = scr[name]
            dbg[name] = nc.dram_tensor("dbg_" + name, list(dsrc.shape),
                                       dsrc.dtype, kind="ExternalOutput").ap()

    cos_np, sin_np = _rope_tables()
    mtq_np, mtk_np = _rot_mats()
    cos_d = nc.inline_tensor(cos_np, "cos_t").ap()
    sin_d = nc.inline_tensor(sin_np, "sin_t").ap()
    mtq_d = nc.inline_tensor(np.asarray(mtq_np), "mtq").ap()
    mtk_d = nc.inline_tensor(np.asarray(mtk_np), "mtk").ap()

    te, ve, se, ge, sy = nc.tensor, nc.vector, nc.scalar, nc.gpsimd, nc.sync

    with ExitStack() as topst, nc.Block():
        ident = nc.alloc_sbuf_tensor("ident", [128, 128], BF16)
        eps_sb = nc.alloc_sbuf_tensor("eps_sb", [128, 1], F32)

        c_w = Ctr(topst, nc, "c_w")
        c_x = Ctr(topst, nc, "c_x")
        c_st = Ctr(topst, nc, "c_st")
        c_t = Ctr(topst, nc, "c_t")
        c_v = Ctr(topst, nc, "c_v")
        c_s = Ctr(topst, nc, "c_s")
        c_g = Ctr(topst, nc, "c_g")

        with ExitStack() as ini:
            iop = ini.enter_context(nc.sbuf_tensor("iop", [128, 128], F32))
            io2 = ini.enter_context(nc.sbuf_tensor("io2", [128, 128], F32))
            c_g.inc(ge.iota(iop[:, :], [[0, 128]], channel_multiplier=1,
                            allow_small_or_imprecise_dtypes=True))
            c_g.inc(ge.iota(io2[:, :], [[1, 128]], channel_multiplier=0,
                            allow_small_or_imprecise_dtypes=True))
            c_g.wait(ve)
            c_v.inc(ve.tensor_tensor(ident[:, :], io2[:, :], iop[:, :],
                                     ALU.is_equal))
            c_v.inc(ve.memset(eps_sb[:, :], EPS))
            c_v.wait(ve)
            ident_ready = c_v.n
            c_v.wait(ge)

        def ln_stats(src_ap_fn, bn24, mv, rstd, negmr, ready_slot):
            wfree(ve, ready_slot)
            for i in range(4):
                ins = ve.bn_stats(bn24[:, 6 * i:6 * i + 6],
                                  src_ap_fn(512 * i, 512 * (i + 1)))
            c_v.inc(ins)
            c_v.wait(ve)
            c_v.inc(ve.bn_aggr(mv[:, :], bn24[:, :]))
            aggr_m = c_v.mark()
            wfree(se, aggr_m)
            c_s.wait(se)
            c_s.inc(se.activation(rstd[:, :], mv[:, 1:2], AF.Ln,
                                  bias=eps_sb[:, :]))
            c_s.wait(se)
            c_s.inc(se.activation(rstd[:, :], rstd[:, :], AF.Exp, scale=-0.5))
            rstd_m = c_s.mark()
            wfree(ve, rstd_m)
            c_v.inc(ve.tensor_scalar(negmr[:, :], mv[:, 0:1], rstd[:, :], -1.0,
                                     ALU.mult, ALU.mult))
            return c_v.mark()

        with ExitStack() as pa:
            ctx = pa.enter_context
            cos_sb = ctx(nc.sbuf_tensor("cos_sb", [HD, S], F32))
            sin_sb = ctx(nc.sbuf_tensor("sin_sb", [HD, S], F32))
            mtq_sb = ctx(nc.sbuf_tensor("mtq_sb", [HD, HD], BF16))
            mtk_sb = ctx(nc.sbuf_tensor("mtk_sb", [ND, ND], BF16))
            c_x.inc(sy.dma_start(cos_sb[:, :], cos_d), 16)
            c_x.inc(sy.dma_start(sin_sb[:, :], sin_d), 16)
            c_x.inc(sy.dma_start(mtq_sb[:, :], mtq_d), 16)
            c_x.inc(sy.dma_start(mtk_sb[:, :], mtk_d), 16)
            consts_ld = c_x.n

            wq_sb = ctx(nc.sbuf_tensor("wq_sb", [128, 16 * D], BF16))
            wkv_sb = ctx(nc.sbuf_tensor("wkv_sb", [128, 4 * D], BF16))
            wkr_sb = ctx(nc.sbuf_tensor("wkr_sb", [128, 16 * ND], BF16))
            wku_sb = ctx(nc.sbuf_tensor("wku_sb", [128, 8 * R], BF16))
            wvu_sb = ctx(nc.sbuf_tensor("wvu_sb", [128, 4 * D], BF16))
            for m in range(16):
                c_w.inc(ge.dma_start(
                    sb3(wq_sb[:, m * D:(m + 1) * D], 16),
                    dr3(wqT[:, m * 128:(m + 1) * 128])), 16)
            for m in range(4):
                c_w.inc(ge.dma_start(
                    sb3(wkv_sb[:, m * D:(m + 1) * D], 16),
                    dr3(wkvT[:, m * 128:(m + 1) * 128])), 16)
            c_w.inc(ge.dma_start(sb3(wkr_sb[:, :], 16), dr3(wkrT)), 16)
            for m in range(8):
                c_w.inc(ge.dma_start(
                    sb3(wku_sb[:, m * R:(m + 1) * R], 4),
                    dr3(wkuT[:, m * 128:(m + 1) * 128])), 16)
            for m in range(4):
                c_w.inc(ge.dma_start(
                    wvu_sb[:, m * D:(m + 1) * D],
                    wvuT[m * 128:(m + 1) * 128, :]), 16)
            w_ready = c_w.n

            xin = [ctx(nc.sbuf_tensor(f"xin{i}", [128, D], F32)) for i in range(2)]
            lnx = [ctx(nc.sbuf_tensor(f"lnx{i}", [128, D], BF16)) for i in range(2)]
            lnxT = [ctx(nc.sbuf_tensor(f"lnxT{i}", [128, 16 * CH], BF16))
                    for i in range(2)]
            ckvT = ctx(nc.sbuf_tensor("ckvT", [128, 4 * CH], BF16))
            bn24 = [ctx(nc.sbuf_tensor(f"bn24_{i}", [128, 24], F32)) for i in range(2)]
            mv = [ctx(nc.sbuf_tensor(f"mv{i}", [128, 2], F32)) for i in range(2)]
            rstd = [ctx(nc.sbuf_tensor(f"rstd{i}", [128, 1], F32)) for i in range(2)]
            negmr = [ctx(nc.sbuf_tensor(f"negmr{i}", [128, 1], F32)) for i in range(2)]
            qbf = [ctx(nc.sbuf_tensor(f"qbf{i}", [128, CH], BF16)) for i in range(2)]
            tcos = [ctx(nc.sbuf_tensor(f"tcos{i}", [128, CH], F32)) for i in range(2)]
            trot = [ctx(nc.sbuf_tensor(f"trot{i}", [128, CH], F32)) for i in range(2)]
            qfin = [ctx(nc.sbuf_tensor(f"qfin{i}", [128, CH], BF16)) for i in range(2)]
            evac = [ctx(nc.sbuf_tensor(f"evac{i}", [128, CH], BF16)) for i in range(2)]

            ps_tr = [ctx(nc.psum_tensor(f"ps_tr{i}", [128, CH], F32)) for i in range(2)]
            ps_pj = [ctx(nc.psum_tensor(f"ps_pj{i}", [128, CH], F32)) for i in range(2)]
            ps_rt = [ctx(nc.psum_tensor(f"ps_rt{i}", [128, CH], F32)) for i in range(2)]

            xin_free = [None, None]
            lnx_free = [None, None]
            lnxT_free = [None, None]
            pstr_free = [None, None]
            pspj_free = [None, None]
            psrt_free = [None, None]
            qbf_free = [None, None]
            evac_free = [None, None]
            qfin_free = [None, None]

            for g in range(NSEQ * NCHUNK):
                s_i, c_i = divmod(g, NCHUNK)
                gl = lnxT[g % 2]
                for u in range(4):
                    ui = u % 2
                    row0 = s_i * S + c_i * CH + u * 128
                    wfree(sy, xin_free[ui])
                    c_x.inc(sy.dma_start(xin[ui][:, :],
                                         x_in[row0:row0 + 128, :]), 16)
                    x_here = c_x.mark()
                    nmr_m = ln_stats(lambda a, b, _x=xin[ui]: _x[:, a:b],
                                     bn24[ui], mv[ui], rstd[ui], negmr[ui],
                                     x_here)
                    wfree(se, nmr_m)
                    wfree(se, lnx_free[ui])
                    c_s.inc(se.activation(lnx[ui][:, :], xin[ui][:, :], AF.Identity,
                                          bias=negmr[ui][:, :],
                                          scale=rstd[ui][:, :]))
                    xin_free[ui] = c_s.mark()
                    lnx_here = c_s.n
                    for grp in range(4):
                        pi = grp % 2
                        c_s.wait(te, lnx_here)
                        c_v.wait(te, ident_ready)
                        wfree(te, pstr_free[pi])
                        for i4 in range(4):
                            k = grp * 4 + i4
                            ins = te.matmul(ps_tr[pi][:, i4 * 128:(i4 + 1) * 128],
                                            lnx[ui][:, k * 128:(k + 1) * 128],
                                            ident[:, :],
                                            start=(i4 == 0), stop=(i4 == 3),
                                            skip_group_check=True)
                        c_t.inc(ins)
                        lnx_free[ui] = c_t.mark()
                        wfree(ve, lnx_free[ui])
                        wfree(ve, lnxT_free[g % 2])
                        dst = gl[:, :].rearrange("p (k c) -> p k c", k=16)[
                            :, grp * 4:grp * 4 + 4, u * 128:(u + 1) * 128]
                        src = ps_tr[pi][:, :].rearrange("p (k c) -> p k c", k=4)
                        c_v.inc(ve.tensor_copy(dst, src))
                        pstr_free[pi] = c_v.mark()
                lnxT_ready_v = c_v.n

                ccols = slice(c_i * CH, (c_i + 1) * CH)
                c_w.wait(te, w_ready)
                c_x.wait(te, consts_ld)
                c_v.wait(te, lnxT_ready_v)

                def q_tail(pi, m, qbf_ready, _ccols=ccols, _s=s_i):
                    wfree(te, qbf_ready)
                    wfree(te, psrt_free[pi])
                    c_t.inc(te.matmul(ps_rt[pi][:, :], mtq_sb[:, :],
                                      qbf[pi][:, :], start=True, stop=True))
                    qbf_rot = c_t.mark()
                    wfree(ve, qbf_rot)
                    c_v.inc(ve.tensor_mul(trot[pi][64:128, :],
                                          ps_rt[pi][64:128, :],
                                          sin_sb[64:128, _ccols]))
                    psrt_free[pi] = c_v.mark()
                    c_v.wait(ve)
                    c_v.inc(ve.tensor_add(qfin[pi][64:128, :],
                                          tcos[pi][64:128, :],
                                          trot[pi][64:128, :]))
                    c_v.inc(ve.tensor_scalar_mul(qfin[pi][0:64, :],
                                                 qbf[pi][0:64, :], SCALE))
                    qbf_free[pi] = c_v.mark()
                    qf_v = c_v.n
                    c_v.wait(se, qf_v)
                    c_st.inc(se.dma_start(
                        qT_s[_s * D + m * 128:_s * D + (m + 1) * 128, _ccols],
                        qfin[pi][:, :]), 16)
                    qfin_free[pi] = c_st.mark()

                q_pend = None
                for m in range(16):
                    pi = m % 2
                    wfree(te, pspj_free[pi])
                    for k in range(16):
                        ins = te.matmul(ps_pj[pi][:, :],
                                        wq_sb[:, m * D + k * 128:
                                              m * D + (k + 1) * 128],
                                        gl[:, k * CH:(k + 1) * CH],
                                        start=(k == 0), stop=(k == 15))
                    c_t.inc(ins)
                    q_mm = c_t.n
                    if q_pend is not None:
                        q_tail(*q_pend)
                    c_t.wait(ve, q_mm)
                    wfree(ve, qbf_free[pi])
                    wfree(ve, qfin_free[pi])
                    c_v.inc(ve.tensor_copy(qbf[pi][:, :], ps_pj[pi][:, :]))
                    c_v.inc(ve.tensor_mul(tcos[pi][64:128, :],
                                          ps_pj[pi][64:128, :],
                                          cos_sb[64:128, ccols]))
                    pspj_free[pi] = c_v.mark()
                    q_pend = (pi, m, c_v.mark())
                q_tail(*q_pend)

                for m in range(4):
                    pi = m % 2
                    wfree(te, pspj_free[pi])
                    for k in range(16):
                        ins = te.matmul(ps_pj[pi][:, :],
                                        wkv_sb[:, m * D + k * 128:
                                               m * D + (k + 1) * 128],
                                        gl[:, k * CH:(k + 1) * CH],
                                        start=(k == 0), stop=(k == 15))
                    c_t.inc(ins)
                    c_t.wait(se, c_t.n)
                    c_s.inc(se.copy(ckvT[:, m * CH:(m + 1) * CH],
                                    ps_pj[pi][:, :]))
                    pspj_free[pi] = c_s.mark()
                ckv_ready_s = c_s.n

                wfree(te, pspj_free[0])
                for k in range(16):
                    ins = te.matmul(ps_pj[0][0:64, :],
                                    wkr_sb[:, k * ND:(k + 1) * ND],
                                    gl[:, k * CH:(k + 1) * CH],
                                    start=(k == 0), stop=(k == 15))
                c_t.inc(ins)
                kr_mm = c_t.n
                c_t.wait(ve, kr_mm)
                wfree(ve, qbf_free[0])
                c_v.inc(ve.tensor_copy(qbf[0][0:64, :], ps_pj[0][0:64, :]))
                c_v.inc(ve.tensor_mul(tcos[0][0:64, :], ps_pj[0][0:64, :],
                                      cos_sb[0:64, ccols]))
                pspj_free[0] = c_v.mark()
                wfree(te, pspj_free[0])
                wfree(te, psrt_free[0])
                c_t.inc(te.matmul(ps_rt[0][0:64, :], mtk_sb[:, :],
                                  qbf[0][0:64, :], start=True, stop=True))
                qbf_free[0] = c_t.mark()
                wfree(ve, qbf_free[0])
                c_v.inc(ve.tensor_mul(trot[0][0:64, :], ps_rt[0][0:64, :],
                                      sin_sb[0:64, ccols]))
                psrt_free[0] = c_v.mark()
                c_v.wait(ve)
                wfree(ve, qfin_free[0])
                c_v.inc(ve.tensor_add(qfin[0][0:64, :], tcos[0][0:64, :],
                                      trot[0][0:64, :]))
                c_v.wait(se)
                c_st.inc(se.dma_start(
                    krope_s[s_i * ND:(s_i + 1) * ND, ccols],
                    qfin[0][0:64, :]), 16)
                qfin_free[0] = c_st.mark()

                for m in range(8):
                    pi = m % 2
                    wfree(te, pspj_free[pi])
                    c_s.wait(te, ckv_ready_s)
                    for k in range(4):
                        ins = te.matmul(ps_pj[pi][:, :],
                                        wku_sb[:, m * R + k * 128:
                                               m * R + (k + 1) * 128],
                                        ckvT[:, k * CH:(k + 1) * CH],
                                        start=(k == 0), stop=(k == 3))
                    c_t.inc(ins)
                    c_t.wait(se, c_t.n)
                    wfree(se, evac_free[pi])
                    c_s.inc(se.copy(evac[pi][:, :], ps_pj[pi][:, :]))
                    pspj_free[pi] = c_s.mark()
                    c_s.wait(se)
                    c_st.inc(se.dma_start(
                        knope_s[s_i * H * ND + m * 128:
                                s_i * H * ND + (m + 1) * 128, ccols],
                        evac[pi][:, :]), 16)
                    evac_free[pi] = c_st.mark()

                for u in range(4):
                    for n in range(4):
                        pi = (u * 4 + n) % 2
                        wfree(te, pspj_free[pi])
                        for k in range(4):
                            ins = te.matmul(
                                ps_pj[pi][:, :],
                                ckvT[:, k * CH + u * 128:k * CH + (u + 1) * 128],
                                wvu_sb[:, k * D + n * CH:k * D + (n + 1) * CH],
                                start=(k == 0), stop=(k == 3))
                        c_t.inc(ins)
                        c_t.wait(se, c_t.n)
                        wfree(se, evac_free[pi])
                        c_s.inc(se.copy(evac[pi][:, :], ps_pj[pi][:, :]))
                        pspj_free[pi] = c_s.mark()
                        c_s.wait(se)
                        c_st.inc(se.dma_start(
                            v_s[s_i * S + c_i * CH + u * 128:
                                s_i * S + c_i * CH + (u + 1) * 128,
                                n * CH:(n + 1) * CH],
                            evac[pi][:, :]), 16)
                        evac_free[pi] = c_st.mark()
                lnxT_free[g % 2] = c_t.mark()

            c_st.wait(sy)
            c_x.wait(sy)
            c_w.wait(ge)
        nc.all_engine_barrier()
        a_done_st = c_st.n

        if upto == "A":
            _finish(nc, sy, c_st, c_x, dump, dbg, scr)
            return nc

        with ExitStack() as pb:
            ctx = pb.enter_context
            masks = ctx(nc.sbuf_tensor("masks", [128, 4 * CH], F32))
            iof = ctx(nc.sbuf_tensor("iof", [128, CH], F32))
            c_g.inc(ge.iota(iof[:, :], [[1, CH]], channel_multiplier=-1,
                            allow_small_or_imprecise_dtypes=True))
            c_g.wait(ve)
            for r in range(4):
                c_v.inc(ve.tensor_single_scalar(masks[:, r * CH:(r + 1) * CH],
                                                iof[:, :], float(r * 128),
                                                ALU.is_gt))
                c_v.wait(ve)
                c_v.inc(ve.tensor_scalar_mul(masks[:, r * CH:(r + 1) * CH],
                                             masks[:, r * CH:(r + 1) * CH],
                                             NEG_BIG))
                c_v.wait(ve)
            masks_v = c_v.n

            q_sb = [ctx(nc.sbuf_tensor(f"q_sb{i}", [128, S], BF16)) for i in range(2)]
            kc_sb = [ctx(nc.sbuf_tensor(f"kc_sb{i}", [128, S], BF16)) for i in range(2)]
            v_hb = [ctx(nc.sbuf_tensor(f"v_hb{i}", [128, H * 128], BF16))
                    for i in range(2)]
            row = [ctx(nc.sbuf_tensor(f"row{i}", [128, S], F32)) for i in range(2)]
            pbf = [ctx(nc.sbuf_tensor(f"pbf{i}", [128, S], BF16)) for i in range(3)]
            diag = [ctx(nc.sbuf_tensor(f"diag{i}", [128, 128], BF16))
                    for i in range(3)]
            cmax = [ctx(nc.sbuf_tensor(f"cmax{i}", [128, 4], F32))
                    for i in range(2)]
            pT_sb = ctx(nc.sbuf_tensor("pT_sb", [128, 16 * CH], BF16))
            attbf = [ctx(nc.sbuf_tensor(f"attbf{i}", [128, CH], BF16))
                     for i in range(2)]
            negm = [ctx(nc.sbuf_tensor(f"negm{i}", [128, 1], F32))
                    for i in range(2)]
            den = [ctx(nc.sbuf_tensor(f"den{i}", [128, 1], F32))
                   for i in range(2)]
            dinv = [ctx(nc.sbuf_tensor(f"dinv{i}", [128, 1], F32))
                    for i in range(2)]

            ps_row = [ctx(nc.psum_tensor(f"ps_row{i}", [128, CH], F32))
                      for i in range(4)]
            ps_pt = [ctx(nc.psum_tensor(f"ps_pt{i}", [128, CH], F32))
                     for i in range(2)]
            ps_av = [ctx(nc.psum_tensor(f"ps_av{i}", [128, CH], F32))
                     for i in range(2)]

            qkv_free = [None, None]
            exp_mark = [None, None]
            att_free = [None, None]
            pspt_free = [None, None]
            psav_free = [None, None]
            load_done = [None, None]
            bank_free = [None] * 4
            row_free = [None, None]
            pbf_free = [None] * 3

            def b_transposes(p):
                qt, qs, qtl, bi, s_i, h = (p[k] for k in
                                           ("qt", "qs", "qtl", "bi", "s_i", "h"))
                pb = pbf[qt % 3]
                dg = diag[qt % 3]
                wfree(te, p["pn_v"])
                ngrp = qt // 4 + 1
                for jg in range(ngrp):
                    gsz = 4 if jg < qt // 4 else (qt % 4 + 1)
                    pi = jg % 2
                    wfree(te, pspt_free[pi])
                    for i4 in range(gsz):
                        jt = jg * 4 + i4
                        ins = te.matmul(
                            ps_pt[pi][:, i4 * 128:(i4 + 1) * 128],
                            pb[:, jt * 128:(jt + 1) * 128],
                            dg[:, :],
                            start=(i4 == 0), stop=(i4 == gsz - 1),
                            skip_group_check=True)
                    c_t.inc(ins)
                    grp_m = c_t.mark()
                    dst = pT_sb[:, :].rearrange("p (j c) -> p j c", j=16)[
                        :, jg * 4:jg * 4 + gsz, qtl * 128:(qtl + 1) * 128]
                    srcp = ps_pt[pi][:, 0:gsz * 128].rearrange(
                        "p (j c) -> p j c", j=gsz)
                    if jg % 2 == 0:
                        wfree(ve, grp_m)
                        c_v.inc(ve.tensor_copy(dst, srcp))
                        pspt_free[pi] = c_v.mark()
                    else:
                        wfree(se, grp_m)
                        c_s.inc(se.copy(dst, srcp))
                        pspt_free[pi] = c_s.mark()
                pbf_free[qt % 3] = c_t.mark()
                if qtl == 3:
                    ai = qs % 2
                    wfree(te, psav_free[ai])
                    wfree(te, pspt_free[0])
                    wfree(te, pspt_free[1])
                    for jt in range(4 * qs + 4):
                        if jt < 4 * qs:
                            ins = te.matmul(
                                ps_av[ai][:, :],
                                v_hb[bi][:, jt * 128:(jt + 1) * 128],
                                pT_sb[:, jt * CH:(jt + 1) * CH],
                                start=(jt == 0), stop=False,
                                skip_group_check=True)
                        else:
                            i = jt - 4 * qs
                            ins = te.matmul(
                                ps_av[ai][:, i * 128:CH],
                                v_hb[bi][:, jt * 128:(jt + 1) * 128],
                                pT_sb[:, jt * CH + i * 128:(jt + 1) * CH],
                                start=(jt == 0), stop=(jt == 4 * qs + 3),
                                skip_group_check=True)
                    c_t.inc(ins)
                    av_mm = c_t.n
                    c_t.wait(se, av_mm)
                    wfree(se, att_free[ai])
                    c_s.inc(se.copy(attbf[ai][:, :], ps_av[ai][:, :]))
                    psav_free[ai] = c_s.mark()
                    c_s.wait(se)
                    c_st.inc(se.dma_start(
                        attnT_s[s_i * D + h * 128:s_i * D + (h + 1) * 128,
                                qs * CH:(qs + 1) * CH], attbf[ai][:, :]), 16)
                    att_free[ai] = c_st.mark()

            def b_softmax(p):
                qt, nch, ncol, rw = p["qt"], p["nch"], p["ncol"], p["rw"]
                p2 = qt % 2
                c_v.wait(ve)
                wfree(ve, exp_mark[p2])
                c_v.inc(ve.reduce_max(negm[p2][:, :], cmax[p2][:, 0:nch],
                                      axis=AX.X, negate=True))
                negm_g = c_v.mark()
                wfree(se, negm_g)
                wfree(se, pbf_free[qt % 3])
                c_s.inc(se.activation(pbf[qt % 3][:, 0:ncol],
                                      rw[:, 0:ncol], AF.Exp,
                                      bias=negm[p2][:, :], scale=1.0,
                                      accum_out=den[p2][:, :]))
                row_free[p2] = c_s.mark()
                exp_mark[p2] = c_s.mark()
                wfree(ve, exp_mark[p2])
                c_v.inc(ve.reciprocal(dinv[p2][:, :], den[p2][:, :]))
                c_v.wait(ve)
                c_v.inc(ve.tensor_scalar_mul(diag[qt % 3][:, :],
                                             ident[:, :], dinv[p2][:, :]))
                p["pn_v"] = c_v.mark()
                return p

            pending = []
            soft_pending = None
            for hi in range(NSEQ * H):
                s_i, h = divmod(hi, H)
                bi = hi % 2
                wfree(sy, qkv_free[bi])
                c_st.wait(sy, a_done_st)
                c_x.inc(sy.dma_start(
                    q_sb[bi][:, :],
                    qT_s[s_i * D + h * 128:s_i * D + (h + 1) * 128, :]), 16)
                c_x.inc(sy.dma_start(
                    kc_sb[bi][0:64, :],
                    knope_s[s_i * H * ND + h * ND:
                            s_i * H * ND + (h + 1) * ND, :]), 16)
                c_x.inc(sy.dma_start(
                    kc_sb[bi][64:128, :],
                    krope_s[s_i * ND:(s_i + 1) * ND, :]), 16)
                c_x.inc(sy.dma_start(
                    sb3(v_hb[bi][:, :], 16),
                    dr3(v_s[s_i * S:(s_i + 1) * S, h * 128:(h + 1) * 128])), 16)
                load_done[bi] = c_x.mark()

                for qs in range(4):
                    nch = qs + 1
                    ncol = nch * CH
                    for qtl in range(4):
                        qt = qs * 4 + qtl
                        rw = row[qt % 2]
                        wfree(se, row_free[qt % 2])
                        wfree(ve, row_free[qt % 2])
                        if qtl == 0:
                            wfree(te, load_done[bi])
                        for jc in range(nch):
                            wfree(te, bank_free[jc])
                            ins = te.matmul(
                                ps_row[jc][:, :],
                                q_sb[bi][:, qt * 128:(qt + 1) * 128],
                                kc_sb[bi][:, jc * CH:(jc + 1) * CH],
                                start=True, stop=True)
                            c_t.inc(ins)
                            mm_m = c_t.n
                            p2 = qt % 2
                            if jc == qs:
                                c_t.wait(ve, mm_m)
                                c_v.wait(ve, masks_v)
                                c_v.inc(ve.tensor_tensor(
                                    rw[:, jc * CH:(jc + 1) * CH],
                                    ps_row[jc][:, :],
                                    masks[:, qtl * CH:(qtl + 1) * CH],
                                    ALU.add))
                                bank_free[jc] = c_v.mark()
                            else:
                                c_t.wait(se, mm_m)
                                c_s.inc(se.copy(rw[:, jc * CH:(jc + 1) * CH],
                                                ps_row[jc][:, :]))
                                bank_free[jc] = c_s.mark()
                            wfree(ve, bank_free[jc])
                            c_v.inc(ve.reduce_max(
                                cmax[p2][:, jc:jc + 1],
                                rw[:, jc * CH:(jc + 1) * CH], axis=AX.X))
                        if len(pending) >= 2:
                            b_transposes(pending.pop(0))
                        if soft_pending is not None:
                            pending.append(b_softmax(soft_pending))
                            soft_pending = None
                        soft_pending = {"qt": qt, "qs": qs, "qtl": qtl,
                                        "bi": bi, "s_i": s_i, "h": h,
                                        "rw": rw, "nch": nch, "ncol": ncol}
                if soft_pending is not None:
                    pending.append(b_softmax(soft_pending))
                    soft_pending = None
                while pending:
                    b_transposes(pending.pop(0))
                qkv_free[bi] = c_t.mark()
            c_st.wait(sy)
            c_x.wait(sy)
        nc.all_engine_barrier()
        b_done_st = c_st.n

        if upto == "B":
            _finish(nc, sy, c_st, c_x, dump, dbg, scr)
            return nc

        with ExitStack() as pc:
            ctx = pc.enter_context
            wo_sb = ctx(nc.sbuf_tensor("wo_sb", [128, 16 * D], BF16))
            for k in range(16):
                c_w.inc(ge.dma_start(wo_sb[:, k * D:(k + 1) * D],
                                     woT[k * 128:(k + 1) * 128, :]), 16)
            wo_ready = c_w.n
            atc = [ctx(nc.sbuf_tensor(f"atc{i}", [128, 16 * CH], BF16))
                   for i in range(2)]
            xin2 = [ctx(nc.sbuf_tensor(f"xin2_{i}", [128, D], F32))
                    for i in range(2)]
            lnat = ctx(nc.sbuf_tensor("lnat", [128, D], F32))
            y_sb = [ctx(nc.sbuf_tensor(f"y_sb{i}", [128, D], F32)) for i in range(2)]
            lny = [ctx(nc.sbuf_tensor(f"lny{i}", [128, D], BF16)) for i in range(2)]
            lnyTst = [ctx(nc.sbuf_tensor(f"lnyTst{i}", [128, 16 * 128], BF16))
                      for i in range(2)]
            bn24c = [ctx(nc.sbuf_tensor(f"bn24c{i}", [128, 24], F32))
                     for i in range(2)]
            mvc = [ctx(nc.sbuf_tensor(f"mvc{i}", [128, 2], F32)) for i in range(2)]
            rstdc = [ctx(nc.sbuf_tensor(f"rstdc{i}", [128, 1], F32))
                     for i in range(2)]
            negmrc = [ctx(nc.sbuf_tensor(f"negmrc{i}", [128, 1], F32))
                      for i in range(2)]
            ps_pr = ctx(nc.psum_tensor("ps_pr", [128, D], F32))
            ps_t2 = [ctx(nc.psum_tensor(f"ps_t2{i}", [128, CH], F32))
                     for i in range(2)]

            atc_load = [None, None]
            atc_free = [None, None]
            pr_readers_s = None
            pr_readers_v = None
            x2_free = [None, None]
            y_free = [None, None]
            lny_free = [None, None]
            pst2_free = [None, None]
            lnyTst_free = [None, None]

            def c_tail(p):
                ui, lny_s, row0, tg = p["ui"], p["lny_s"], p["row0"], p["tg"]
                wfree(ve, lnyTst_free[ui])
                for grp in range(4):
                    pi = grp % 2
                    c_s.wait(te, lny_s)
                    wfree(te, pst2_free[pi])
                    for i4 in range(4):
                        k = grp * 4 + i4
                        ins = te.matmul(
                            ps_t2[pi][:, i4 * 128:(i4 + 1) * 128],
                            lny[ui][:, k * 128:(k + 1) * 128],
                            ident[:, :],
                            start=(i4 == 0), stop=(i4 == 3),
                            skip_group_check=True)
                    c_t.inc(ins)
                    lny_free[ui] = c_t.mark()
                    c_t.wait(ve)
                    c_v.inc(ve.tensor_copy(
                        lnyTst[ui][:, grp * 512:(grp + 1) * 512],
                        ps_t2[pi][:, :]))
                    pst2_free[pi] = c_v.mark()
                lnyT_v = c_v.n
                c_v.wait(se, lnyT_v)
                c_st.inc(se.dma_start(
                    dr3(lnyT_s[:, tg:tg + 128]),
                    sb3(lnyTst[ui][:, :], 16)), 16)
                lnyTst_free[ui] = c_st.mark()

            c_pending = None
            for g in range(NSEQ * NCHUNK):
                s_i, c_i = divmod(g, NCHUNK)
                gi = g % 2
                if g == 0:
                    c_st.wait(sy, b_done_st)
                    c_x.inc(sy.dma_start(
                        sb3(atc[0][:, :], 16),
                        dr3(attnT_s[0:D, 0:CH])), 16)
                    atc_load[0] = c_x.mark()
                if g + 1 < NSEQ * NCHUNK:
                    s_n, c_n = divmod(g + 1, NCHUNK)
                    wfree(sy, atc_free[(g + 1) % 2])
                    c_x.inc(sy.dma_start(
                        sb3(atc[(g + 1) % 2][:, :], 16),
                        dr3(attnT_s[s_n * D:(s_n + 1) * D,
                                    c_n * CH:(c_n + 1) * CH])), 16)
                    atc_load[(g + 1) % 2] = c_x.mark()
                for u in range(4):
                    ui = u % 2
                    row0 = s_i * S + c_i * CH + u * 128
                    wfree(sy, x2_free[ui])
                    c_x.inc(sy.dma_start(xin2[ui][:, :],
                                         x_in[row0:row0 + 128, :]), 16)
                    x2_here = c_x.mark()
                    wfree(te, atc_load[gi])
                    c_w.wait(te, wo_ready)
                    wfree(te, pr_readers_s)
                    wfree(te, pr_readers_v)
                    if c_pending is not None:
                        c_tail(c_pending)
                        c_pending = None
                    for n in range(4):
                        for k in range(16):
                            ins = te.matmul(
                                ps_pr[:, n * CH:(n + 1) * CH],
                                atc[gi][:, k * CH + u * 128:
                                        k * CH + (u + 1) * 128],
                                wo_sb[:, k * D + n * CH:k * D + (n + 1) * CH],
                                start=(k == 0), stop=(k == 15))
                    c_t.inc(ins)
                    pr_mm = c_t.mark()
                    atc_free[gi] = c_t.mark()
                    pr_readers_v = ln_stats(lambda a, b: ps_pr[:, a:b],
                                            bn24c[ui], mvc[ui], rstdc[ui],
                                            negmrc[ui], pr_mm)
                    wfree(se, pr_readers_v)
                    c_s.inc(se.activation(lnat[:, :], ps_pr[:, :], AF.Identity,
                                          bias=negmrc[ui][:, :],
                                          scale=rstdc[ui][:, :]))
                    lnat_s = c_s.n
                    pr_readers_s = c_s.mark()
                    c_s.wait(ve, lnat_s)
                    wfree(ve, x2_here)
                    wfree(ve, y_free[ui])
                    c_v.inc(ve.tensor_add(y_sb[ui][:, :], lnat[:, :],
                                          xin2[ui][:, :]))
                    y_v = c_v.mark()
                    x2_free[ui] = c_v.mark()
                    c_v.wait(se, y_v[1])
                    c_st.inc(se.dma_start(y_s[row0:row0 + 128, :],
                                          y_sb[ui][:, :]), 16)
                    y_free[ui] = c_st.mark()
                    nmr2 = ln_stats(lambda a, b, _y=y_sb[ui]: _y[:, a:b],
                                    bn24c[ui], mvc[ui], rstdc[ui],
                                    negmrc[ui], y_v)
                    wfree(se, nmr2)
                    wfree(se, lny_free[ui])
                    c_s.inc(se.activation(lny[ui][:, :], y_sb[ui][:, :],
                                          AF.Identity, bias=negmrc[ui][:, :],
                                          scale=rstdc[ui][:, :]))
                    lny_s = c_s.n
                    c_pending = {"ui": ui, "lny_s": lny_s, "row0": row0,
                                 "tg": s_i * S + c_i * CH + u * 128}
            if c_pending is not None:
                c_tail(c_pending)
                c_pending = None
            c_st.wait(sy)
            c_x.wait(sy)
            c_w.wait(ge)
        nc.all_engine_barrier()
        c_done_st = c_st.n

        if upto == "C":
            _finish(nc, sy, c_st, c_x, dump, dbg, scr)
            return nc

        with ExitStack() as pd1:
            ctx = pd1.enter_context
            lnyT_sb = ctx(nc.sbuf_tensor("lnyT_sb", [128, 16 * T], BF16))
            c_st.wait(sy, c_done_st)
            for kq in range(4):
                c_x.inc(sy.dma_start(
                    sb3(lnyT_sb[:, kq * 4 * T:(kq + 1) * 4 * T], 4),
                    dr3(lnyT_s[kq * 512:(kq + 1) * 512, :])), 16)
            lnyT_ld = c_x.n
            wg_sb = [ctx(nc.sbuf_tensor(f"wg_sb{i}", [128, D], BF16))
                     for i in range(2)]
            wu_sb = [ctx(nc.sbuf_tensor(f"wu_sb{i}", [128, D], BF16))
                     for i in range(2)]
            sg = [ctx(nc.sbuf_tensor(f"sg{i}", [128, CH], BF16)) for i in range(2)]
            hbf = [ctx(nc.sbuf_tensor(f"hbf{i}", [128, CH], BF16)) for i in range(2)]
            ps_g = [ctx(nc.psum_tensor(f"ps_g{i}", [128, CH], F32)) for i in range(2)]
            ps_u = [ctx(nc.psum_tensor(f"ps_u{i}", [128, CH], F32)) for i in range(2)]

            w_loaded = [None, None]
            w_free = [None, None]
            psg_free = [None, None]
            psu_free = [None, None]
            hbf_free = [None, None]
            sg_free = [None, None]

            for m in range(64):
                wi = m % 2
                wfree(ge, w_free[wi])
                c_w.inc(ge.dma_start(
                    sb3(wg_sb[wi][:, :], 16),
                    dr3(wgT[:, m * 128:(m + 1) * 128])), 16)
                c_w.inc(ge.dma_start(
                    sb3(wu_sb[wi][:, :], 16),
                    dr3(wuT[:, m * 128:(m + 1) * 128])), 16)
                w_loaded[wi] = c_w.mark()
                for g in range(NSEQ * NCHUNK):
                    pi = g % 2
                    wfree(te, w_loaded[wi])
                    c_x.wait(te, lnyT_ld)
                    wfree(te, psg_free[pi])
                    for k in range(16):
                        ins = te.matmul(ps_g[pi][:, :],
                                        wg_sb[wi][:, k * 128:(k + 1) * 128],
                                        lnyT_sb[:, k * T + g * CH:
                                                k * T + (g + 1) * CH],
                                        start=(k == 0), stop=(k == 15))
                    c_t.inc(ins)
                    g_mm = c_t.n
                    wfree(te, psu_free[pi])
                    for k in range(16):
                        ins = te.matmul(ps_u[pi][:, :],
                                        wu_sb[wi][:, k * 128:(k + 1) * 128],
                                        lnyT_sb[:, k * T + g * CH:
                                                k * T + (g + 1) * CH],
                                        start=(k == 0), stop=(k == 15))
                    c_t.inc(ins)
                    u_mm = c_t.n
                    c_t.wait(se, g_mm)
                    wfree(se, sg_free[pi])
                    c_s.inc(se.activation(sg[pi][:, :], ps_g[pi][:, :], AF.Silu))
                    psg_free[pi] = c_s.mark()
                    sg_s = c_s.n
                    c_t.wait(ve, u_mm)
                    c_s.wait(ve, sg_s)
                    wfree(ve, hbf_free[pi])
                    c_v.inc(ve.tensor_mul(hbf[pi][:, :], sg[pi][:, :],
                                          ps_u[pi][:, :]))
                    psu_free[pi] = c_v.mark()
                    sg_free[pi] = c_v.mark()
                    h_v = c_v.n
                    c_v.wait(sy, h_v)
                    c_st.inc(sy.dma_start(
                        hT_s[m * 128:(m + 1) * 128, g * CH:(g + 1) * CH],
                        hbf[pi][:, :]), 16)
                    hbf_free[pi] = c_st.mark()
                w_free[wi] = c_t.mark()
            c_st.wait(sy)
            c_x.wait(sy)
            c_w.wait(ge)
        nc.all_engine_barrier()
        d1_done_st = c_st.n

        if upto == "D1":
            _finish(nc, sy, c_st, c_x, dump, dbg, scr)
            return nc

        with ExitStack() as pd2:
            ctx = pd2.enter_context
            wd_sb = ctx(nc.sbuf_tensor("wd_sb", [128, 32 * D], BF16))
            hsl = [ctx(nc.sbuf_tensor(f"hsl{i}", [128, 32 * 128], BF16))
                   for i in range(2)]
            dpart = [ctx(nc.sbuf_tensor(f"dpart{i}", [128, D], F32))
                     for i in range(2)]
            dfull = ctx(nc.sbuf_tensor("dfull", [128, D], F32))
            lnd = ctx(nc.sbuf_tensor("lnd", [128, D], F32))
            osb = ctx(nc.sbuf_tensor("osb", [128, D], F32))
            y2 = [ctx(nc.sbuf_tensor(f"y2_{i}", [128, D], F32))
                  for i in range(2)]
            bn24d = ctx(nc.sbuf_tensor("bn24d", [128, 24], F32))
            mvd = ctx(nc.sbuf_tensor("mvd", [128, 2], F32))
            rstdd = ctx(nc.sbuf_tensor("rstdd", [128, 1], F32))
            negmrd = ctx(nc.sbuf_tensor("negmrd", [128, 1], F32))
            ps_d = [ctx(nc.psum_tensor(f"ps_d{i}", [128, D], F32))
                    for i in range(2)]

            hsl_load = [None, None]
            hsl_free = [None, None]
            psd_free = [None, None]
            dpart_free = [None, None]
            dpart_ld = [None, None]
            y2_free = [None, None]
            y2_ld = [None, None]
            osb_free = None
            dpart_all_stored = None

            for hp in range(2):
                c_st.wait(sy, d1_done_st)
                if hp == 1:
                    c_t.wait(ge)
                    wfree(sy, dpart_all_stored)
                for k in range(32):
                    c_w.inc(ge.dma_start(
                        wd_sb[:, k * D:(k + 1) * D],
                        wdT[hp * 4096 + k * 128:
                            hp * 4096 + (k + 1) * 128, :]), 16)
                wd_ready = c_w.n

                def d2_loads(ts, _hp=hp):
                    bi = ts % 2
                    row0 = ts * 128
                    wfree(sy, hsl_free[bi])
                    c_x.inc(sy.dma_start(
                        sb3(hsl[bi][:, :], 32),
                        dr3(hT_s[_hp * 4096:(_hp + 1) * 4096,
                                 row0:row0 + 128])), 16)
                    hsl_load[bi] = c_x.mark()
                    if _hp == 1:
                        wfree(sy, dpart_free[bi])
                        c_x.inc(sy.dma_start(dpart[bi][:, :],
                                             dpart_s[row0:row0 + 128, :]), 16)
                        dpart_ld[bi] = c_x.mark()
                        wfree(sy, y2_free[bi])
                        c_x.inc(sy.dma_start(y2[bi][:, :],
                                             y_s[row0:row0 + 128, :]), 16)
                        y2_ld[bi] = c_x.mark()

                d2_loads(0)
                for ts_i in range(32):
                    bi = ts_i % 2
                    row0 = ts_i * 128
                    if ts_i + 1 < 32:
                        d2_loads(ts_i + 1)
                    pd_i = ts_i % 2
                    c_w.wait(te, wd_ready)
                    wfree(te, hsl_load[bi])
                    wfree(te, psd_free[pd_i])
                    for n in range(4):
                        for k in range(32):
                            ins = te.matmul(
                                ps_d[pd_i][:, n * CH:(n + 1) * CH],
                                hsl[bi][:, k * 128:(k + 1) * 128],
                                wd_sb[:, k * D + n * CH:k * D + (n + 1) * CH],
                                start=(k == 0), stop=(k == 31))
                    c_t.inc(ins)
                    d_mm = c_t.mark()
                    hsl_free[bi] = c_t.mark()
                    if hp == 0:
                        c_t.wait(se, d_mm[1])
                        wfree(se, dpart_free[bi])
                        c_s.inc(se.copy(dpart[bi][:, :], ps_d[pd_i][:, :]))
                        psd_free[pd_i] = c_s.mark()
                        c_s.wait(se)
                        c_st.inc(se.dma_start(dpart_s[row0:row0 + 128, :],
                                              dpart[bi][:, :]), 16)
                        dpart_free[bi] = c_st.mark()
                    else:
                        c_t.wait(ve, d_mm[1])
                        wfree(ve, dpart_ld[bi])
                        c_v.inc(ve.tensor_add(dfull[:, :], ps_d[pd_i][:, :],
                                              dpart[bi][:, :]))
                        psd_free[pd_i] = c_v.mark()
                        dpart_free[bi] = c_v.mark()
                        df_v = c_v.mark()
                        ln_stats(lambda a, b: dfull[:, a:b], bn24d, mvd,
                                 rstdd, negmrd, df_v)
                        c_v.wait(se)
                        c_s.inc(se.activation(lnd[:, :], dfull[:, :], AF.Identity,
                                              bias=negmrd[:, :],
                                              scale=rstdd[:, :]))
                        lnd_s = c_s.n
                        c_s.wait(ve, lnd_s)
                        wfree(ve, y2_ld[bi])
                        wfree(ve, osb_free)
                        c_v.inc(ve.tensor_add(osb[:, :], lnd[:, :], y2[bi][:, :]))
                        o_v = c_v.n
                        y2_free[bi] = c_v.mark()
                        c_v.wait(sy, o_v)
                        c_st.inc(sy.dma_start(out_d[row0:row0 + 128, :],
                                              osb[:, :]), 16)
                        osb_free = c_st.mark()
                if hp == 0:
                    dpart_all_stored = c_st.mark()
                    dpart_free = [None, None]
                    hsl_load = [None, None]
            c_st.wait(sy)
            c_x.wait(sy)
            c_w.wait(ge)
        nc.all_engine_barrier()
        _finish(nc, sy, c_st, c_x, dump, dbg, scr)
    return nc


def _finish(nc, sy, c_st, c_x, dump, dbg, scr):
    for name, d in dbg.items():
        c_st.inc(sy.dma_start(d, scr[name]), 16)
    c_st.wait(sy)


def kernel(L0_attn_x, L0_attn_wq, L0_attn_w_kv_down, L0_attn_w_k_up,
           L0_attn_w_k_rope, L0_attn_w_v_up, L0_attn_wo,
           L0_ffn_w_gate, L0_ffn_w_up, L0_ffn_w_down):
    x = np.ascontiguousarray(np.asarray(L0_attn_x, dtype=np.float32))
    assert x.shape == (16, S, D)

    def tr(w):
        return np.ascontiguousarray(np.asarray(w, dtype=np.float32).T)

    weights = {
        "wqT": tr(L0_attn_wq),
        "wkvT": tr(L0_attn_w_kv_down),
        "wkrT": tr(L0_attn_w_k_rope),
        "wkuT": tr(L0_attn_w_k_up),
        "wvuT": tr(L0_attn_w_v_up),
        "woT": tr(L0_attn_wo),
        "wgT": tr(L0_ffn_w_gate),
        "wuT": tr(L0_ffn_w_up),
        "wdT": tr(L0_ffn_w_down),
    }
    nc = build_program()
    in_maps = []
    for core in range(8):
        xc = np.ascontiguousarray(
            x[NSEQ * core:NSEQ * (core + 1)].reshape(T, D))
        m = {"x": xc}
        m.update(weights)
        in_maps.append(m)
    res = run_bass_kernel_spmd(nc, in_maps, core_ids=list(range(8)))
    out = np.concatenate(
        [np.asarray(res.results[i]["out"]).reshape(NSEQ, S, D)
         for i in range(8)], axis=0)
    return out


# version 14
# speedup vs baseline: 1.1969x; 1.0429x over previous
import numpy as np
from contextlib import ExitStack

import concourse.bass as bass
import concourse.mybir as mybir
from concourse.bass_utils import run_bass_kernel_spmd

F32 = mybir.dt.float32
BF16 = mybir.dt.bfloat16
AF = mybir.ActivationFunctionType
ALU = mybir.AluOpType
AX = mybir.AxisListType

S = 2048
D = 2048
H = 16
HD = 128
ND = 64
R = 512
HID = 8192
NSEQ = 2
T = NSEQ * S
THETA = 10000.0
EPS = 1e-5
SCALE = float(1.0 / np.float32(np.sqrt(np.float64(HD))))
NEG_BIG = -3.0e38
NCHUNK = 4
CH = 512


class Ctr:
    def __init__(self, stack, nc, name):
        self.h = stack.enter_context(nc.semaphore(name))
        self.n = 0

    def inc(self, ins, k=1):
        ins.then_inc(self.h, k)
        self.n += k
        return self.n

    def wait(self, eng, val=None):
        v = self.n if val is None else val
        assert v <= self.n, "wait on a not-yet-emitted increment"
        if v > 0:
            eng.wait_ge(self.h, v)

    def mark(self):
        return (self, self.n)


def wfree(eng, slot):
    if slot is not None:
        ctr, val = slot
        ctr.wait(eng, val)


def dr3(ap, p=128):
    return ap.rearrange("(k p) c -> p k c", p=p)


def sb3(ap, k):
    return ap.rearrange("p (k c) -> p k c", k=k)


def _rope_tables():
    pos = np.arange(S, dtype=np.float64)
    rd = np.arange(ND)
    freqs = THETA ** (-(2.0 * (rd // 2)) / ND)
    ang = pos[None, :] * freqs[:, None]
    cos64 = np.cos(ang)
    sin64 = np.sin(ang)
    return (np.concatenate([cos64, cos64 * SCALE], 0).astype(np.float32),
            np.concatenate([sin64, sin64 * SCALE], 0).astype(np.float32))


def _rot_mats():
    import ml_dtypes
    mq = np.zeros((HD, HD), dtype=np.float32)
    for i in range(ND // 2):
        a = ND + 2 * i
        mq[a, a + 1] = -1.0
        mq[a + 1, a] = 1.0
    mk = np.zeros((ND, ND), dtype=np.float32)
    for i in range(ND // 2):
        mk[2 * i, 2 * i + 1] = -1.0
        mk[2 * i + 1, 2 * i] = 1.0
    return (np.ascontiguousarray(mq.T).astype(ml_dtypes.bfloat16),
            np.ascontiguousarray(mk.T).astype(ml_dtypes.bfloat16))


def build_program(upto="D2", dump=None):
    nc = bass.Bass("TRN2", target_bir_lowering=False)

    x_in = nc.dram_tensor("x", [T, D], F32, kind="ExternalInput").ap()
    wqT = nc.dram_tensor("wqT", [D, D], F32, kind="ExternalInput").ap()
    wkvT = nc.dram_tensor("wkvT", [D, R], F32, kind="ExternalInput").ap()
    wkrT = nc.dram_tensor("wkrT", [D, ND], F32, kind="ExternalInput").ap()
    wkuT = nc.dram_tensor("wkuT", [R, H * ND], F32, kind="ExternalInput").ap()
    wvuT = nc.dram_tensor("wvuT", [R, D], F32, kind="ExternalInput").ap()
    woT = nc.dram_tensor("woT", [D, D], F32, kind="ExternalInput").ap()
    wgT = nc.dram_tensor("wgT", [D, HID], F32, kind="ExternalInput").ap()
    wuT = nc.dram_tensor("wuT", [D, HID], F32, kind="ExternalInput").ap()
    wdT = nc.dram_tensor("wdT", [HID, D], F32, kind="ExternalInput").ap()
    out_d = nc.dram_tensor("out", [T, D], F32, kind="ExternalOutput").ap()

    scr = {}
    scr["qT_s"] = qT_s = nc.dram_tensor("qT_s", [NSEQ * D, S], BF16).ap()
    scr["knope_s"] = knope_s = nc.dram_tensor("knope_s", [NSEQ * H * ND, S], BF16).ap()
    scr["krope_s"] = krope_s = nc.dram_tensor("krope_s", [NSEQ * ND, S], BF16).ap()
    scr["v_s"] = v_s = nc.dram_tensor("v_s", [T, D], BF16).ap()
    scr["attnT_s"] = attnT_s = nc.dram_tensor("attnT_s", [NSEQ * D, S], BF16).ap()
    scr["y_s"] = y_s = nc.dram_tensor("y_s", [T, D], F32).ap()
    scr["lnyT_s"] = lnyT_s = nc.dram_tensor("lnyT_s", [D, T], BF16).ap()
    scr["hT_s"] = hT_s = nc.dram_tensor("hT_s", [HID, T], BF16).ap()
    scr["dpart_s"] = dpart_s = nc.dram_tensor("dpart_s", [T, D], F32).ap()

    dbg = {}
    if dump:
        for name in ([dump] if isinstance(dump, str) else dump):
            dsrc = scr[name]
            dbg[name] = nc.dram_tensor("dbg_" + name, list(dsrc.shape),
                                       dsrc.dtype, kind="ExternalOutput").ap()

    cos_np, sin_np = _rope_tables()
    mtq_np, mtk_np = _rot_mats()
    cos_d = nc.inline_tensor(cos_np, "cos_t").ap()
    sin_d = nc.inline_tensor(sin_np, "sin_t").ap()
    mtq_d = nc.inline_tensor(np.asarray(mtq_np), "mtq").ap()
    mtk_d = nc.inline_tensor(np.asarray(mtk_np), "mtk").ap()

    te, ve, se, ge, sy = nc.tensor, nc.vector, nc.scalar, nc.gpsimd, nc.sync

    with ExitStack() as topst, nc.Block():
        ident = nc.alloc_sbuf_tensor("ident", [128, 128], BF16)
        eps_sb = nc.alloc_sbuf_tensor("eps_sb", [128, 1], F32)

        c_w = Ctr(topst, nc, "c_w")
        c_x = Ctr(topst, nc, "c_x")
        c_st = Ctr(topst, nc, "c_st")
        c_t = Ctr(topst, nc, "c_t")
        c_v = Ctr(topst, nc, "c_v")
        c_s = Ctr(topst, nc, "c_s")
        c_g = Ctr(topst, nc, "c_g")

        with ExitStack() as ini:
            iop = ini.enter_context(nc.sbuf_tensor("iop", [128, 128], F32))
            io2 = ini.enter_context(nc.sbuf_tensor("io2", [128, 128], F32))
            c_g.inc(ge.iota(iop[:, :], [[0, 128]], channel_multiplier=1,
                            allow_small_or_imprecise_dtypes=True))
            c_g.inc(ge.iota(io2[:, :], [[1, 128]], channel_multiplier=0,
                            allow_small_or_imprecise_dtypes=True))
            c_g.wait(ve)
            c_v.inc(ve.tensor_tensor(ident[:, :], io2[:, :], iop[:, :],
                                     ALU.is_equal))
            c_v.inc(ve.memset(eps_sb[:, :], EPS))
            c_v.wait(ve)
            ident_ready = c_v.n
            c_v.wait(ge)

        def ln_stats(src_ap_fn, bn24, mv, rstd, negmr, ready_slot):
            wfree(ve, ready_slot)
            for i in range(4):
                ins = ve.bn_stats(bn24[:, 6 * i:6 * i + 6],
                                  src_ap_fn(512 * i, 512 * (i + 1)))
            c_v.inc(ins)
            c_v.wait(ve)
            c_v.inc(ve.bn_aggr(mv[:, :], bn24[:, :]))
            aggr_m = c_v.mark()
            wfree(se, aggr_m)
            c_s.wait(se)
            c_s.inc(se.activation(rstd[:, :], mv[:, 1:2], AF.Ln,
                                  bias=eps_sb[:, :]))
            c_s.wait(se)
            c_s.inc(se.activation(rstd[:, :], rstd[:, :], AF.Exp, scale=-0.5))
            rstd_m = c_s.mark()
            wfree(ve, rstd_m)
            c_v.inc(ve.tensor_scalar(negmr[:, :], mv[:, 0:1], rstd[:, :], -1.0,
                                     ALU.mult, ALU.mult))
            return c_v.mark()

        with ExitStack() as pa:
            ctx = pa.enter_context
            cos_sb = ctx(nc.sbuf_tensor("cos_sb", [HD, S], F32))
            sin_sb = ctx(nc.sbuf_tensor("sin_sb", [HD, S], F32))
            mtq_sb = ctx(nc.sbuf_tensor("mtq_sb", [HD, HD], BF16))
            mtk_sb = ctx(nc.sbuf_tensor("mtk_sb", [ND, ND], BF16))
            c_v.wait(sy, ident_ready)
            c_x.inc(sy.dma_start(cos_sb[:, :], cos_d), 16)
            c_x.inc(sy.dma_start(sin_sb[:, :], sin_d), 16)
            c_x.inc(sy.dma_start(mtq_sb[:, :], mtq_d), 16)
            c_x.inc(sy.dma_start(mtk_sb[:, :], mtk_d), 16)
            consts_ld = c_x.n

            wq_sb = ctx(nc.sbuf_tensor("wq_sb", [128, 16 * D], BF16))
            wkv_sb = ctx(nc.sbuf_tensor("wkv_sb", [128, 4 * D], BF16))
            wkr_sb = ctx(nc.sbuf_tensor("wkr_sb", [128, 16 * ND], BF16))
            wku_sb = ctx(nc.sbuf_tensor("wku_sb", [128, 8 * R], BF16))
            wvu_sb = ctx(nc.sbuf_tensor("wvu_sb", [128, 4 * D], BF16))
            for m in range(16):
                c_w.inc(ge.dma_start(
                    sb3(wq_sb[:, m * D:(m + 1) * D], 16),
                    dr3(wqT[:, m * 128:(m + 1) * 128])), 16)
            for m in range(4):
                c_w.inc(ge.dma_start(
                    sb3(wkv_sb[:, m * D:(m + 1) * D], 16),
                    dr3(wkvT[:, m * 128:(m + 1) * 128])), 16)
            c_w.inc(ge.dma_start(sb3(wkr_sb[:, :], 16), dr3(wkrT)), 16)
            for m in range(8):
                c_w.inc(ge.dma_start(
                    sb3(wku_sb[:, m * R:(m + 1) * R], 4),
                    dr3(wkuT[:, m * 128:(m + 1) * 128])), 16)
            for m in range(4):
                c_w.inc(ge.dma_start(
                    wvu_sb[:, m * D:(m + 1) * D],
                    wvuT[m * 128:(m + 1) * 128, :]), 16)
            w_ready = c_w.n

            xin = [ctx(nc.sbuf_tensor(f"xin{i}", [128, D], F32)) for i in range(2)]
            lnx = [ctx(nc.sbuf_tensor(f"lnx{i}", [128, D], BF16)) for i in range(2)]
            lnxT = [ctx(nc.sbuf_tensor(f"lnxT{i}", [128, 16 * CH], BF16))
                    for i in range(2)]
            ckvT = ctx(nc.sbuf_tensor("ckvT", [128, 4 * CH], BF16))
            bn24 = [ctx(nc.sbuf_tensor(f"bn24_{i}", [128, 24], F32)) for i in range(2)]
            mv = [ctx(nc.sbuf_tensor(f"mv{i}", [128, 2], F32)) for i in range(2)]
            rstd = [ctx(nc.sbuf_tensor(f"rstd{i}", [128, 1], F32)) for i in range(2)]
            negmr = [ctx(nc.sbuf_tensor(f"negmr{i}", [128, 1], F32)) for i in range(2)]
            qbf = [ctx(nc.sbuf_tensor(f"qbf{i}", [128, CH], BF16)) for i in range(2)]
            tcos = [ctx(nc.sbuf_tensor(f"tcos{i}", [128, CH], F32)) for i in range(2)]
            trot = [ctx(nc.sbuf_tensor(f"trot{i}", [128, CH], F32)) for i in range(2)]
            qfin = [ctx(nc.sbuf_tensor(f"qfin{i}", [128, CH], BF16)) for i in range(2)]
            evac = [ctx(nc.sbuf_tensor(f"evac{i}", [128, CH], BF16)) for i in range(2)]

            ps_tr = [ctx(nc.psum_tensor(f"ps_tr{i}", [128, CH], F32)) for i in range(2)]
            ps_pj = [ctx(nc.psum_tensor(f"ps_pj{i}", [128, CH], F32)) for i in range(2)]
            ps_rt = [ctx(nc.psum_tensor(f"ps_rt{i}", [128, CH], F32)) for i in range(2)]

            xin_free = [None, None]
            lnx_free = [None, None]
            lnxT_free = [None, None]
            pstr_free = [None, None]
            pspj_free = [None, None]
            psrt_free = [None, None]
            qbf_free = [None, None]
            evac_free = [None, None]
            qfin_free = [None, None]

            for g in range(NSEQ * NCHUNK):
                s_i, c_i = divmod(g, NCHUNK)
                gl = lnxT[g % 2]
                for u in range(4):
                    ui = u % 2
                    row0 = s_i * S + c_i * CH + u * 128
                    wfree(sy, xin_free[ui])
                    c_x.inc(sy.dma_start(xin[ui][:, :],
                                         x_in[row0:row0 + 128, :]), 16)
                    x_here = c_x.mark()
                    nmr_m = ln_stats(lambda a, b, _x=xin[ui]: _x[:, a:b],
                                     bn24[ui], mv[ui], rstd[ui], negmr[ui],
                                     x_here)
                    wfree(se, nmr_m)
                    wfree(se, lnx_free[ui])
                    c_s.inc(se.activation(lnx[ui][:, :], xin[ui][:, :], AF.Identity,
                                          bias=negmr[ui][:, :],
                                          scale=rstd[ui][:, :]))
                    xin_free[ui] = c_s.mark()
                    lnx_here = c_s.n
                    for grp in range(4):
                        pi = grp % 2
                        c_s.wait(te, lnx_here)
                        c_v.wait(te, ident_ready)
                        wfree(te, pstr_free[pi])
                        for i4 in range(4):
                            k = grp * 4 + i4
                            ins = te.matmul(ps_tr[pi][:, i4 * 128:(i4 + 1) * 128],
                                            lnx[ui][:, k * 128:(k + 1) * 128],
                                            ident[:, :],
                                            start=(i4 == 0), stop=(i4 == 3),
                                            skip_group_check=True)
                        c_t.inc(ins)
                        lnx_free[ui] = c_t.mark()
                        wfree(ve, lnx_free[ui])
                        wfree(ve, lnxT_free[g % 2])
                        dst = gl[:, :].rearrange("p (k c) -> p k c", k=16)[
                            :, grp * 4:grp * 4 + 4, u * 128:(u + 1) * 128]
                        src = ps_tr[pi][:, :].rearrange("p (k c) -> p k c", k=4)
                        c_v.inc(ve.tensor_copy(dst, src))
                        pstr_free[pi] = c_v.mark()
                lnxT_ready_v = c_v.n

                ccols = slice(c_i * CH, (c_i + 1) * CH)
                c_w.wait(te, w_ready)
                c_x.wait(te, consts_ld)
                c_v.wait(te, lnxT_ready_v)

                def q_tail(pi, m, qbf_ready, _ccols=ccols, _s=s_i):
                    wfree(te, qbf_ready)
                    wfree(te, psrt_free[pi])
                    c_t.inc(te.matmul(ps_rt[pi][:, :], mtq_sb[:, :],
                                      qbf[pi][:, :], start=True, stop=True))
                    qbf_rot = c_t.mark()
                    wfree(ve, qbf_rot)
                    c_v.inc(ve.tensor_mul(trot[pi][64:128, :],
                                          ps_rt[pi][64:128, :],
                                          sin_sb[64:128, _ccols]))
                    psrt_free[pi] = c_v.mark()
                    c_v.wait(ve)
                    c_v.inc(ve.tensor_add(qfin[pi][64:128, :],
                                          tcos[pi][64:128, :],
                                          trot[pi][64:128, :]))
                    c_v.inc(ve.tensor_scalar_mul(qfin[pi][0:64, :],
                                                 qbf[pi][0:64, :], SCALE))
                    qbf_free[pi] = c_v.mark()
                    qf_v = c_v.n
                    c_v.wait(se, qf_v)
                    c_st.inc(se.dma_start(
                        qT_s[_s * D + m * 128:_s * D + (m + 1) * 128, _ccols],
                        qfin[pi][:, :]), 16)
                    qfin_free[pi] = c_st.mark()

                q_pend = None
                for m in range(16):
                    pi = m % 2
                    wfree(te, pspj_free[pi])
                    for k in range(16):
                        ins = te.matmul(ps_pj[pi][:, :],
                                        wq_sb[:, m * D + k * 128:
                                              m * D + (k + 1) * 128],
                                        gl[:, k * CH:(k + 1) * CH],
                                        start=(k == 0), stop=(k == 15))
                    c_t.inc(ins)
                    q_mm = c_t.n
                    if q_pend is not None:
                        q_tail(*q_pend)
                    c_t.wait(ve, q_mm)
                    wfree(ve, qbf_free[pi])
                    wfree(ve, qfin_free[pi])
                    c_v.inc(ve.tensor_copy(qbf[pi][:, :], ps_pj[pi][:, :]))
                    c_v.inc(ve.tensor_mul(tcos[pi][64:128, :],
                                          ps_pj[pi][64:128, :],
                                          cos_sb[64:128, ccols]))
                    pspj_free[pi] = c_v.mark()
                    q_pend = (pi, m, c_v.mark())
                q_tail(*q_pend)

                for m in range(4):
                    pi = m % 2
                    wfree(te, pspj_free[pi])
                    for k in range(16):
                        ins = te.matmul(ps_pj[pi][:, :],
                                        wkv_sb[:, m * D + k * 128:
                                               m * D + (k + 1) * 128],
                                        gl[:, k * CH:(k + 1) * CH],
                                        start=(k == 0), stop=(k == 15))
                    c_t.inc(ins)
                    c_t.wait(se, c_t.n)
                    c_s.inc(se.copy(ckvT[:, m * CH:(m + 1) * CH],
                                    ps_pj[pi][:, :]))
                    pspj_free[pi] = c_s.mark()
                ckv_ready_s = c_s.n

                wfree(te, pspj_free[0])
                for k in range(16):
                    ins = te.matmul(ps_pj[0][0:64, :],
                                    wkr_sb[:, k * ND:(k + 1) * ND],
                                    gl[:, k * CH:(k + 1) * CH],
                                    start=(k == 0), stop=(k == 15))
                c_t.inc(ins)
                kr_mm = c_t.n
                c_t.wait(ve, kr_mm)
                wfree(ve, qbf_free[0])
                c_v.inc(ve.tensor_copy(qbf[0][0:64, :], ps_pj[0][0:64, :]))
                c_v.inc(ve.tensor_mul(tcos[0][0:64, :], ps_pj[0][0:64, :],
                                      cos_sb[0:64, ccols]))
                pspj_free[0] = c_v.mark()
                wfree(te, pspj_free[0])
                wfree(te, psrt_free[0])
                c_t.inc(te.matmul(ps_rt[0][0:64, :], mtk_sb[:, :],
                                  qbf[0][0:64, :], start=True, stop=True))
                qbf_free[0] = c_t.mark()
                wfree(ve, qbf_free[0])
                c_v.inc(ve.tensor_mul(trot[0][0:64, :], ps_rt[0][0:64, :],
                                      sin_sb[0:64, ccols]))
                psrt_free[0] = c_v.mark()
                c_v.wait(ve)
                wfree(ve, qfin_free[0])
                c_v.inc(ve.tensor_add(qfin[0][0:64, :], tcos[0][0:64, :],
                                      trot[0][0:64, :]))
                c_v.wait(se)
                c_st.inc(se.dma_start(
                    krope_s[s_i * ND:(s_i + 1) * ND, ccols],
                    qfin[0][0:64, :]), 16)
                qfin_free[0] = c_st.mark()

                for m in range(8):
                    pi = m % 2
                    wfree(te, pspj_free[pi])
                    c_s.wait(te, ckv_ready_s)
                    for k in range(4):
                        ins = te.matmul(ps_pj[pi][:, :],
                                        wku_sb[:, m * R + k * 128:
                                               m * R + (k + 1) * 128],
                                        ckvT[:, k * CH:(k + 1) * CH],
                                        start=(k == 0), stop=(k == 3))
                    c_t.inc(ins)
                    c_t.wait(se, c_t.n)
                    wfree(se, evac_free[pi])
                    c_s.inc(se.copy(evac[pi][:, :], ps_pj[pi][:, :]))
                    pspj_free[pi] = c_s.mark()
                    c_s.wait(se)
                    c_st.inc(se.dma_start(
                        knope_s[s_i * H * ND + m * 128:
                                s_i * H * ND + (m + 1) * 128, ccols],
                        evac[pi][:, :]), 16)
                    evac_free[pi] = c_st.mark()

                for u in range(4):
                    for n in range(4):
                        pi = (u * 4 + n) % 2
                        wfree(te, pspj_free[pi])
                        for k in range(4):
                            ins = te.matmul(
                                ps_pj[pi][:, :],
                                ckvT[:, k * CH + u * 128:k * CH + (u + 1) * 128],
                                wvu_sb[:, k * D + n * CH:k * D + (n + 1) * CH],
                                start=(k == 0), stop=(k == 3))
                        c_t.inc(ins)
                        c_t.wait(se, c_t.n)
                        wfree(se, evac_free[pi])
                        c_s.inc(se.copy(evac[pi][:, :], ps_pj[pi][:, :]))
                        pspj_free[pi] = c_s.mark()
                        c_s.wait(se)
                        c_st.inc(se.dma_start(
                            v_s[s_i * S + c_i * CH + u * 128:
                                s_i * S + c_i * CH + (u + 1) * 128,
                                n * CH:(n + 1) * CH],
                            evac[pi][:, :]), 16)
                        evac_free[pi] = c_st.mark()
                lnxT_free[g % 2] = c_t.mark()

            c_st.wait(sy)
            c_x.wait(sy)
            c_w.wait(ge)
        nc.all_engine_barrier()
        a_done_st = c_st.n

        if upto == "A":
            _finish(nc, sy, c_st, c_x, dump, dbg, scr)
            return nc

        with ExitStack() as pb:
            ctx = pb.enter_context
            masks = ctx(nc.sbuf_tensor("masks", [128, 4 * CH], F32))
            iof = ctx(nc.sbuf_tensor("iof", [128, CH], F32))
            c_g.inc(ge.iota(iof[:, :], [[1, CH]], channel_multiplier=-1,
                            allow_small_or_imprecise_dtypes=True))
            c_g.wait(ve)
            for r in range(4):
                c_v.inc(ve.tensor_single_scalar(masks[:, r * CH:(r + 1) * CH],
                                                iof[:, :], float(r * 128),
                                                ALU.is_gt))
                c_v.wait(ve)
                c_v.inc(ve.tensor_scalar_mul(masks[:, r * CH:(r + 1) * CH],
                                             masks[:, r * CH:(r + 1) * CH],
                                             NEG_BIG))
                c_v.wait(ve)
            masks_v = c_v.n

            q_sb = [ctx(nc.sbuf_tensor(f"q_sb{i}", [128, S], BF16)) for i in range(2)]
            kc_sb = [ctx(nc.sbuf_tensor(f"kc_sb{i}", [128, S], BF16)) for i in range(2)]
            v_hb = [ctx(nc.sbuf_tensor(f"v_hb{i}", [128, H * 128], BF16))
                    for i in range(2)]
            row = [ctx(nc.sbuf_tensor(f"row{i}", [128, S], F32)) for i in range(3)]
            pbf = [ctx(nc.sbuf_tensor(f"pbf{i}", [128, S], BF16)) for i in range(3)]
            diag = [ctx(nc.sbuf_tensor(f"diag{i}", [128, 128], BF16))
                    for i in range(3)]
            cmax = [ctx(nc.sbuf_tensor(f"cmax{i}", [128, 4], F32))
                    for i in range(3)]
            pT_sb = ctx(nc.sbuf_tensor("pT_sb", [128, 16 * CH], BF16))
            attbf = [ctx(nc.sbuf_tensor(f"attbf{i}", [128, CH], BF16))
                     for i in range(2)]
            negm = [ctx(nc.sbuf_tensor(f"negm{i}", [128, 1], F32))
                    for i in range(2)]
            den = [ctx(nc.sbuf_tensor(f"den{i}", [128, 1], F32))
                   for i in range(2)]
            dinv = [ctx(nc.sbuf_tensor(f"dinv{i}", [128, 1], F32))
                    for i in range(2)]

            ps_row = [ctx(nc.psum_tensor(f"ps_row{i}", [128, CH], F32))
                      for i in range(4)]
            ps_pt = [ctx(nc.psum_tensor(f"ps_pt{i}", [128, CH], F32))
                     for i in range(2)]
            ps_av = [ctx(nc.psum_tensor(f"ps_av{i}", [128, CH], F32))
                     for i in range(2)]

            qkv_free = [None, None]
            exp_mark = [None, None]
            att_free = [None, None]
            pspt_free = [None, None]
            psav_free = [None, None]
            load_done = [None, None]
            bank_free = [None] * 4
            row_free = [None] * 3
            pbf_free = [None] * 3

            def b_transposes(p):
                qt, qs, qtl, bi, s_i, h = (p[k] for k in
                                           ("qt", "qs", "qtl", "bi", "s_i", "h"))
                pb = pbf[qt % 3]
                dg = diag[qt % 3]
                wfree(te, p["pn_v"])
                ngrp = qt // 4 + 1
                for jg in range(ngrp):
                    gsz = 4 if jg < qt // 4 else (qt % 4 + 1)
                    pi = jg % 2
                    wfree(te, pspt_free[pi])
                    for i4 in range(gsz):
                        jt = jg * 4 + i4
                        ins = te.matmul(
                            ps_pt[pi][:, i4 * 128:(i4 + 1) * 128],
                            pb[:, jt * 128:(jt + 1) * 128],
                            dg[:, :],
                            start=(i4 == 0), stop=(i4 == gsz - 1),
                            skip_group_check=True)
                    c_t.inc(ins)
                    grp_m = c_t.mark()
                    dst = pT_sb[:, :].rearrange("p (j c) -> p j c", j=16)[
                        :, jg * 4:jg * 4 + gsz, qtl * 128:(qtl + 1) * 128]
                    srcp = ps_pt[pi][:, 0:gsz * 128].rearrange(
                        "p (j c) -> p j c", j=gsz)
                    if jg % 2 == 0:
                        wfree(ve, grp_m)
                        c_v.inc(ve.tensor_copy(dst, srcp))
                        pspt_free[pi] = c_v.mark()
                    else:
                        wfree(se, grp_m)
                        c_s.inc(se.copy(dst, srcp))
                        pspt_free[pi] = c_s.mark()
                pbf_free[qt % 3] = c_t.mark()
                if qtl == 3:
                    ai = qs % 2
                    wfree(te, psav_free[ai])
                    wfree(te, pspt_free[0])
                    wfree(te, pspt_free[1])
                    for jt in range(4 * qs + 4):
                        if jt < 4 * qs:
                            ins = te.matmul(
                                ps_av[ai][:, :],
                                v_hb[bi][:, jt * 128:(jt + 1) * 128],
                                pT_sb[:, jt * CH:(jt + 1) * CH],
                                start=(jt == 0), stop=False,
                                skip_group_check=True)
                        else:
                            i = jt - 4 * qs
                            ins = te.matmul(
                                ps_av[ai][:, i * 128:CH],
                                v_hb[bi][:, jt * 128:(jt + 1) * 128],
                                pT_sb[:, jt * CH + i * 128:(jt + 1) * CH],
                                start=(jt == 0), stop=(jt == 4 * qs + 3),
                                skip_group_check=True)
                    c_t.inc(ins)
                    av_mm = c_t.n
                    c_t.wait(se, av_mm)
                    wfree(se, att_free[ai])
                    c_s.inc(se.copy(attbf[ai][:, :], ps_av[ai][:, :]))
                    psav_free[ai] = c_s.mark()
                    c_s.wait(se)
                    c_st.inc(se.dma_start(
                        attnT_s[s_i * D + h * 128:s_i * D + (h + 1) * 128,
                                qs * CH:(qs + 1) * CH], attbf[ai][:, :]), 16)
                    att_free[ai] = c_st.mark()

            def b_softmax(p):
                qt, nch, ncol, rw = p["qt"], p["nch"], p["ncol"], p["rw"]
                p2 = qt % 2
                p3 = qt % 3
                c_v.wait(ve)
                wfree(ve, exp_mark[p2])
                c_v.inc(ve.reduce_max(negm[p2][:, :], cmax[p3][:, 0:nch],
                                      axis=AX.X, negate=True))
                negm_g = c_v.mark()
                wfree(se, negm_g)
                wfree(se, pbf_free[qt % 3])
                c_s.inc(se.activation(pbf[qt % 3][:, 0:ncol],
                                      rw[:, 0:ncol], AF.Exp,
                                      bias=negm[p2][:, :], scale=1.0,
                                      accum_out=den[p2][:, :]))
                row_free[p3] = c_s.mark()
                exp_mark[p2] = c_s.mark()
                wfree(ve, exp_mark[p2])
                c_v.inc(ve.reciprocal(dinv[p2][:, :], den[p2][:, :]))
                c_v.wait(ve)
                wfree(ve, pbf_free[qt % 3])
                c_v.inc(ve.tensor_scalar_mul(diag[qt % 3][:, :],
                                             ident[:, :], dinv[p2][:, :]))
                p["pn_v"] = c_v.mark()
                return p

            pending = []
            soft_pending = []
            for hi in range(NSEQ * H):
                s_i, h = divmod(hi, H)
                bi = hi % 2
                wfree(sy, qkv_free[bi])
                c_st.wait(sy, a_done_st)
                c_x.inc(sy.dma_start(
                    q_sb[bi][:, :],
                    qT_s[s_i * D + h * 128:s_i * D + (h + 1) * 128, :]), 16)
                c_x.inc(sy.dma_start(
                    kc_sb[bi][0:64, :],
                    knope_s[s_i * H * ND + h * ND:
                            s_i * H * ND + (h + 1) * ND, :]), 16)
                c_x.inc(sy.dma_start(
                    kc_sb[bi][64:128, :],
                    krope_s[s_i * ND:(s_i + 1) * ND, :]), 16)
                c_x.inc(sy.dma_start(
                    sb3(v_hb[bi][:, :], 16),
                    dr3(v_s[s_i * S:(s_i + 1) * S, h * 128:(h + 1) * 128])), 16)
                load_done[bi] = c_x.mark()

                for qs in range(4):
                    nch = qs + 1
                    ncol = nch * CH
                    for qtl in range(4):
                        qt = qs * 4 + qtl
                        rw = row[qt % 3]
                        wfree(se, row_free[qt % 3])
                        wfree(ve, row_free[qt % 3])
                        if qtl == 0:
                            wfree(te, load_done[bi])
                        for jc in range(nch):
                            wfree(te, bank_free[jc])
                            ins = te.matmul(
                                ps_row[jc][:, :],
                                q_sb[bi][:, qt * 128:(qt + 1) * 128],
                                kc_sb[bi][:, jc * CH:(jc + 1) * CH],
                                start=True, stop=True)
                            c_t.inc(ins)
                            mm_m = c_t.n
                            p2 = qt % 2
                            if jc == qs:
                                c_t.wait(ve, mm_m)
                                c_v.wait(ve, masks_v)
                                c_v.inc(ve.tensor_tensor(
                                    rw[:, jc * CH:(jc + 1) * CH],
                                    ps_row[jc][:, :],
                                    masks[:, qtl * CH:(qtl + 1) * CH],
                                    ALU.add))
                                bank_free[jc] = c_v.mark()
                            else:
                                c_t.wait(se, mm_m)
                                c_s.inc(se.copy(rw[:, jc * CH:(jc + 1) * CH],
                                                ps_row[jc][:, :]))
                                bank_free[jc] = c_s.mark()
                            wfree(ve, bank_free[jc])
                            c_v.inc(ve.reduce_max(
                                cmax[qt % 3][:, jc:jc + 1],
                                rw[:, jc * CH:(jc + 1) * CH], axis=AX.X))
                        if len(pending) >= 2:
                            b_transposes(pending.pop(0))
                        if len(soft_pending) >= 2:
                            pending.append(b_softmax(soft_pending.pop(0)))
                        soft_pending.append(
                            {"qt": qt, "qs": qs, "qtl": qtl,
                             "bi": bi, "s_i": s_i, "h": h,
                             "rw": rw, "nch": nch, "ncol": ncol})
                while soft_pending:
                    if len(pending) >= 2:
                        b_transposes(pending.pop(0))
                    pending.append(b_softmax(soft_pending.pop(0)))
                while pending:
                    b_transposes(pending.pop(0))
                qkv_free[bi] = c_t.mark()
            c_st.wait(sy)
            c_x.wait(sy)
        nc.all_engine_barrier()
        b_done_st = c_st.n

        if upto == "B":
            _finish(nc, sy, c_st, c_x, dump, dbg, scr)
            return nc

        with ExitStack() as pc:
            ctx = pc.enter_context
            wo_sb = ctx(nc.sbuf_tensor("wo_sb", [128, 16 * D], BF16))
            for k in range(16):
                c_w.inc(ge.dma_start(wo_sb[:, k * D:(k + 1) * D],
                                     woT[k * 128:(k + 1) * 128, :]), 16)
            wo_ready = c_w.n
            atc = [ctx(nc.sbuf_tensor(f"atc{i}", [128, 16 * CH], BF16))
                   for i in range(2)]
            xin2 = [ctx(nc.sbuf_tensor(f"xin2_{i}", [128, D], F32))
                    for i in range(2)]
            lnat = ctx(nc.sbuf_tensor("lnat", [128, D], F32))
            prbuf = [ctx(nc.sbuf_tensor(f"prbuf{i}", [128, D], F32))
                     for i in range(2)]
            y_sb = [ctx(nc.sbuf_tensor(f"y_sb{i}", [128, D], F32)) for i in range(2)]
            lny = [ctx(nc.sbuf_tensor(f"lny{i}", [128, D], BF16)) for i in range(2)]
            lnyTst = [ctx(nc.sbuf_tensor(f"lnyTst{i}", [128, 16 * 128], BF16))
                      for i in range(2)]
            bn24c = [ctx(nc.sbuf_tensor(f"bn24c{i}", [128, 24], F32))
                     for i in range(2)]
            mvc = [ctx(nc.sbuf_tensor(f"mvc{i}", [128, 2], F32)) for i in range(2)]
            rstdc = [ctx(nc.sbuf_tensor(f"rstdc{i}", [128, 1], F32))
                     for i in range(2)]
            negmrc = [ctx(nc.sbuf_tensor(f"negmrc{i}", [128, 1], F32))
                      for i in range(2)]
            ps_pr = ctx(nc.psum_tensor("ps_pr", [128, D], F32))
            ps_t2 = [ctx(nc.psum_tensor(f"ps_t2{i}", [128, CH], F32))
                     for i in range(2)]

            atc_load = [None, None]
            atc_free = [None, None]
            pr_readers_s = None
            pr_readers_v = None
            prbuf_free = [None, None]
            x2_free = [None, None]
            y_free = [None, None]
            lny_free = [None, None]
            pst2_free = [None, None]
            lnyTst_free = [None, None]

            def c_tail(p):
                ui, lny_s, row0, tg = p["ui"], p["lny_s"], p["row0"], p["tg"]
                wfree(ve, lnyTst_free[ui])
                for grp in range(4):
                    pi = grp % 2
                    c_s.wait(te, lny_s)
                    wfree(te, pst2_free[pi])
                    for i4 in range(4):
                        k = grp * 4 + i4
                        ins = te.matmul(
                            ps_t2[pi][:, i4 * 128:(i4 + 1) * 128],
                            lny[ui][:, k * 128:(k + 1) * 128],
                            ident[:, :],
                            start=(i4 == 0), stop=(i4 == 3),
                            skip_group_check=True)
                    c_t.inc(ins)
                    lny_free[ui] = c_t.mark()
                    c_t.wait(ve)
                    c_v.inc(ve.tensor_copy(
                        lnyTst[ui][:, grp * 512:(grp + 1) * 512],
                        ps_t2[pi][:, :]))
                    pst2_free[pi] = c_v.mark()
                lnyT_v = c_v.n
                c_v.wait(se, lnyT_v)
                c_st.inc(se.dma_start(
                    dr3(lnyT_s[:, tg:tg + 128]),
                    sb3(lnyTst[ui][:, :], 16)), 16)
                lnyTst_free[ui] = c_st.mark()

            c_pending = None
            for g in range(NSEQ * NCHUNK):
                s_i, c_i = divmod(g, NCHUNK)
                gi = g % 2
                if g == 0:
                    c_st.wait(sy, b_done_st)
                    c_x.inc(sy.dma_start(
                        sb3(atc[0][:, :], 16),
                        dr3(attnT_s[0:D, 0:CH])), 16)
                    atc_load[0] = c_x.mark()
                if g + 1 < NSEQ * NCHUNK:
                    s_n, c_n = divmod(g + 1, NCHUNK)
                    wfree(sy, atc_free[(g + 1) % 2])
                    c_x.inc(sy.dma_start(
                        sb3(atc[(g + 1) % 2][:, :], 16),
                        dr3(attnT_s[s_n * D:(s_n + 1) * D,
                                    c_n * CH:(c_n + 1) * CH])), 16)
                    atc_load[(g + 1) % 2] = c_x.mark()
                for u in range(4):
                    ui = u % 2
                    row0 = s_i * S + c_i * CH + u * 128
                    wfree(sy, x2_free[ui])
                    c_x.inc(sy.dma_start(xin2[ui][:, :],
                                         x_in[row0:row0 + 128, :]), 16)
                    x2_here = c_x.mark()
                    wfree(te, atc_load[gi])
                    c_w.wait(te, wo_ready)
                    wfree(te, pr_readers_s)
                    if c_pending is not None:
                        c_tail(c_pending)
                        c_pending = None
                    for n in range(4):
                        for k in range(16):
                            ins = te.matmul(
                                ps_pr[:, n * CH:(n + 1) * CH],
                                atc[gi][:, k * CH + u * 128:
                                        k * CH + (u + 1) * 128],
                                wo_sb[:, k * D + n * CH:k * D + (n + 1) * CH],
                                start=(k == 0), stop=(k == 15))
                    c_t.inc(ins)
                    pr_mm = c_t.mark()
                    atc_free[gi] = c_t.mark()
                    wfree(se, pr_mm)
                    wfree(se, prbuf_free[ui])
                    c_s.inc(se.copy(prbuf[ui][:, :], ps_pr[:, :]))
                    pr_readers_s = c_s.mark()
                    pr_readers_v = None
                    nmr1 = ln_stats(lambda a, b, _p=prbuf[ui]: _p[:, a:b],
                                    bn24c[ui], mvc[ui], rstdc[ui],
                                    negmrc[ui], pr_readers_s)
                    wfree(se, nmr1)
                    c_s.inc(se.activation(lnat[:, :], prbuf[ui][:, :],
                                          AF.Identity,
                                          bias=negmrc[ui][:, :],
                                          scale=rstdc[ui][:, :]))
                    lnat_s = c_s.n
                    prbuf_free[ui] = c_s.mark()
                    c_s.wait(ve, lnat_s)
                    wfree(ve, x2_here)
                    wfree(ve, y_free[ui])
                    c_v.inc(ve.tensor_add(y_sb[ui][:, :], lnat[:, :],
                                          xin2[ui][:, :]))
                    y_v = c_v.mark()
                    x2_free[ui] = c_v.mark()
                    c_v.wait(se, y_v[1])
                    c_st.inc(se.dma_start(y_s[row0:row0 + 128, :],
                                          y_sb[ui][:, :]), 16)
                    y_free[ui] = c_st.mark()
                    nmr2 = ln_stats(lambda a, b, _y=y_sb[ui]: _y[:, a:b],
                                    bn24c[ui], mvc[ui], rstdc[ui],
                                    negmrc[ui], y_v)
                    wfree(se, nmr2)
                    wfree(se, lny_free[ui])
                    c_s.inc(se.activation(lny[ui][:, :], y_sb[ui][:, :],
                                          AF.Identity, bias=negmrc[ui][:, :],
                                          scale=rstdc[ui][:, :]))
                    lny_s = c_s.n
                    c_pending = {"ui": ui, "lny_s": lny_s, "row0": row0,
                                 "tg": s_i * S + c_i * CH + u * 128}
            if c_pending is not None:
                c_tail(c_pending)
                c_pending = None
            c_st.wait(sy)
            c_x.wait(sy)
            c_w.wait(ge)
        nc.all_engine_barrier()
        c_done_st = c_st.n

        if upto == "C":
            _finish(nc, sy, c_st, c_x, dump, dbg, scr)
            return nc

        with ExitStack() as pd1:
            ctx = pd1.enter_context
            lnyT_sb = ctx(nc.sbuf_tensor("lnyT_sb", [128, 16 * T], BF16))
            c_st.wait(sy, c_done_st)
            for kq in range(4):
                c_x.inc(sy.dma_start(
                    sb3(lnyT_sb[:, kq * 4 * T:(kq + 1) * 4 * T], 4),
                    dr3(lnyT_s[kq * 512:(kq + 1) * 512, :])), 16)
            lnyT_ld = c_x.n
            wg_sb = [ctx(nc.sbuf_tensor(f"wg_sb{i}", [128, D], BF16))
                     for i in range(2)]
            wu_sb = [ctx(nc.sbuf_tensor(f"wu_sb{i}", [128, D], BF16))
                     for i in range(2)]
            sg = [ctx(nc.sbuf_tensor(f"sg{i}", [128, CH], BF16)) for i in range(2)]
            hbf = [ctx(nc.sbuf_tensor(f"hbf{i}", [128, CH], BF16)) for i in range(2)]
            ps_g = [ctx(nc.psum_tensor(f"ps_g{i}", [128, CH], F32)) for i in range(2)]
            ps_u = [ctx(nc.psum_tensor(f"ps_u{i}", [128, CH], F32)) for i in range(2)]

            w_loaded = [None, None]
            w_free = [None, None]
            psg_free = [None, None]
            psu_free = [None, None]
            hbf_free = [None, None]
            sg_free = [None, None]

            for m in range(64):
                wi = m % 2
                wfree(ge, w_free[wi])
                c_w.inc(ge.dma_start(
                    sb3(wg_sb[wi][:, :], 16),
                    dr3(wgT[:, m * 128:(m + 1) * 128])), 16)
                c_w.inc(ge.dma_start(
                    sb3(wu_sb[wi][:, :], 16),
                    dr3(wuT[:, m * 128:(m + 1) * 128])), 16)
                w_loaded[wi] = c_w.mark()
                for g in range(NSEQ * NCHUNK):
                    pi = g % 2
                    wfree(te, w_loaded[wi])
                    c_x.wait(te, lnyT_ld)
                    wfree(te, psg_free[pi])
                    for k in range(16):
                        ins = te.matmul(ps_g[pi][:, :],
                                        wg_sb[wi][:, k * 128:(k + 1) * 128],
                                        lnyT_sb[:, k * T + g * CH:
                                                k * T + (g + 1) * CH],
                                        start=(k == 0), stop=(k == 15))
                    c_t.inc(ins)
                    g_mm = c_t.n
                    wfree(te, psu_free[pi])
                    for k in range(16):
                        ins = te.matmul(ps_u[pi][:, :],
                                        wu_sb[wi][:, k * 128:(k + 1) * 128],
                                        lnyT_sb[:, k * T + g * CH:
                                                k * T + (g + 1) * CH],
                                        start=(k == 0), stop=(k == 15))
                    c_t.inc(ins)
                    u_mm = c_t.n
                    c_t.wait(se, g_mm)
                    wfree(se, sg_free[pi])
                    c_s.inc(se.activation(sg[pi][:, :], ps_g[pi][:, :], AF.Silu))
                    psg_free[pi] = c_s.mark()
                    sg_s = c_s.n
                    c_t.wait(ve, u_mm)
                    c_s.wait(ve, sg_s)
                    wfree(ve, hbf_free[pi])
                    c_v.inc(ve.tensor_mul(hbf[pi][:, :], sg[pi][:, :],
                                          ps_u[pi][:, :]))
                    psu_free[pi] = c_v.mark()
                    sg_free[pi] = c_v.mark()
                    h_v = c_v.n
                    c_v.wait(sy, h_v)
                    c_st.inc(sy.dma_start(
                        hT_s[m * 128:(m + 1) * 128, g * CH:(g + 1) * CH],
                        hbf[pi][:, :]), 16)
                    hbf_free[pi] = c_st.mark()
                w_free[wi] = c_t.mark()
            c_st.wait(sy)
            c_x.wait(sy)
            c_w.wait(ge)
        nc.all_engine_barrier()
        d1_done_st = c_st.n

        if upto == "D1":
            _finish(nc, sy, c_st, c_x, dump, dbg, scr)
            return nc

        with ExitStack() as pd2:
            ctx = pd2.enter_context
            wd_sb = ctx(nc.sbuf_tensor("wd_sb", [128, 32 * D], BF16))
            hsl = [ctx(nc.sbuf_tensor(f"hsl{i}", [128, 32 * 128], BF16))
                   for i in range(2)]
            dpart = [ctx(nc.sbuf_tensor(f"dpart{i}", [128, D], F32))
                     for i in range(2)]
            dfull = ctx(nc.sbuf_tensor("dfull", [128, D], F32))
            lnd = ctx(nc.sbuf_tensor("lnd", [128, D], F32))
            osb = ctx(nc.sbuf_tensor("osb", [128, D], F32))
            y2 = [ctx(nc.sbuf_tensor(f"y2_{i}", [128, D], F32))
                  for i in range(2)]
            bn24d = ctx(nc.sbuf_tensor("bn24d", [128, 24], F32))
            mvd = ctx(nc.sbuf_tensor("mvd", [128, 2], F32))
            rstdd = ctx(nc.sbuf_tensor("rstdd", [128, 1], F32))
            negmrd = ctx(nc.sbuf_tensor("negmrd", [128, 1], F32))
            ps_d = [ctx(nc.psum_tensor(f"ps_d{i}", [128, D], F32))
                    for i in range(2)]

            hsl_load = [None, None]
            hsl_free = [None, None]
            psd_free = [None, None]
            dpart_free = [None, None]
            dpart_ld = [None, None]
            y2_free = [None, None]
            y2_ld = [None, None]
            osb_free = None
            dpart_all_stored = None

            for hp in range(2):
                c_st.wait(sy, d1_done_st)
                if hp == 1:
                    c_t.wait(ge)
                    wfree(sy, dpart_all_stored)
                for k in range(32):
                    c_w.inc(ge.dma_start(
                        wd_sb[:, k * D:(k + 1) * D],
                        wdT[hp * 4096 + k * 128:
                            hp * 4096 + (k + 1) * 128, :]), 16)
                wd_ready = c_w.n

                def d2_loads(ts, _hp=hp):
                    bi = ts % 2
                    row0 = ts * 128
                    wfree(sy, hsl_free[bi])
                    c_x.inc(sy.dma_start(
                        sb3(hsl[bi][:, :], 32),
                        dr3(hT_s[_hp * 4096:(_hp + 1) * 4096,
                                 row0:row0 + 128])), 16)
                    hsl_load[bi] = c_x.mark()
                    if _hp == 1:
                        wfree(sy, dpart_free[bi])
                        c_x.inc(sy.dma_start(dpart[bi][:, :],
                                             dpart_s[row0:row0 + 128, :]), 16)
                        dpart_ld[bi] = c_x.mark()
                        wfree(sy, y2_free[bi])
                        c_x.inc(sy.dma_start(y2[bi][:, :],
                                             y_s[row0:row0 + 128, :]), 16)
                        y2_ld[bi] = c_x.mark()

                d2_loads(0)
                for ts_i in range(32):
                    bi = ts_i % 2
                    row0 = ts_i * 128
                    if ts_i + 1 < 32:
                        d2_loads(ts_i + 1)
                    pd_i = ts_i % 2
                    c_w.wait(te, wd_ready)
                    wfree(te, hsl_load[bi])
                    wfree(te, psd_free[pd_i])
                    for n in range(4):
                        for k in range(32):
                            ins = te.matmul(
                                ps_d[pd_i][:, n * CH:(n + 1) * CH],
                                hsl[bi][:, k * 128:(k + 1) * 128],
                                wd_sb[:, k * D + n * CH:k * D + (n + 1) * CH],
                                start=(k == 0), stop=(k == 31))
                    c_t.inc(ins)
                    d_mm = c_t.mark()
                    hsl_free[bi] = c_t.mark()
                    if hp == 0:
                        c_t.wait(se, d_mm[1])
                        wfree(se, dpart_free[bi])
                        c_s.inc(se.copy(dpart[bi][:, :], ps_d[pd_i][:, :]))
                        psd_free[pd_i] = c_s.mark()
                        c_s.wait(se)
                        c_st.inc(se.dma_start(dpart_s[row0:row0 + 128, :],
                                              dpart[bi][:, :]), 16)
                        dpart_free[bi] = c_st.mark()
                    else:
                        c_t.wait(ve, d_mm[1])
                        wfree(ve, dpart_ld[bi])
                        c_v.inc(ve.tensor_add(dfull[:, :], ps_d[pd_i][:, :],
                                              dpart[bi][:, :]))
                        psd_free[pd_i] = c_v.mark()
                        dpart_free[bi] = c_v.mark()
                        df_v = c_v.mark()
                        ln_stats(lambda a, b: dfull[:, a:b], bn24d, mvd,
                                 rstdd, negmrd, df_v)
                        c_v.wait(se)
                        c_s.inc(se.activation(lnd[:, :], dfull[:, :], AF.Identity,
                                              bias=negmrd[:, :],
                                              scale=rstdd[:, :]))
                        lnd_s = c_s.n
                        c_s.wait(ve, lnd_s)
                        wfree(ve, y2_ld[bi])
                        wfree(ve, osb_free)
                        c_v.inc(ve.tensor_add(osb[:, :], lnd[:, :], y2[bi][:, :]))
                        o_v = c_v.n
                        y2_free[bi] = c_v.mark()
                        c_v.wait(sy, o_v)
                        c_st.inc(sy.dma_start(out_d[row0:row0 + 128, :],
                                              osb[:, :]), 16)
                        osb_free = c_st.mark()
                if hp == 0:
                    dpart_all_stored = c_st.mark()
                    dpart_free = [None, None]
                    hsl_load = [None, None]
            c_st.wait(sy)
            c_x.wait(sy)
            c_w.wait(ge)
        nc.all_engine_barrier()
        _finish(nc, sy, c_st, c_x, dump, dbg, scr)
    return nc


def _finish(nc, sy, c_st, c_x, dump, dbg, scr):
    for name, d in dbg.items():
        c_st.inc(sy.dma_start(d, scr[name]), 16)
    c_st.wait(sy)


def kernel(L0_attn_x, L0_attn_wq, L0_attn_w_kv_down, L0_attn_w_k_up,
           L0_attn_w_k_rope, L0_attn_w_v_up, L0_attn_wo,
           L0_ffn_w_gate, L0_ffn_w_up, L0_ffn_w_down):
    x = np.ascontiguousarray(np.asarray(L0_attn_x, dtype=np.float32))
    assert x.shape == (16, S, D)

    def tr(w):
        return np.ascontiguousarray(np.asarray(w, dtype=np.float32).T)

    weights = {
        "wqT": tr(L0_attn_wq),
        "wkvT": tr(L0_attn_w_kv_down),
        "wkrT": tr(L0_attn_w_k_rope),
        "wkuT": tr(L0_attn_w_k_up),
        "wvuT": tr(L0_attn_w_v_up),
        "woT": tr(L0_attn_wo),
        "wgT": tr(L0_ffn_w_gate),
        "wuT": tr(L0_ffn_w_up),
        "wdT": tr(L0_ffn_w_down),
    }
    nc = build_program()
    in_maps = []
    for core in range(8):
        xc = np.ascontiguousarray(
            x[NSEQ * core:NSEQ * (core + 1)].reshape(T, D))
        m = {"x": xc}
        m.update(weights)
        in_maps.append(m)
    res = run_bass_kernel_spmd(nc, in_maps, core_ids=list(range(8)))
    out = np.concatenate(
        [np.asarray(res.results[i]["out"]).reshape(NSEQ, S, D)
         for i in range(8)], axis=0)
    return out


# version 16
# speedup vs baseline: 1.2119x; 1.0125x over previous
import numpy as np
from contextlib import ExitStack

import concourse.bass as bass
import concourse.mybir as mybir
from concourse.bass_utils import run_bass_kernel_spmd

F32 = mybir.dt.float32
BF16 = mybir.dt.bfloat16
AF = mybir.ActivationFunctionType
ALU = mybir.AluOpType
AX = mybir.AxisListType

S = 2048
D = 2048
H = 16
HD = 128
ND = 64
R = 512
HID = 8192
NSEQ = 2
T = NSEQ * S
THETA = 10000.0
EPS = 1e-5
SCALE = float(1.0 / np.float32(np.sqrt(np.float64(HD))))
NEG_BIG = -3.0e38
NCHUNK = 4
CH = 512


class Ctr:
    def __init__(self, stack, nc, name):
        self.h = stack.enter_context(nc.semaphore(name))
        self.n = 0

    def inc(self, ins, k=1):
        ins.then_inc(self.h, k)
        self.n += k
        return self.n

    def wait(self, eng, val=None):
        v = self.n if val is None else val
        assert v <= self.n, "wait on a not-yet-emitted increment"
        if v > 0:
            eng.wait_ge(self.h, v)

    def mark(self):
        return (self, self.n)


def wfree(eng, slot):
    if slot is not None:
        ctr, val = slot
        ctr.wait(eng, val)


def dr3(ap, p=128):
    return ap.rearrange("(k p) c -> p k c", p=p)


def sb3(ap, k):
    return ap.rearrange("p (k c) -> p k c", k=k)


def _rope_tables():
    pos = np.arange(S, dtype=np.float64)
    rd = np.arange(ND)
    freqs = THETA ** (-(2.0 * (rd // 2)) / ND)
    ang = pos[None, :] * freqs[:, None]
    cos64 = np.cos(ang)
    sin64 = np.sin(ang)
    return (np.concatenate([cos64, cos64 * SCALE], 0).astype(np.float32),
            np.concatenate([sin64, sin64 * SCALE], 0).astype(np.float32))


def _rot_mats():
    import ml_dtypes
    mq = np.zeros((HD, HD), dtype=np.float32)
    for i in range(ND // 2):
        a = ND + 2 * i
        mq[a, a + 1] = -1.0
        mq[a + 1, a] = 1.0
    mk = np.zeros((ND, ND), dtype=np.float32)
    for i in range(ND // 2):
        mk[2 * i, 2 * i + 1] = -1.0
        mk[2 * i + 1, 2 * i] = 1.0
    return (np.ascontiguousarray(mq.T).astype(ml_dtypes.bfloat16),
            np.ascontiguousarray(mk.T).astype(ml_dtypes.bfloat16))


def build_program(upto="D2", dump=None):
    nc = bass.Bass("TRN2", target_bir_lowering=False)

    x_in = nc.dram_tensor("x", [T, D], F32, kind="ExternalInput").ap()
    wqT = nc.dram_tensor("wqT", [D, D], F32, kind="ExternalInput").ap()
    wkvT = nc.dram_tensor("wkvT", [D, R], F32, kind="ExternalInput").ap()
    wkrT = nc.dram_tensor("wkrT", [D, ND], F32, kind="ExternalInput").ap()
    wkuT = nc.dram_tensor("wkuT", [R, H * ND], F32, kind="ExternalInput").ap()
    wvuT = nc.dram_tensor("wvuT", [R, D], F32, kind="ExternalInput").ap()
    woT = nc.dram_tensor("woT", [D, D], F32, kind="ExternalInput").ap()
    wgT = nc.dram_tensor("wgT", [D, HID], F32, kind="ExternalInput").ap()
    wuT = nc.dram_tensor("wuT", [D, HID], F32, kind="ExternalInput").ap()
    wdT = nc.dram_tensor("wdT", [HID, D], F32, kind="ExternalInput").ap()
    out_d = nc.dram_tensor("out", [T, D], F32, kind="ExternalOutput").ap()

    scr = {}
    scr["qT_s"] = qT_s = nc.dram_tensor("qT_s", [NSEQ * D, S], BF16).ap()
    scr["knope_s"] = knope_s = nc.dram_tensor("knope_s", [NSEQ * H * ND, S], BF16).ap()
    scr["krope_s"] = krope_s = nc.dram_tensor("krope_s", [NSEQ * ND, S], BF16).ap()
    scr["v_s"] = v_s = nc.dram_tensor("v_s", [T, D], BF16).ap()
    scr["attnT_s"] = attnT_s = nc.dram_tensor("attnT_s", [NSEQ * D, S], BF16).ap()
    scr["y_s"] = y_s = nc.dram_tensor("y_s", [T, D], F32).ap()
    scr["lnyT_s"] = lnyT_s = nc.dram_tensor("lnyT_s", [D, T], BF16).ap()
    scr["hT_s"] = hT_s = nc.dram_tensor("hT_s", [HID, T], BF16).ap()
    scr["dpart_s"] = dpart_s = nc.dram_tensor("dpart_s", [T, D], F32).ap()

    dbg = {}
    if dump:
        for name in ([dump] if isinstance(dump, str) else dump):
            dsrc = scr[name]
            dbg[name] = nc.dram_tensor("dbg_" + name, list(dsrc.shape),
                                       dsrc.dtype, kind="ExternalOutput").ap()

    cos_np, sin_np = _rope_tables()
    mtq_np, mtk_np = _rot_mats()
    cos_d = nc.inline_tensor(cos_np, "cos_t").ap()
    sin_d = nc.inline_tensor(sin_np, "sin_t").ap()
    mtq_d = nc.inline_tensor(np.asarray(mtq_np), "mtq").ap()
    mtk_d = nc.inline_tensor(np.asarray(mtk_np), "mtk").ap()

    te, ve, se, ge, sy = nc.tensor, nc.vector, nc.scalar, nc.gpsimd, nc.sync

    with ExitStack() as topst, nc.Block():
        ident = nc.alloc_sbuf_tensor("ident", [128, 128], BF16)
        eps_sb = nc.alloc_sbuf_tensor("eps_sb", [128, 1], F32)

        c_w = Ctr(topst, nc, "c_w")
        c_x = Ctr(topst, nc, "c_x")
        c_st = Ctr(topst, nc, "c_st")
        c_t = Ctr(topst, nc, "c_t")
        c_v = Ctr(topst, nc, "c_v")
        c_s = Ctr(topst, nc, "c_s")
        c_g = Ctr(topst, nc, "c_g")

        with ExitStack() as ini:
            iop = ini.enter_context(nc.sbuf_tensor("iop", [128, 128], F32))
            io2 = ini.enter_context(nc.sbuf_tensor("io2", [128, 128], F32))
            c_g.inc(ge.iota(iop[:, :], [[0, 128]], channel_multiplier=1,
                            allow_small_or_imprecise_dtypes=True))
            c_g.inc(ge.iota(io2[:, :], [[1, 128]], channel_multiplier=0,
                            allow_small_or_imprecise_dtypes=True))
            c_g.wait(ve)
            c_v.inc(ve.tensor_tensor(ident[:, :], io2[:, :], iop[:, :],
                                     ALU.is_equal))
            c_v.inc(ve.memset(eps_sb[:, :], EPS))
            c_v.wait(ve)
            ident_ready = c_v.n
            c_v.wait(ge)

        def ln_stats(src_ap_fn, bn24, mv, rstd, negmr, ready_slot):
            wfree(ve, ready_slot)
            for i in range(4):
                ins = ve.bn_stats(bn24[:, 6 * i:6 * i + 6],
                                  src_ap_fn(512 * i, 512 * (i + 1)))
            c_v.inc(ins)
            c_v.wait(ve)
            c_v.inc(ve.bn_aggr(mv[:, :], bn24[:, :]))
            aggr_m = c_v.mark()
            wfree(se, aggr_m)
            c_s.wait(se)
            c_s.inc(se.activation(rstd[:, :], mv[:, 1:2], AF.Ln,
                                  bias=eps_sb[:, :]))
            c_s.wait(se)
            c_s.inc(se.activation(rstd[:, :], rstd[:, :], AF.Exp, scale=-0.5))
            rstd_m = c_s.mark()
            wfree(ve, rstd_m)
            c_v.inc(ve.tensor_scalar(negmr[:, :], mv[:, 0:1], rstd[:, :], -1.0,
                                     ALU.mult, ALU.mult))
            return c_v.mark()

        with ExitStack() as pa:
            ctx = pa.enter_context
            cos_sb = ctx(nc.sbuf_tensor("cos_sb", [HD, S], F32))
            sin_sb = ctx(nc.sbuf_tensor("sin_sb", [HD, S], F32))
            mtq_sb = ctx(nc.sbuf_tensor("mtq_sb", [HD, HD], BF16))
            mtk_sb = ctx(nc.sbuf_tensor("mtk_sb", [ND, ND], BF16))
            c_v.wait(sy, ident_ready)
            c_x.inc(sy.dma_start(cos_sb[:, :], cos_d), 16)
            c_x.inc(sy.dma_start(sin_sb[:, :], sin_d), 16)
            c_x.inc(sy.dma_start(mtq_sb[:, :], mtq_d), 16)
            c_x.inc(sy.dma_start(mtk_sb[:, :], mtk_d), 16)
            consts_ld = c_x.n

            wq_sb = ctx(nc.sbuf_tensor("wq_sb", [128, 16 * D], BF16))
            wkv_sb = ctx(nc.sbuf_tensor("wkv_sb", [128, 4 * D], BF16))
            wkr_sb = ctx(nc.sbuf_tensor("wkr_sb", [128, 16 * ND], BF16))
            wku_sb = ctx(nc.sbuf_tensor("wku_sb", [128, 8 * R], BF16))
            wvu_sb = ctx(nc.sbuf_tensor("wvu_sb", [128, 4 * D], BF16))
            for m in range(16):
                c_w.inc(ge.dma_start(
                    sb3(wq_sb[:, m * D:(m + 1) * D], 16),
                    dr3(wqT[:, m * 128:(m + 1) * 128])), 16)
            for m in range(4):
                c_w.inc(ge.dma_start(
                    sb3(wkv_sb[:, m * D:(m + 1) * D], 16),
                    dr3(wkvT[:, m * 128:(m + 1) * 128])), 16)
            c_w.inc(ge.dma_start(sb3(wkr_sb[:, :], 16), dr3(wkrT)), 16)
            for m in range(8):
                c_w.inc(ge.dma_start(
                    sb3(wku_sb[:, m * R:(m + 1) * R], 4),
                    dr3(wkuT[:, m * 128:(m + 1) * 128])), 16)
            for m in range(4):
                c_w.inc(ge.dma_start(
                    wvu_sb[:, m * D:(m + 1) * D],
                    wvuT[m * 128:(m + 1) * 128, :]), 16)
            w_ready = c_w.n

            xin = [ctx(nc.sbuf_tensor(f"xin{i}", [128, D], F32)) for i in range(2)]
            lnx = [ctx(nc.sbuf_tensor(f"lnx{i}", [128, D], BF16)) for i in range(2)]
            lnxT = [ctx(nc.sbuf_tensor(f"lnxT{i}", [128, 16 * CH], BF16))
                    for i in range(2)]
            ckvT = ctx(nc.sbuf_tensor("ckvT", [128, 4 * CH], BF16))
            bn24 = [ctx(nc.sbuf_tensor(f"bn24_{i}", [128, 24], F32)) for i in range(2)]
            mv = [ctx(nc.sbuf_tensor(f"mv{i}", [128, 2], F32)) for i in range(2)]
            rstd = [ctx(nc.sbuf_tensor(f"rstd{i}", [128, 1], F32)) for i in range(2)]
            negmr = [ctx(nc.sbuf_tensor(f"negmr{i}", [128, 1], F32)) for i in range(2)]
            qbf = [ctx(nc.sbuf_tensor(f"qbf{i}", [128, CH], BF16)) for i in range(2)]
            tcos = [ctx(nc.sbuf_tensor(f"tcos{i}", [128, CH], F32)) for i in range(2)]
            trot = [ctx(nc.sbuf_tensor(f"trot{i}", [128, CH], F32)) for i in range(2)]
            qfin = [ctx(nc.sbuf_tensor(f"qfin{i}", [128, CH], BF16)) for i in range(2)]
            evac = [ctx(nc.sbuf_tensor(f"evac{i}", [128, CH], BF16)) for i in range(2)]

            ps_tr = [ctx(nc.psum_tensor(f"ps_tr{i}", [128, CH], F32)) for i in range(2)]
            ps_pj = [ctx(nc.psum_tensor(f"ps_pj{i}", [128, CH], F32)) for i in range(2)]
            ps_rt = [ctx(nc.psum_tensor(f"ps_rt{i}", [128, CH], F32)) for i in range(2)]

            xin_free = [None, None]
            lnx_free = [None, None]
            lnxT_free = [None, None]
            pstr_free = [None, None]
            pspj_free = [None, None]
            psrt_free = [None, None]
            qbf_free = [None, None]
            evac_free = [None, None]
            qfin_free = [None, None]

            for g in range(NSEQ * NCHUNK):
                s_i, c_i = divmod(g, NCHUNK)
                gl = lnxT[g % 2]
                for u in range(4):
                    ui = u % 2
                    row0 = s_i * S + c_i * CH + u * 128
                    wfree(sy, xin_free[ui])
                    c_x.inc(sy.dma_start(xin[ui][:, :],
                                         x_in[row0:row0 + 128, :]), 16)
                    x_here = c_x.mark()
                    nmr_m = ln_stats(lambda a, b, _x=xin[ui]: _x[:, a:b],
                                     bn24[ui], mv[ui], rstd[ui], negmr[ui],
                                     x_here)
                    wfree(se, nmr_m)
                    wfree(se, lnx_free[ui])
                    c_s.inc(se.activation(lnx[ui][:, :], xin[ui][:, :], AF.Identity,
                                          bias=negmr[ui][:, :],
                                          scale=rstd[ui][:, :]))
                    xin_free[ui] = c_s.mark()
                    lnx_here = c_s.n
                    for grp in range(4):
                        pi = grp % 2
                        c_s.wait(te, lnx_here)
                        c_v.wait(te, ident_ready)
                        wfree(te, pstr_free[pi])
                        for i4 in range(4):
                            k = grp * 4 + i4
                            ins = te.matmul(ps_tr[pi][:, i4 * 128:(i4 + 1) * 128],
                                            lnx[ui][:, k * 128:(k + 1) * 128],
                                            ident[:, :],
                                            start=(i4 == 0), stop=(i4 == 3),
                                            skip_group_check=True)
                        c_t.inc(ins)
                        lnx_free[ui] = c_t.mark()
                        wfree(ve, lnx_free[ui])
                        wfree(ve, lnxT_free[g % 2])
                        dst = gl[:, :].rearrange("p (k c) -> p k c", k=16)[
                            :, grp * 4:grp * 4 + 4, u * 128:(u + 1) * 128]
                        src = ps_tr[pi][:, :].rearrange("p (k c) -> p k c", k=4)
                        c_v.inc(ve.tensor_copy(dst, src))
                        pstr_free[pi] = c_v.mark()
                lnxT_ready_v = c_v.n

                ccols = slice(c_i * CH, (c_i + 1) * CH)
                c_w.wait(te, w_ready)
                c_x.wait(te, consts_ld)
                c_v.wait(te, lnxT_ready_v)

                def q_tail(pi, m, qbf_ready, _ccols=ccols, _s=s_i):
                    wfree(te, qbf_ready)
                    wfree(te, psrt_free[pi])
                    c_t.inc(te.matmul(ps_rt[pi][:, :], mtq_sb[:, :],
                                      qbf[pi][:, :], start=True, stop=True))
                    qbf_rot = c_t.mark()
                    wfree(ve, qbf_rot)
                    c_v.inc(ve.tensor_mul(trot[pi][64:128, :],
                                          ps_rt[pi][64:128, :],
                                          sin_sb[64:128, _ccols]))
                    psrt_free[pi] = c_v.mark()
                    c_v.wait(ve)
                    c_v.inc(ve.tensor_add(qfin[pi][64:128, :],
                                          tcos[pi][64:128, :],
                                          trot[pi][64:128, :]))
                    c_v.inc(ve.tensor_scalar_mul(qfin[pi][0:64, :],
                                                 qbf[pi][0:64, :], SCALE))
                    qbf_free[pi] = c_v.mark()
                    qf_v = c_v.n
                    c_v.wait(se, qf_v)
                    c_st.inc(se.dma_start(
                        qT_s[_s * D + m * 128:_s * D + (m + 1) * 128, _ccols],
                        qfin[pi][:, :]), 16)
                    qfin_free[pi] = c_st.mark()

                q_pend = None
                for m in range(16):
                    pi = m % 2
                    wfree(te, pspj_free[pi])
                    for k in range(16):
                        ins = te.matmul(ps_pj[pi][:, :],
                                        wq_sb[:, m * D + k * 128:
                                              m * D + (k + 1) * 128],
                                        gl[:, k * CH:(k + 1) * CH],
                                        start=(k == 0), stop=(k == 15))
                    c_t.inc(ins)
                    q_mm = c_t.n
                    if q_pend is not None:
                        q_tail(*q_pend)
                    c_t.wait(ve, q_mm)
                    wfree(ve, qbf_free[pi])
                    wfree(ve, qfin_free[pi])
                    c_v.inc(ve.tensor_copy(qbf[pi][:, :], ps_pj[pi][:, :]))
                    c_v.inc(ve.tensor_mul(tcos[pi][64:128, :],
                                          ps_pj[pi][64:128, :],
                                          cos_sb[64:128, ccols]))
                    pspj_free[pi] = c_v.mark()
                    q_pend = (pi, m, c_v.mark())
                q_tail(*q_pend)

                for m in range(4):
                    pi = m % 2
                    wfree(te, pspj_free[pi])
                    for k in range(16):
                        ins = te.matmul(ps_pj[pi][:, :],
                                        wkv_sb[:, m * D + k * 128:
                                               m * D + (k + 1) * 128],
                                        gl[:, k * CH:(k + 1) * CH],
                                        start=(k == 0), stop=(k == 15))
                    c_t.inc(ins)
                    c_t.wait(se, c_t.n)
                    c_s.inc(se.copy(ckvT[:, m * CH:(m + 1) * CH],
                                    ps_pj[pi][:, :]))
                    pspj_free[pi] = c_s.mark()
                ckv_ready_s = c_s.n

                wfree(te, pspj_free[0])
                for k in range(16):
                    ins = te.matmul(ps_pj[0][0:64, :],
                                    wkr_sb[:, k * ND:(k + 1) * ND],
                                    gl[:, k * CH:(k + 1) * CH],
                                    start=(k == 0), stop=(k == 15))
                c_t.inc(ins)
                kr_mm = c_t.n
                c_t.wait(ve, kr_mm)
                wfree(ve, qbf_free[0])
                c_v.inc(ve.tensor_copy(qbf[0][0:64, :], ps_pj[0][0:64, :]))
                c_v.inc(ve.tensor_mul(tcos[0][0:64, :], ps_pj[0][0:64, :],
                                      cos_sb[0:64, ccols]))
                pspj_free[0] = c_v.mark()
                wfree(te, pspj_free[0])
                wfree(te, psrt_free[0])
                c_t.inc(te.matmul(ps_rt[0][0:64, :], mtk_sb[:, :],
                                  qbf[0][0:64, :], start=True, stop=True))
                qbf_free[0] = c_t.mark()
                wfree(ve, qbf_free[0])
                c_v.inc(ve.tensor_mul(trot[0][0:64, :], ps_rt[0][0:64, :],
                                      sin_sb[0:64, ccols]))
                psrt_free[0] = c_v.mark()
                c_v.wait(ve)
                wfree(ve, qfin_free[0])
                c_v.inc(ve.tensor_add(qfin[0][0:64, :], tcos[0][0:64, :],
                                      trot[0][0:64, :]))
                c_v.wait(se)
                c_st.inc(se.dma_start(
                    krope_s[s_i * ND:(s_i + 1) * ND, ccols],
                    qfin[0][0:64, :]), 16)
                qfin_free[0] = c_st.mark()

                for m in range(8):
                    pi = m % 2
                    wfree(te, pspj_free[pi])
                    c_s.wait(te, ckv_ready_s)
                    for k in range(4):
                        ins = te.matmul(ps_pj[pi][:, :],
                                        wku_sb[:, m * R + k * 128:
                                               m * R + (k + 1) * 128],
                                        ckvT[:, k * CH:(k + 1) * CH],
                                        start=(k == 0), stop=(k == 3))
                    c_t.inc(ins)
                    c_t.wait(se, c_t.n)
                    wfree(se, evac_free[pi])
                    c_s.inc(se.copy(evac[pi][:, :], ps_pj[pi][:, :]))
                    pspj_free[pi] = c_s.mark()
                    c_s.wait(se)
                    c_st.inc(se.dma_start(
                        knope_s[s_i * H * ND + m * 128:
                                s_i * H * ND + (m + 1) * 128, ccols],
                        evac[pi][:, :]), 16)
                    evac_free[pi] = c_st.mark()

                for u in range(4):
                    for n in range(4):
                        pi = (u * 4 + n) % 2
                        wfree(te, pspj_free[pi])
                        for k in range(4):
                            ins = te.matmul(
                                ps_pj[pi][:, :],
                                ckvT[:, k * CH + u * 128:k * CH + (u + 1) * 128],
                                wvu_sb[:, k * D + n * CH:k * D + (n + 1) * CH],
                                start=(k == 0), stop=(k == 3))
                        c_t.inc(ins)
                        c_t.wait(se, c_t.n)
                        wfree(se, evac_free[pi])
                        c_s.inc(se.copy(evac[pi][:, :], ps_pj[pi][:, :]))
                        pspj_free[pi] = c_s.mark()
                        c_s.wait(se)
                        c_st.inc(se.dma_start(
                            v_s[s_i * S + c_i * CH + u * 128:
                                s_i * S + c_i * CH + (u + 1) * 128,
                                n * CH:(n + 1) * CH],
                            evac[pi][:, :]), 16)
                        evac_free[pi] = c_st.mark()
                lnxT_free[g % 2] = c_t.mark()

            c_st.wait(sy)
            c_x.wait(sy)
            c_w.wait(ge)
        nc.all_engine_barrier()
        a_done_st = c_st.n

        if upto == "A":
            _finish(nc, sy, c_st, c_x, dump, dbg, scr)
            return nc

        with ExitStack() as pb:
            ctx = pb.enter_context
            masks = ctx(nc.sbuf_tensor("masks", [128, 4 * CH], F32))
            iof = ctx(nc.sbuf_tensor("iof", [128, CH], F32))
            c_g.inc(ge.iota(iof[:, :], [[1, CH]], channel_multiplier=-1,
                            allow_small_or_imprecise_dtypes=True))
            c_g.wait(ve)
            for r in range(4):
                c_v.inc(ve.tensor_single_scalar(masks[:, r * CH:(r + 1) * CH],
                                                iof[:, :], float(r * 128),
                                                ALU.is_gt))
                c_v.wait(ve)
                c_v.inc(ve.tensor_scalar_mul(masks[:, r * CH:(r + 1) * CH],
                                             masks[:, r * CH:(r + 1) * CH],
                                             NEG_BIG))
                c_v.wait(ve)
            masks_v = c_v.n

            q_sb = [ctx(nc.sbuf_tensor(f"q_sb{i}", [128, S], BF16)) for i in range(2)]
            kc_sb = [ctx(nc.sbuf_tensor(f"kc_sb{i}", [128, S], BF16)) for i in range(2)]
            v_hb = [ctx(nc.sbuf_tensor(f"v_hb{i}", [128, H * 128], BF16))
                    for i in range(2)]
            pbf = [ctx(nc.sbuf_tensor(f"pbf{i}", [128, S], BF16)) for i in range(4)]
            diags = [ctx(nc.sbuf_tensor(f"diags{i}", [128, 4 * 128], BF16))
                     for i in range(4)]
            negc = [ctx(nc.sbuf_tensor(f"negc{i}", [128, 4], F32))
                    for i in range(4)]
            denc = [ctx(nc.sbuf_tensor(f"denc{i}", [128, 4], F32))
                    for i in range(4)]
            fv = ctx(nc.sbuf_tensor("fv", [128, 4], F32))
            gv = ctx(nc.sbuf_tensor("gv", [128, 4], F32))
            pT_sb = ctx(nc.sbuf_tensor("pT_sb", [128, 16 * CH], BF16))
            attbf = [ctx(nc.sbuf_tensor(f"attbf{i}", [128, CH], BF16))
                     for i in range(2)]
            negm = ctx(nc.sbuf_tensor("negm", [128, 1], F32))
            den = ctx(nc.sbuf_tensor("den", [128, 1], F32))
            dinv = ctx(nc.sbuf_tensor("dinv", [128, 1], F32))

            ps_row = [ctx(nc.psum_tensor(f"ps_row{i}", [128, CH], F32))
                      for i in range(4)]
            ps_pt = [ctx(nc.psum_tensor(f"ps_pt{i}", [128, CH], F32))
                     for i in range(2)]
            ps_av = [ctx(nc.psum_tensor(f"ps_av{i}", [128, CH], F32))
                     for i in range(2)]

            qkv_free = [None, None]
            exp_mark = [None, None]
            att_free = [None, None]
            pspt_free = [None, None]
            psav_free = [None, None]
            load_done = [None, None]
            bank_free = [None] * 4
            soft_free = [None] * 4
            expc_last = [None] * 4
            pbf_free = [None] * 4

            def b_transposes(p):
                qt, qs, qtl, bi, s_i, h = (p[k] for k in
                                           ("qt", "qs", "qtl", "bi", "s_i", "h"))
                pb = pbf[qt % 4]
                dgs = diags[qt % 4]
                wfree(te, p["pn_v"])
                ngrp = qt // 4 + 1
                for jg in range(ngrp):
                    gsz = 4 if jg < qt // 4 else (qt % 4 + 1)
                    pi = jg % 2
                    wfree(te, pspt_free[pi])
                    for i4 in range(gsz):
                        jt = jg * 4 + i4
                        ins = te.matmul(
                            ps_pt[pi][:, i4 * 128:(i4 + 1) * 128],
                            pb[:, jt * 128:(jt + 1) * 128],
                            dgs[:, (jt // 4) * 128:(jt // 4 + 1) * 128],
                            start=(i4 == 0), stop=(i4 == gsz - 1),
                            skip_group_check=True)
                    c_t.inc(ins)
                    grp_m = c_t.mark()
                    dst = pT_sb[:, :].rearrange("p (j c) -> p j c", j=16)[
                        :, jg * 4:jg * 4 + gsz, qtl * 128:(qtl + 1) * 128]
                    srcp = ps_pt[pi][:, 0:gsz * 128].rearrange(
                        "p (j c) -> p j c", j=gsz)
                    if jg % 2 == 0:
                        wfree(ve, grp_m)
                        c_v.inc(ve.tensor_copy(dst, srcp))
                        pspt_free[pi] = c_v.mark()
                    else:
                        wfree(se, grp_m)
                        c_s.inc(se.copy(dst, srcp))
                        pspt_free[pi] = c_s.mark()
                pbf_free[qt % 4] = c_t.mark()
                if qtl == 3:
                    ai = qs % 2
                    wfree(te, psav_free[ai])
                    wfree(te, pspt_free[0])
                    wfree(te, pspt_free[1])
                    for jt in range(4 * qs + 4):
                        if jt < 4 * qs:
                            ins = te.matmul(
                                ps_av[ai][:, :],
                                v_hb[bi][:, jt * 128:(jt + 1) * 128],
                                pT_sb[:, jt * CH:(jt + 1) * CH],
                                start=(jt == 0), stop=False,
                                skip_group_check=True)
                        else:
                            i = jt - 4 * qs
                            ins = te.matmul(
                                ps_av[ai][:, i * 128:CH],
                                v_hb[bi][:, jt * 128:(jt + 1) * 128],
                                pT_sb[:, jt * CH + i * 128:(jt + 1) * CH],
                                start=(jt == 0), stop=(jt == 4 * qs + 3),
                                skip_group_check=True)
                    c_t.inc(ins)
                    av_mm = c_t.n
                    c_t.wait(se, av_mm)
                    wfree(se, att_free[ai])
                    c_s.inc(se.copy(attbf[ai][:, :], ps_av[ai][:, :]))
                    psav_free[ai] = c_s.mark()
                    c_s.wait(se)
                    c_st.inc(se.dma_start(
                        attnT_s[s_i * D + h * 128:s_i * D + (h + 1) * 128,
                                qs * CH:(qs + 1) * CH], attbf[ai][:, :]), 16)
                    att_free[ai] = c_st.mark()

            def b_softmax(p):
                qt, nch, ncol = p["qt"], p["nch"], p["ncol"]
                p3 = qt % 4
                wfree(ve, p["expc_m"])
                c_v.wait(ve)
                c_v.inc(ve.tensor_reduce(negm[:, :], negc[p3][:, 0:nch],
                                         AX.X, ALU.min))
                c_v.wait(ve)
                c_v.inc(ve.tensor_scalar(fv[:, 0:nch], negc[p3][:, 0:nch],
                                         negm[:, :], None, ALU.subtract))
                fv_m = c_v.mark()
                wfree(se, fv_m)
                c_s.inc(se.activation(fv[:, 0:nch], fv[:, 0:nch], AF.Exp,
                                      scale=-1.0))
                fexp_m = c_s.mark()
                wfree(ve, fexp_m)
                c_v.inc(ve.tensor_mul(gv[:, 0:nch], fv[:, 0:nch],
                                      denc[p3][:, 0:nch]))
                c_v.wait(ve)
                c_v.inc(ve.reduce_sum(den[:, :], gv[:, 0:nch], axis=AX.X))
                c_v.wait(ve)
                c_v.inc(ve.reciprocal(dinv[:, :], den[:, :]))
                c_v.wait(ve)
                c_v.inc(ve.tensor_scalar_mul(gv[:, 0:nch], fv[:, 0:nch],
                                             dinv[:, :]))
                soft_free[p3] = c_v.mark()
                c_v.wait(ve)
                wfree(ve, pbf_free[p3])
                for c in range(nch):
                    c_v.inc(ve.tensor_scalar_mul(
                        diags[p3][:, c * 128:(c + 1) * 128],
                        ident[:, :], gv[:, c:c + 1]))
                p["pn_v"] = c_v.mark()
                return p

            pending = []
            soft_pending = []
            for hi in range(NSEQ * H):
                s_i, h = divmod(hi, H)
                bi = hi % 2
                wfree(sy, qkv_free[bi])
                c_st.wait(sy, a_done_st)
                c_x.inc(sy.dma_start(
                    q_sb[bi][:, :],
                    qT_s[s_i * D + h * 128:s_i * D + (h + 1) * 128, :]), 16)
                c_x.inc(sy.dma_start(
                    kc_sb[bi][0:64, :],
                    knope_s[s_i * H * ND + h * ND:
                            s_i * H * ND + (h + 1) * ND, :]), 16)
                c_x.inc(sy.dma_start(
                    kc_sb[bi][64:128, :],
                    krope_s[s_i * ND:(s_i + 1) * ND, :]), 16)
                c_x.inc(sy.dma_start(
                    sb3(v_hb[bi][:, :], 16),
                    dr3(v_s[s_i * S:(s_i + 1) * S, h * 128:(h + 1) * 128])), 16)
                load_done[bi] = c_x.mark()

                for qs in range(4):
                    nch = qs + 1
                    ncol = nch * CH
                    for qtl in range(4):
                        qt = qs * 4 + qtl
                        p3 = qt % 4
                        if len(pending) >= 2:
                            b_transposes(pending.pop(0))
                        if len(soft_pending) >= 2:
                            pending.append(b_softmax(soft_pending.pop(0)))
                        if qtl == 0:
                            wfree(te, load_done[bi])
                        for jc in range(nch):
                            wfree(te, bank_free[jc])
                            ins = te.matmul(
                                ps_row[jc][:, :],
                                q_sb[bi][:, qt * 128:(qt + 1) * 128],
                                kc_sb[bi][:, jc * CH:(jc + 1) * CH],
                                start=True, stop=True)
                            c_t.inc(ins)
                            mm_m = c_t.n
                            c_t.wait(ve, mm_m)
                            if jc == qs:
                                c_v.wait(ve, masks_v)
                                c_v.inc(ve.tensor_tensor(
                                    ps_row[jc][:, :], ps_row[jc][:, :],
                                    masks[:, qtl * CH:(qtl + 1) * CH],
                                    ALU.add))
                                c_v.wait(ve)
                            if jc == 0:
                                wfree(ve, soft_free[p3])
                            c_v.inc(ve.reduce_max(negc[p3][:, jc:jc + 1],
                                                  ps_row[jc][:, :],
                                                  axis=AX.X, negate=True))
                            cm_m = c_v.mark()
                            wfree(se, cm_m)
                            if jc == 0:
                                wfree(se, pbf_free[p3])
                                wfree(se, soft_free[p3])
                            c_s.inc(se.activation(
                                pbf[p3][:, jc * CH:(jc + 1) * CH],
                                ps_row[jc][:, :], AF.Exp,
                                bias=negc[p3][:, jc:jc + 1], scale=1.0,
                                accum_out=denc[p3][:, jc:jc + 1]))
                            bank_free[jc] = c_s.mark()
                        expc_last[p3] = bank_free[nch - 1]
                        soft_pending.append(
                            {"qt": qt, "qs": qs, "qtl": qtl,
                             "bi": bi, "s_i": s_i, "h": h,
                             "expc_m": expc_last[p3], "nch": nch,
                             "ncol": ncol})
                while soft_pending:
                    if len(pending) >= 2:
                        b_transposes(pending.pop(0))
                    pending.append(b_softmax(soft_pending.pop(0)))
                while pending:
                    b_transposes(pending.pop(0))
                qkv_free[bi] = c_t.mark()
            c_st.wait(sy)
            c_x.wait(sy)
        nc.all_engine_barrier()
        b_done_st = c_st.n

        if upto == "B":
            _finish(nc, sy, c_st, c_x, dump, dbg, scr)
            return nc

        with ExitStack() as pc:
            ctx = pc.enter_context
            wo_sb = ctx(nc.sbuf_tensor("wo_sb", [128, 16 * D], BF16))
            for k in range(16):
                c_w.inc(ge.dma_start(wo_sb[:, k * D:(k + 1) * D],
                                     woT[k * 128:(k + 1) * 128, :]), 16)
            wo_ready = c_w.n
            atc = [ctx(nc.sbuf_tensor(f"atc{i}", [128, 16 * CH], BF16))
                   for i in range(2)]
            xin2 = [ctx(nc.sbuf_tensor(f"xin2_{i}", [128, D], F32))
                    for i in range(2)]
            lnat = ctx(nc.sbuf_tensor("lnat", [128, D], F32))
            prbuf = [ctx(nc.sbuf_tensor(f"prbuf{i}", [128, D], F32))
                     for i in range(2)]
            y_sb = [ctx(nc.sbuf_tensor(f"y_sb{i}", [128, D], F32)) for i in range(2)]
            lny = [ctx(nc.sbuf_tensor(f"lny{i}", [128, D], BF16)) for i in range(2)]
            lnyTst = [ctx(nc.sbuf_tensor(f"lnyTst{i}", [128, 16 * 128], BF16))
                      for i in range(2)]
            bn24c = [ctx(nc.sbuf_tensor(f"bn24c{i}", [128, 24], F32))
                     for i in range(2)]
            mvc = [ctx(nc.sbuf_tensor(f"mvc{i}", [128, 2], F32)) for i in range(2)]
            rstdc = [ctx(nc.sbuf_tensor(f"rstdc{i}", [128, 1], F32))
                     for i in range(2)]
            negmrc = [ctx(nc.sbuf_tensor(f"negmrc{i}", [128, 1], F32))
                      for i in range(2)]
            ps_pr = ctx(nc.psum_tensor("ps_pr", [128, D], F32))
            ps_t2 = [ctx(nc.psum_tensor(f"ps_t2{i}", [128, CH], F32))
                     for i in range(2)]

            atc_load = [None, None]
            atc_free = [None, None]
            pr_readers_s = None
            pr_readers_v = None
            prbuf_free = [None, None]
            x2_free = [None, None]
            y_free = [None, None]
            lny_free = [None, None]
            pst2_free = [None, None]
            lnyTst_free = [None, None]

            def c_tail(p):
                ui, lny_s, row0, tg = p["ui"], p["lny_s"], p["row0"], p["tg"]
                wfree(ve, lnyTst_free[ui])
                for grp in range(4):
                    pi = grp % 2
                    c_s.wait(te, lny_s)
                    wfree(te, pst2_free[pi])
                    for i4 in range(4):
                        k = grp * 4 + i4
                        ins = te.matmul(
                            ps_t2[pi][:, i4 * 128:(i4 + 1) * 128],
                            lny[ui][:, k * 128:(k + 1) * 128],
                            ident[:, :],
                            start=(i4 == 0), stop=(i4 == 3),
                            skip_group_check=True)
                    c_t.inc(ins)
                    lny_free[ui] = c_t.mark()
                    c_t.wait(ve)
                    c_v.inc(ve.tensor_copy(
                        lnyTst[ui][:, grp * 512:(grp + 1) * 512],
                        ps_t2[pi][:, :]))
                    pst2_free[pi] = c_v.mark()
                lnyT_v = c_v.n
                c_v.wait(se, lnyT_v)
                c_st.inc(se.dma_start(
                    dr3(lnyT_s[:, tg:tg + 128]),
                    sb3(lnyTst[ui][:, :], 16)), 16)
                lnyTst_free[ui] = c_st.mark()

            c_pending = None
            for g in range(NSEQ * NCHUNK):
                s_i, c_i = divmod(g, NCHUNK)
                gi = g % 2
                if g == 0:
                    c_st.wait(sy, b_done_st)
                    c_x.inc(sy.dma_start(
                        sb3(atc[0][:, :], 16),
                        dr3(attnT_s[0:D, 0:CH])), 16)
                    atc_load[0] = c_x.mark()
                if g + 1 < NSEQ * NCHUNK:
                    s_n, c_n = divmod(g + 1, NCHUNK)
                    wfree(sy, atc_free[(g + 1) % 2])
                    c_x.inc(sy.dma_start(
                        sb3(atc[(g + 1) % 2][:, :], 16),
                        dr3(attnT_s[s_n * D:(s_n + 1) * D,
                                    c_n * CH:(c_n + 1) * CH])), 16)
                    atc_load[(g + 1) % 2] = c_x.mark()
                for u in range(4):
                    ui = u % 2
                    row0 = s_i * S + c_i * CH + u * 128
                    wfree(sy, x2_free[ui])
                    c_x.inc(sy.dma_start(xin2[ui][:, :],
                                         x_in[row0:row0 + 128, :]), 16)
                    x2_here = c_x.mark()
                    wfree(te, atc_load[gi])
                    c_w.wait(te, wo_ready)
                    wfree(te, pr_readers_s)
                    if c_pending is not None:
                        c_tail(c_pending)
                        c_pending = None
                    for n in range(4):
                        for k in range(16):
                            ins = te.matmul(
                                ps_pr[:, n * CH:(n + 1) * CH],
                                atc[gi][:, k * CH + u * 128:
                                        k * CH + (u + 1) * 128],
                                wo_sb[:, k * D + n * CH:k * D + (n + 1) * CH],
                                start=(k == 0), stop=(k == 15))
                    c_t.inc(ins)
                    pr_mm = c_t.mark()
                    atc_free[gi] = c_t.mark()
                    wfree(se, pr_mm)
                    wfree(se, prbuf_free[ui])
                    c_s.inc(se.copy(prbuf[ui][:, :], ps_pr[:, :]))
                    pr_readers_s = c_s.mark()
                    pr_readers_v = None
                    nmr1 = ln_stats(lambda a, b, _p=prbuf[ui]: _p[:, a:b],
                                    bn24c[ui], mvc[ui], rstdc[ui],
                                    negmrc[ui], pr_readers_s)
                    wfree(se, nmr1)
                    c_s.inc(se.activation(lnat[:, :], prbuf[ui][:, :],
                                          AF.Identity,
                                          bias=negmrc[ui][:, :],
                                          scale=rstdc[ui][:, :]))
                    lnat_s = c_s.n
                    prbuf_free[ui] = c_s.mark()
                    c_s.wait(ve, lnat_s)
                    wfree(ve, x2_here)
                    wfree(ve, y_free[ui])
                    c_v.inc(ve.tensor_add(y_sb[ui][:, :], lnat[:, :],
                                          xin2[ui][:, :]))
                    y_v = c_v.mark()
                    x2_free[ui] = c_v.mark()
                    c_v.wait(se, y_v[1])
                    c_st.inc(se.dma_start(y_s[row0:row0 + 128, :],
                                          y_sb[ui][:, :]), 16)
                    y_free[ui] = c_st.mark()
                    nmr2 = ln_stats(lambda a, b, _y=y_sb[ui]: _y[:, a:b],
                                    bn24c[ui], mvc[ui], rstdc[ui],
                                    negmrc[ui], y_v)
                    wfree(se, nmr2)
                    wfree(se, lny_free[ui])
                    c_s.inc(se.activation(lny[ui][:, :], y_sb[ui][:, :],
                                          AF.Identity, bias=negmrc[ui][:, :],
                                          scale=rstdc[ui][:, :]))
                    lny_s = c_s.n
                    c_pending = {"ui": ui, "lny_s": lny_s, "row0": row0,
                                 "tg": s_i * S + c_i * CH + u * 128}
            if c_pending is not None:
                c_tail(c_pending)
                c_pending = None
            c_st.wait(sy)
            c_x.wait(sy)
            c_w.wait(ge)
        nc.all_engine_barrier()
        c_done_st = c_st.n

        if upto == "C":
            _finish(nc, sy, c_st, c_x, dump, dbg, scr)
            return nc

        with ExitStack() as pd1:
            ctx = pd1.enter_context
            lnyT_sb = ctx(nc.sbuf_tensor("lnyT_sb", [128, 16 * T], BF16))
            c_st.wait(sy, c_done_st)
            for kq in range(4):
                c_x.inc(sy.dma_start(
                    sb3(lnyT_sb[:, kq * 4 * T:(kq + 1) * 4 * T], 4),
                    dr3(lnyT_s[kq * 512:(kq + 1) * 512, :])), 16)
            lnyT_ld = c_x.n
            wg_sb = [ctx(nc.sbuf_tensor(f"wg_sb{i}", [128, D], BF16))
                     for i in range(2)]
            wu_sb = [ctx(nc.sbuf_tensor(f"wu_sb{i}", [128, D], BF16))
                     for i in range(2)]
            sg = [ctx(nc.sbuf_tensor(f"sg{i}", [128, CH], BF16)) for i in range(2)]
            hbf = [ctx(nc.sbuf_tensor(f"hbf{i}", [128, CH], BF16)) for i in range(2)]
            ps_g = [ctx(nc.psum_tensor(f"ps_g{i}", [128, CH], F32)) for i in range(2)]
            ps_u = [ctx(nc.psum_tensor(f"ps_u{i}", [128, CH], F32)) for i in range(2)]

            w_loaded = [None, None]
            w_free = [None, None]
            psg_free = [None, None]
            psu_free = [None, None]
            hbf_free = [None, None]
            sg_free = [None, None]

            for m in range(64):
                wi = m % 2
                wfree(ge, w_free[wi])
                c_w.inc(ge.dma_start(
                    sb3(wg_sb[wi][:, :], 16),
                    dr3(wgT[:, m * 128:(m + 1) * 128])), 16)
                c_w.inc(ge.dma_start(
                    sb3(wu_sb[wi][:, :], 16),
                    dr3(wuT[:, m * 128:(m + 1) * 128])), 16)
                w_loaded[wi] = c_w.mark()
                for g in range(NSEQ * NCHUNK):
                    pi = g % 2
                    wfree(te, w_loaded[wi])
                    c_x.wait(te, lnyT_ld)
                    wfree(te, psg_free[pi])
                    for k in range(16):
                        ins = te.matmul(ps_g[pi][:, :],
                                        wg_sb[wi][:, k * 128:(k + 1) * 128],
                                        lnyT_sb[:, k * T + g * CH:
                                                k * T + (g + 1) * CH],
                                        start=(k == 0), stop=(k == 15))
                    c_t.inc(ins)
                    g_mm = c_t.n
                    wfree(te, psu_free[pi])
                    for k in range(16):
                        ins = te.matmul(ps_u[pi][:, :],
                                        wu_sb[wi][:, k * 128:(k + 1) * 128],
                                        lnyT_sb[:, k * T + g * CH:
                                                k * T + (g + 1) * CH],
                                        start=(k == 0), stop=(k == 15))
                    c_t.inc(ins)
                    u_mm = c_t.n
                    c_t.wait(se, g_mm)
                    wfree(se, sg_free[pi])
                    c_s.inc(se.activation(sg[pi][:, :], ps_g[pi][:, :], AF.Silu))
                    psg_free[pi] = c_s.mark()
                    sg_s = c_s.n
                    c_t.wait(ve, u_mm)
                    c_s.wait(ve, sg_s)
                    wfree(ve, hbf_free[pi])
                    c_v.inc(ve.tensor_mul(hbf[pi][:, :], sg[pi][:, :],
                                          ps_u[pi][:, :]))
                    psu_free[pi] = c_v.mark()
                    sg_free[pi] = c_v.mark()
                    h_v = c_v.n
                    c_v.wait(sy, h_v)
                    c_st.inc(sy.dma_start(
                        hT_s[m * 128:(m + 1) * 128, g * CH:(g + 1) * CH],
                        hbf[pi][:, :]), 16)
                    hbf_free[pi] = c_st.mark()
                w_free[wi] = c_t.mark()
            c_st.wait(sy)
            c_x.wait(sy)
            c_w.wait(ge)
        nc.all_engine_barrier()
        d1_done_st = c_st.n

        if upto == "D1":
            _finish(nc, sy, c_st, c_x, dump, dbg, scr)
            return nc

        with ExitStack() as pd2:
            ctx = pd2.enter_context
            wd_sb = ctx(nc.sbuf_tensor("wd_sb", [128, 32 * D], BF16))
            hsl = [ctx(nc.sbuf_tensor(f"hsl{i}", [128, 32 * 128], BF16))
                   for i in range(2)]
            dpart = [ctx(nc.sbuf_tensor(f"dpart{i}", [128, D], F32))
                     for i in range(2)]
            dfull = ctx(nc.sbuf_tensor("dfull", [128, D], F32))
            lnd = ctx(nc.sbuf_tensor("lnd", [128, D], F32))
            osb = ctx(nc.sbuf_tensor("osb", [128, D], F32))
            y2 = [ctx(nc.sbuf_tensor(f"y2_{i}", [128, D], F32))
                  for i in range(2)]
            bn24d = ctx(nc.sbuf_tensor("bn24d", [128, 24], F32))
            mvd = ctx(nc.sbuf_tensor("mvd", [128, 2], F32))
            rstdd = ctx(nc.sbuf_tensor("rstdd", [128, 1], F32))
            negmrd = ctx(nc.sbuf_tensor("negmrd", [128, 1], F32))
            ps_d = [ctx(nc.psum_tensor(f"ps_d{i}", [128, D], F32))
                    for i in range(2)]

            hsl_load = [None, None]
            hsl_free = [None, None]
            psd_free = [None, None]
            dpart_free = [None, None]
            dpart_ld = [None, None]
            y2_free = [None, None]
            y2_ld = [None, None]
            osb_free = None
            dpart_all_stored = None

            for hp in range(2):
                c_st.wait(sy, d1_done_st)
                if hp == 1:
                    c_t.wait(ge)
                    wfree(sy, dpart_all_stored)
                for k in range(32):
                    c_w.inc(ge.dma_start(
                        wd_sb[:, k * D:(k + 1) * D],
                        wdT[hp * 4096 + k * 128:
                            hp * 4096 + (k + 1) * 128, :]), 16)
                wd_ready = c_w.n

                def d2_loads(ts, _hp=hp):
                    bi = ts % 2
                    row0 = ts * 128
                    wfree(sy, hsl_free[bi])
                    c_x.inc(sy.dma_start(
                        sb3(hsl[bi][:, :], 32),
                        dr3(hT_s[_hp * 4096:(_hp + 1) * 4096,
                                 row0:row0 + 128])), 16)
                    hsl_load[bi] = c_x.mark()
                    if _hp == 1:
                        wfree(sy, dpart_free[bi])
                        c_x.inc(sy.dma_start(dpart[bi][:, :],
                                             dpart_s[row0:row0 + 128, :]), 16)
                        dpart_ld[bi] = c_x.mark()
                        wfree(sy, y2_free[bi])
                        c_x.inc(sy.dma_start(y2[bi][:, :],
                                             y_s[row0:row0 + 128, :]), 16)
                        y2_ld[bi] = c_x.mark()

                d2_loads(0)
                for ts_i in range(32):
                    bi = ts_i % 2
                    row0 = ts_i * 128
                    if ts_i + 1 < 32:
                        d2_loads(ts_i + 1)
                    pd_i = ts_i % 2
                    c_w.wait(te, wd_ready)
                    wfree(te, hsl_load[bi])
                    wfree(te, psd_free[pd_i])
                    for n in range(4):
                        for k in range(32):
                            ins = te.matmul(
                                ps_d[pd_i][:, n * CH:(n + 1) * CH],
                                hsl[bi][:, k * 128:(k + 1) * 128],
                                wd_sb[:, k * D + n * CH:k * D + (n + 1) * CH],
                                start=(k == 0), stop=(k == 31))
                    c_t.inc(ins)
                    d_mm = c_t.mark()
                    hsl_free[bi] = c_t.mark()
                    if hp == 0:
                        c_t.wait(se, d_mm[1])
                        wfree(se, dpart_free[bi])
                        c_s.inc(se.copy(dpart[bi][:, :], ps_d[pd_i][:, :]))
                        psd_free[pd_i] = c_s.mark()
                        c_s.wait(se)
                        c_st.inc(se.dma_start(dpart_s[row0:row0 + 128, :],
                                              dpart[bi][:, :]), 16)
                        dpart_free[bi] = c_st.mark()
                    else:
                        c_t.wait(ve, d_mm[1])
                        wfree(ve, dpart_ld[bi])
                        c_v.inc(ve.tensor_add(dfull[:, :], ps_d[pd_i][:, :],
                                              dpart[bi][:, :]))
                        psd_free[pd_i] = c_v.mark()
                        dpart_free[bi] = c_v.mark()
                        df_v = c_v.mark()
                        ln_stats(lambda a, b: dfull[:, a:b], bn24d, mvd,
                                 rstdd, negmrd, df_v)
                        c_v.wait(se)
                        c_s.inc(se.activation(lnd[:, :], dfull[:, :], AF.Identity,
                                              bias=negmrd[:, :],
                                              scale=rstdd[:, :]))
                        lnd_s = c_s.n
                        c_s.wait(ve, lnd_s)
                        wfree(ve, y2_ld[bi])
                        wfree(ve, osb_free)
                        c_v.inc(ve.tensor_add(osb[:, :], lnd[:, :], y2[bi][:, :]))
                        o_v = c_v.n
                        y2_free[bi] = c_v.mark()
                        c_v.wait(sy, o_v)
                        c_st.inc(sy.dma_start(out_d[row0:row0 + 128, :],
                                              osb[:, :]), 16)
                        osb_free = c_st.mark()
                if hp == 0:
                    dpart_all_stored = c_st.mark()
                    dpart_free = [None, None]
                    hsl_load = [None, None]
            c_st.wait(sy)
            c_x.wait(sy)
            c_w.wait(ge)
        nc.all_engine_barrier()
        _finish(nc, sy, c_st, c_x, dump, dbg, scr)
    return nc


def _finish(nc, sy, c_st, c_x, dump, dbg, scr):
    for name, d in dbg.items():
        c_st.inc(sy.dma_start(d, scr[name]), 16)
    c_st.wait(sy)


def kernel(L0_attn_x, L0_attn_wq, L0_attn_w_kv_down, L0_attn_w_k_up,
           L0_attn_w_k_rope, L0_attn_w_v_up, L0_attn_wo,
           L0_ffn_w_gate, L0_ffn_w_up, L0_ffn_w_down):
    x = np.ascontiguousarray(np.asarray(L0_attn_x, dtype=np.float32))
    assert x.shape == (16, S, D)

    def tr(w):
        return np.ascontiguousarray(np.asarray(w, dtype=np.float32).T)

    weights = {
        "wqT": tr(L0_attn_wq),
        "wkvT": tr(L0_attn_w_kv_down),
        "wkrT": tr(L0_attn_w_k_rope),
        "wkuT": tr(L0_attn_w_k_up),
        "wvuT": tr(L0_attn_w_v_up),
        "woT": tr(L0_attn_wo),
        "wgT": tr(L0_ffn_w_gate),
        "wuT": tr(L0_ffn_w_up),
        "wdT": tr(L0_ffn_w_down),
    }
    nc = build_program()
    in_maps = []
    for core in range(8):
        xc = np.ascontiguousarray(
            x[NSEQ * core:NSEQ * (core + 1)].reshape(T, D))
        m = {"x": xc}
        m.update(weights)
        in_maps.append(m)
    res = run_bass_kernel_spmd(nc, in_maps, core_ids=list(range(8)))
    out = np.concatenate(
        [np.asarray(res.results[i]["out"]).reshape(NSEQ, S, D)
         for i in range(8)], axis=0)
    return out
